# version 05
# baseline (speedup 1.0000x reference)
import numpy as np
import ml_dtypes
from contextlib import ExitStack

import concourse.bass as bass
import concourse.mybir as mybir
from concourse.bass_utils import run_bass_kernel_spmd

F32 = mybir.dt.float32
BF16 = mybir.dt.bfloat16
I8 = mybir.dt.int8
AF = mybir.ActivationFunctionType
ALU = mybir.AluOpType
AX = mybir.AxisListType

NCORES = 8
S_TILES = 16
E_BLKS = 16
O_TILES = 64
CHUNK = 512
N_CHUNKS = 4

RMS_EPS_L0 = 1e-6
RMS_EPS_L1 = float(np.finfo(np.float32).eps)
LN_EPS = 1e-5
FWHT_SCALE = float(1.0 / np.sqrt(8192.0))
SM_SCALE_L1 = float(128.0 ** -0.5)

ENGINES = ("sync", "gpsimd", "scalar", "vector", "tensor")


class Sem:
    def __init__(self, nc, ctx, name):
        self.h = ctx.enter_context(nc.semaphore(name))
        self.n = 0


class Prog:

    def __init__(self, nc, ctx):
        self.nc = nc
        self.ctx = ctx
        self.alloc_ctx = ctx
        self.q = {e: [] for e in ENGINES}
        self._sems = {}

    def barrier(self):
        waits = [(s, None) for s in self._sems.values() if s.n > 0]
        for eng in ENGINES:
            self.emit(eng, lambda e: None, waits=waits)

    def sem(self, name):
        if name not in self._sems:
            self._sems[name] = Sem(self.nc, self.ctx, name)
        return self._sems[name]

    def emit(self, eng, fn, waits=(), incs=()):
        wlist = [(s.h, (s.n if v is None else v)) for (s, v) in waits]
        ilist = []
        for s, k in incs:
            s.n += k
            ilist.append((s.h, k))

        def run(e):
            for h, v in wlist:
                if v > 0:
                    e.wait_ge(h, v)
            inst = fn(e)
            for h, k in ilist:
                assert inst is not None
                inst = inst.then_inc(h, k)
        self.q[eng].append(run)

    def dma(self, eng, out_ap, in_ap, waits=(), incs=()):
        self.emit(eng, lambda e: e.dma_start(out_ap, in_ap), waits, incs)

    def finalize(self):
        with self.nc.Block() as b:
            for name in ENGINES:
                fns = self.q[name]
                if not fns:
                    continue
                getattr(b, name)(lambda e, fns=fns: [f(e) for f in fns])


def _c(a):
    return np.ascontiguousarray(a)


def prep_weights(ins):
    w = {}
    for nm in ("wq", "wk", "wv"):
        a = ins[f"L0_attn_{nm}"]
        w[f"l0_{nm}"] = _c(a.reshape(16, 128, 16, 128).transpose(2, 1, 0, 3))
    w["l0_wo"] = _c(ins["L0_attn_wo"].reshape(16, 128, 2048))
    w["l0_qscale"] = _c(ins["L0_attn_q_scale"].reshape(128, 1))
    w["l0_kscale"] = _c(ins["L0_attn_k_scale"].reshape(128, 1))
    for nm, key in (("wg", "L0_ffn_w_gate"), ("wu", "L0_ffn_w_up")):
        a = ins[key]
        w[f"l0_{nm}"] = _c(a.reshape(16, 128, 64, 128).transpose(2, 1, 0, 3))
    w["l0_wd"] = _c(ins["L0_ffn_w_down"].reshape(64, 128, 2048))
    w["l0_sg"] = _c(ins["L0_ffn_s_gate"].reshape(64, 128).T)
    w["l0_su"] = _c(ins["L0_ffn_s_up"].reshape(64, 128).T)
    w["l0_sdown"] = _c(ins["L0_ffn_s_down"].reshape(1, 2048))
    a = ins["L1_attn_Wq"]
    w["l1_wq"] = _c(a.reshape(16, 128, 16, 128).transpose(0, 3, 2, 1))
    for nm in ("Wk", "Wv"):
        a = ins[f"L1_attn_{nm}"]
        w[f"l1_{nm.lower()}"] = _c(a.reshape(4, 128, 16, 128).transpose(0, 3, 2, 1))
    w["l1_wv_n"] = _c(ins["L1_attn_Wv"].T.reshape(16, 128, 512))
    w["l1_wo"] = _c(ins["L1_attn_Wo"].T.reshape(16, 128, 2048))
    for nm, key in (("w1", "L1_ffn_W1"), ("w3", "L1_ffn_W3")):
        a = ins[key]
        w[f"l1_{nm}"] = _c(a.reshape(64, 128, 16, 128).transpose(0, 3, 2, 1))
    w["l1_w2t"] = _c(ins["L1_ffn_W2"].T.reshape(64, 128, 2048))
    return w


def host_consts():
    c = {}
    c["idn_bf"] = np.eye(128, dtype=ml_dtypes.bfloat16)
    c["idn_f32"] = np.eye(128, dtype=np.float32)

    def had(n):
        h = np.array([[1.0]])
        while h.shape[0] < n:
            h = np.block([[h, h], [h, -h]])
        return h
    c["h128"] = had(128).astype(ml_dtypes.bfloat16)
    h64 = had(64)
    z = np.zeros((64, 64))
    c["h2x64"] = np.block([[h64, z], [z, h64]]).astype(ml_dtypes.bfloat16)
    return c


class Consts:
    def __init__(self, nc, ctx, p, T):
        sb = lambda name, shape, dt: ctx.enter_context(nc.sbuf_tensor(name, shape, dt))
        self.idn_bf = sb("c_idn_bf", [128, 128], BF16)
        self.idn_f32 = sb("c_idn_f32", [128, 128], F32)
        self.bcast1_f32 = sb("c_bcast1", [1, 128], F32)
        self.ones_row_bf = sb("c_ones_row_bf", [1, 128], BF16)
        self.red1_f32 = sb("c_red1_f", [128, 1], F32)
        self.red1_bf = sb("c_red1_b", [128, 1], BF16)
        self.vals = sb("c_vals", [128, 12], F32)
        ld = p.sem("c_ld")
        p.dma("sync", self.idn_bf[:, :], T["idn_bf"].ap()[:, :], incs=[(ld, 16)])
        p.dma("sync", self.idn_f32[:, :], T["idn_f32"].ap()[:, :], incs=[(ld, 16)])
        g = p.sem("c_set")
        for col, v in ((0, -1.0 / 2048.0), (1, 1.0 / 2048.0), (2, LN_EPS),
                       (3, RMS_EPS_L0), (4, RMS_EPS_L1), (5, FWHT_SCALE),
                       (6, 1.0 / 128.0), (7, 1e-24), (8, SM_SCALE_L1)):
            p.emit("vector", lambda e, col=col, v=v: e.memset(self.vals[:, col:col + 1], v),
                   incs=[(g, 1)])
        for t in (self.bcast1_f32, self.ones_row_bf, self.red1_f32, self.red1_bf):
            p.emit("vector", lambda e, t=t: e.memset(t[:, :], 1.0), incs=[(g, 1)])
        self.ld = ld
        self.ready = g

    def v(self, col):
        return self.vals[:, col:col + 1]

    def v1(self, col):
        return self.vals[0:1, col:col + 1]



def emit_ffn(p, C, layer, T, tag):
    nc = p.nc
    ctx = p.alloc_ctx
    sb = lambda name, shape, dt: ctx.enter_context(nc.sbuf_tensor(tag + name, shape, dt))

    x_in = T["x_in"].ap()
    xnT_d = T["xnT"].ap()
    xacc_d = T["x_acc"].ap()
    if layer == 0:
        wg_d, wu_d, wd_d = T["l0_wg"].ap(), T["l0_wu"].ap(), T["l0_wd"].ap()
    else:
        wg_d, wu_d, wd_d = T["l1_w1"].ap(), T["l1_w3"].ap(), T["l1_w2pp"].ap()

    xbuf = sb("xbuf", [128, 2048], F32)
    xsq = sb("xsq", [128, 2048], F32)
    st = sb("stats", [128, 8], F32)
    xn_bf = sb("xn_bf", [128, 2048], BF16)
    xnT_sb = sb("xnT_sb", [128, E_BLKS * CHUNK], BF16)
    hidT = sb("hidT", [128, O_TILES * CHUNK], BF16)
    wgl = sb("wgl", [128, 2048], BF16)
    wul = sb("wul", [128, 2048], BF16)
    wdp = sb("wdp", [128, 1024], BF16)
    silu_s = sb("silu_s", [128, CHUNK], F32)
    up_s = sb("up_s", [128, CHUNK], F32)
    xacc = [sb(f"xacc{i}", [128, 2048], F32) for i in range(4)]
    if layer == 0:
        sg_sb = sb("sg", [128, 64], F32)
        su_sb = sb("su", [128, 64], F32)
        sdb = sb("sdb", [128, 2048], F32)

    xld = p.sem(tag + "xld")
    ald = p.sem(tag + "ald")
    wld = p.sem(tag + "wld")
    ggu = p.sem(tag + "ggu")
    gwd = p.sem(tag + "gwd")
    lnv = p.sem(tag + "lnv")
    trp = p.sem(tag + "trp")
    xnv = p.sem(tag + "xnv")
    mmg = p.sem(tag + "mmg")
    ev = p.sem(tag + "ev")
    mmd = p.sem(tag + "mmd")
    dev = p.sem(tag + "dev")
    sto = p.sem(tag + "sto")

    if layer == 0:
        p.dma("sync", sg_sb[:, :], T["l0_sg"].ap()[:, :], incs=[(wld, 16)])
        p.dma("sync", su_sb[:, :], T["l0_su"].ap()[:, :], incs=[(wld, 16)])
        sdrow = sb("sdrow", [1, 2048], F32)
        p.dma("sync", sdrow[:, :], T["l0_sdown"].ap()[:, :], incs=[(wld, 16)])
        with nc.psum_tensor(tag + "pbc", [128, 512], F32) as pbc:
            for i in range(4):
                p.emit("tensor",
                       lambda e, i=i, pbc=pbc: e.matmul(pbc[:, :], C.bcast1_f32[:, :],
                                               sdrow[:, i * 512:(i + 1) * 512],
                                               start=True, stop=True),
                       waits=[(wld, None), (C.ready, None), (ev, None)], incs=[(mmg, 1)])
                p.emit("vector",
                       lambda e, i=i, pbc=pbc: e.tensor_copy(sdb[:, i * 512:(i + 1) * 512], pbc[:, :]),
                       waits=[(mmg, None)], incs=[(ev, 1)])

    for q in range(N_CHUNKS):
        tiles = [q * 4 + i for i in range(4)]
        with nc.psum_tensor(f"{tag}ptr{q}", [128, 512], BF16) as ptr, \
             nc.psum_tensor(f"{tag}pg{q}", [128, 512], F32) as pg, \
             nc.psum_tensor(f"{tag}pu{q}", [128, 512], F32) as pu:
            for i, t in enumerate(tiles):
                p.dma("sync", xbuf[:, :], x_in[t], waits=[(lnv, None)], incs=[(xld, 16)])
                p.emit("vector", lambda e: e.reduce_sum(st[:, 0:1], xbuf[:, :], axis=AX.X),
                       waits=[(xld, None)], incs=[(lnv, 1)])
                p.emit("scalar", lambda e: e.activation(xsq[:, :], xbuf[:, :], AF.Square),
                       waits=[(xld, None), (lnv, None)], incs=[(lnv, 1)])
                p.emit("vector", lambda e: e.reduce_sum(st[:, 1:2], xsq[:, :], axis=AX.X),
                       waits=[(lnv, None)], incs=[(lnv, 1)])
                p.emit("vector", lambda e: e.tensor_scalar(st[:, 2:3], st[:, 0:1],
                                                           C.v(0), None, ALU.mult),
                       waits=[(lnv, None)], incs=[(lnv, 1)])
                p.emit("vector", lambda e: e.tensor_tensor(st[:, 3:4], st[:, 2:3], st[:, 2:3],
                                                           ALU.mult),
                       waits=[(lnv, None)], incs=[(lnv, 1)])
                p.emit("vector", lambda e: e.scalar_tensor_tensor(st[:, 4:5], st[:, 1:2],
                                                                  C.v(1), st[:, 3:4],
                                                                  ALU.mult, ALU.subtract),
                       waits=[(lnv, None)], incs=[(lnv, 1)])
                p.emit("scalar", lambda e: e.activation(st[:, 5:6], st[:, 4:5], AF.Ln,
                                                        bias=C.v(2)),
                       waits=[(lnv, None)], incs=[(lnv, 1)])
                p.emit("scalar", lambda e: e.activation(st[:, 6:7], st[:, 5:6], AF.Exp,
                                                        scale=-0.5),
                       waits=[(lnv, None)], incs=[(lnv, 1)])
                p.emit("vector", lambda e: e.tensor_scalar(xn_bf[:, :], xbuf[:, :],
                                                           st[:, 2:3], st[:, 6:7],
                                                           ALU.add, ALU.mult),
                       waits=[(lnv, None), (trp, None)], incs=[(lnv, 1)])
                for gblk in range(4):
                    for j in range(4):
                        eb = gblk * 4 + j
                        w = [(lnv, None), (C.ld, None), (xnv, None), (dev, None),
                             (ev, None)] if j == 0 else []
                        p.emit("tensor",
                               lambda e, eb=eb, j=j, ptr=ptr: e.transpose(
                                   ptr[:, j * 128:(j + 1) * 128],
                                   xn_bf[:, eb * 128:(eb + 1) * 128],
                                   C.idn_bf[:, :]),
                               waits=w, incs=([(trp, 1)] if j == 3 else []))
                    dst = xnT_sb[:, :].rearrange("p (eb s) -> p eb s", eb=E_BLKS)
                    p.emit("vector",
                           lambda e, gblk=gblk, i=i, dst=dst, ptr=ptr: e.tensor_copy(
                               dst[:, gblk * 4:gblk * 4 + 4, i * 128:(i + 1) * 128],
                               ptr[:, :].rearrange("p (j s) -> p j s", j=4)),
                           waits=[(trp, None), (sto, None)], incs=[(xnv, 1)])
            for eb in range(E_BLKS):
                p.dma("sync", xnT_d[eb, :, q * CHUNK:(q + 1) * CHUNK],
                      xnT_sb[:, eb * CHUNK:(eb + 1) * CHUNK],
                      waits=[(xnv, None)], incs=[(sto, 16)])

            for ot in range(O_TILES):
                p.dma("gpsimd", wgl[:, :], wg_d[ot].rearrange("p c j -> p (c j)"),
                      waits=[(mmg, None)], incs=[(ggu, 16)])
                p.dma("gpsimd", wul[:, :], wu_d[ot].rearrange("p c j -> p (c j)"),
                      incs=[(ggu, 16)])

                def mm_gate(e, pg=pg):
                    for c in range(E_BLKS):
                        inst = e.matmul(pg[:, :], wgl[:, c * 128:(c + 1) * 128],
                                        xnT_sb[:, c * CHUNK:(c + 1) * CHUNK],
                                        start=(c == 0), stop=(c == E_BLKS - 1))
                    return inst
                p.emit("tensor", mm_gate,
                       waits=[(ggu, None), (xnv, None), (ev, None), (dev, None)],
                       incs=[(mmg, 1)])

                def mm_up(e, pu=pu):
                    for c in range(E_BLKS):
                        inst = e.matmul(pu[:, :], wul[:, c * 128:(c + 1) * 128],
                                        xnT_sb[:, c * CHUNK:(c + 1) * CHUNK],
                                        start=(c == 0), stop=(c == E_BLKS - 1))
                    return inst
                p.emit("tensor", mm_up, incs=[(mmg, 1)])

                if layer == 0:
                    p.emit("scalar",
                           lambda e, ot=ot, pg=pg: e.activation(silu_s[:, :], pg[:, :], AF.Silu,
                                                         scale=sg_sb[:, ot:ot + 1]),
                           waits=[(mmg, None), (ev, None), (wld, None)], incs=[(ev, 1)])
                    p.emit("vector",
                           lambda e, ot=ot, pu=pu: e.tensor_scalar(up_s[:, :], pu[:, :],
                                                            su_sb[:, ot:ot + 1], None,
                                                            ALU.mult),
                           waits=[(mmg, None), (wld, None)], incs=[(ev, 1)])
                    p.emit("vector",
                           lambda e, ot=ot: e.tensor_tensor(
                               hidT[:, ot * CHUNK:(ot + 1) * CHUNK],
                               silu_s[:, :], up_s[:, :], ALU.mult),
                           waits=[(ev, None)], incs=[(ev, 1)])
                else:
                    p.emit("scalar",
                           lambda e, pg=pg: e.activation(silu_s[:, :], pg[:, :], AF.Silu),
                           waits=[(mmg, None), (ev, None)], incs=[(ev, 1)])
                    p.emit("vector",
                           lambda e, ot=ot, pu=pu: e.tensor_tensor(
                               hidT[:, ot * CHUNK:(ot + 1) * CHUNK],
                               silu_s[:, :], pu[:, :], ALU.mult),
                           waits=[(mmg, None), (ev, None)], incs=[(ev, 1)])

        for i, t in enumerate(tiles):
            p.dma("sync", xacc[i][:, :], x_in[t], waits=[(dev, None)], incs=[(ald, 16)])
        with ExitStack() as ps:
            pd = [ps.enter_context(nc.psum_tensor(f"{tag}pd{q}_{i}", [128, 512], F32))
                  for i in range(8)]
            for dh in range(2):
                for tt in range(O_TILES):
                    p.dma("gpsimd", wdp[:, :], wd_d[tt, :, dh * 1024:(dh + 1) * 1024],
                          waits=[(mmd, None)], incs=[(gwd, 16)])

                    def mm_down(e, tt=tt, dh=dh, pd=pd):
                        for i in range(4):
                            for dc in range(2):
                                inst = e.matmul(
                                    pd[i * 2 + dc][:, :],
                                    hidT[:, tt * CHUNK + i * 128: tt * CHUNK + (i + 1) * 128],
                                    wdp[:, dc * 512:(dc + 1) * 512],
                                    start=(tt == 0), stop=(tt == O_TILES - 1))
                        return inst
                    p.emit("tensor", mm_down,
                           waits=[(gwd, None)] + ([(ev, None), (dev, None)] if tt == 0 else []),
                           incs=[(mmd, 1)])
                for i in range(4):
                    for dc in range(2):
                        col = dh * 1024 + dc * 512
                        if layer == 0:
                            p.emit("vector",
                                   lambda e, i=i, dc=dc, col=col, pd=pd: e.tensor_tensor(
                                       up_s[:, :], pd[i * 2 + dc][:, :],
                                       sdb[:, col:col + 512], ALU.mult),
                                   waits=[(mmd, None), (ald, None), (dev, None)],
                                   incs=[(dev, 1)])
                            p.emit("vector",
                                   lambda e, i=i, col=col: e.tensor_tensor(
                                       xacc[i][:, col:col + 512], xacc[i][:, col:col + 512],
                                       up_s[:, :], ALU.add),
                                   waits=[(dev, None)], incs=[(dev, 1)])
                        else:
                            p.emit("vector",
                                   lambda e, i=i, dc=dc, col=col, pd=pd: e.tensor_tensor(
                                       xacc[i][:, col:col + 512], xacc[i][:, col:col + 512],
                                       pd[i * 2 + dc][:, :], ALU.add),
                                   waits=[(mmd, None), (ald, None)], incs=[(dev, 1)])
        for i, t in enumerate(tiles):
            p.dma("sync", xacc_d[t], xacc[i][:, :], waits=[(dev, None)], incs=[(sto, 16)])

    return sto



def emit_w2pp(p, C, T, tag="w2_"):
    nc = p.nc
    ctx = p.alloc_ctx
    sb = lambda name, shape, dt: ctx.enter_context(nc.sbuf_tensor(tag + name, shape, dt))
    w2t = T["l1_w2t"].ap()
    w2a = T["l1_w2a"].ap()
    w2pp_flat = T["l1_w2pp"].ap().rearrange("t p d -> (t p) d")

    h128 = sb("h128", [128, 128], BF16)
    h2x64 = sb("h2x64", [128, 128], BF16)
    win = sb("win", [128, 2048], BF16)
    wout = sb("wout", [128, 2048], BF16)
    ld = p.sem(tag + "ld")
    gld = p.sem(tag + "gld")
    mm = p.sem(tag + "mm")
    evs = p.sem(tag + "ev")
    sto = p.sem(tag + "sto")
    p.dma("sync", h128[:, :], T["h128"].ap()[:, :], incs=[(ld, 16)])
    p.dma("sync", h2x64[:, :], T["h2x64"].ap()[:, :], incs=[(ld, 16)])

    w2a_v = w2a.rearrange("(a b tt) d -> a b tt d", a=64, b=2)
    w2a_g = w2a.rearrange("(g q) d -> g q d", g=64)
    w2pp_v = w2pp_flat.rearrange("(tau gb) d -> tau gb d", gb=128)

    with nc.psum_tensor(tag + "ps", [128, 512], F32) as ps:
        for t in range(64):
            p.dma("gpsimd", win[:, :], w2t[t], waits=[(sto, None)], incs=[(gld, 16)])
            for dchunk in range(4):
                p.emit("tensor",
                       lambda e, dchunk=dchunk: e.matmul(
                           ps[:, :], h128[:, :], win[:, dchunk * 512:(dchunk + 1) * 512],
                           start=True, stop=True),
                       waits=[(ld, None), (gld, None), (evs, None)], incs=[(mm, 1)])
                p.emit("vector",
                       lambda e, dchunk=dchunk: e.tensor_copy(
                           wout[:, dchunk * 512:(dchunk + 1) * 512], ps[:, :]),
                       waits=[(mm, None)], incs=[(evs, 1)])
            p.dma("sync", w2a_v[:, :, t, :].rearrange("a b d -> (a b) d"), wout[:, :],
                  waits=[(evs, None)], incs=[(sto, 16)])
        for g in range(64):
            p.dma("gpsimd", win[:, :], w2a_g[g], waits=[(sto, None)], incs=[(gld, 16)])
            for dchunk in range(4):
                p.emit("tensor",
                       lambda e, dchunk=dchunk: e.matmul(
                           ps[:, :], h2x64[:, :], win[:, dchunk * 512:(dchunk + 1) * 512],
                           start=True, stop=True),
                       waits=[(ld, None), (gld, None), (evs, None)], incs=[(mm, 1)])
                p.emit("vector",
                       lambda e, dchunk=dchunk: e.tensor_scalar(
                           wout[:, dchunk * 512:(dchunk + 1) * 512], ps[:, :],
                           C.v(5), None, ALU.mult),
                       waits=[(mm, None), (C.ready, None)], incs=[(evs, 1)])
            for beta in range(2):
                p.dma("sync", w2pp_v[:, g * 2 + beta, :], wout[beta * 64:(beta + 1) * 64, :],
                      waits=[(evs, None)], incs=[(sto, 16)])
    return sto



def emit_attn(p, C, layer, T, tag, dbg=False):
    nc = p.nc
    ctx = p.alloc_ctx
    sb = lambda name, shape, dt: ctx.enter_context(nc.sbuf_tensor(tag + name, shape, dt))

    xnT_d = T["xnT"].ap()
    xbase_d = T["x_base"].ap()
    xout_d = T["x_out"].ap()
    aoT_d = T["aoT"].ap()
    NH = 16

    xnT_sb = sb("xnT", [128, E_BLKS * 2048], BF16)
    wql = sb("wql", [128, 2048], BF16)
    wkl = sb("wkl", [128, 2048], BF16)
    wvl = sb("wvl", [128, 2048], BF16)
    fT = sb("fT", [128, 2048], F32)
    fsq = sb("fsq", [128, 2048], F32)
    rstd2 = sb("rstd2", [128, 2048], F32)
    lnrow = sb("lnrow", [1, 2048], F32)
    qhatT = sb("qhatT", [128, 2048], BF16)
    eT = sb("eT", [128, 16 * 512], BF16)
    cols = sb("cols", [128, 34], F32)
    zrow = sb("zrow", [1, 2048], F32)
    zb = sb("zb", [128, 512], F32)
    ao_p = sb("ao_p", [128, 512], BF16)
    xbh = fsq[:, 0:1024]
    xoh = fsq[:, 1024:2048]
    aop_sb = sb("aop", [128, 512], BF16)
    wop = sb("wop", [128, 1024], BF16)
    if layer == 0:
        e_nat = sb("e_nat", [128, 2048], BF16)
        mrow = sb("mrow", [1, 2048], BF16)
        khatT = [sb("khatT", [128, 2048], BF16)]
        v_nat = [sb("v", [128, 2048], BF16)]
        qs_c = sb("qs_c", [128, 1], F32)
        ks_c = sb("ks_c", [128, 1], F32)
        s2c = sb("s2c", [128, 1], F32)
    else:
        khatT = [sb(f"khatT{g}", [128, 2048], BF16) for g in range(4)]
        vT_bf = [sb(f"vT{g}", [128, 2048], BF16) for g in range(4)]
        v_nat = sb("v", [128, 16 * 512], BF16)
        ut_f = sb("ut_f", [128, 512], F32)
        uv_f = sb("uv_f", [128, 512], F32)
        g2b = sb("g2b", [128, 512], F32)
        g2row = sb("g2row", [1, 2048], F32)

    ld = p.sem(tag + "ld")
    aold = p.sem(tag + "aold")
    xbld = p.sem(tag + "xbld")
    gld = p.sem(tag + "gld")
    gvn = p.sem(tag + "gvn")
    glw = p.sem(tag + "glw")
    mm = p.sem(tag + "mm")
    se = p.sem(tag + "se")
    ve = p.sem(tag + "ve")
    sto = p.sem(tag + "sto")

    for eb in range(E_BLKS):
        p.dma("sync", xnT_sb[:, eb * 2048:(eb + 1) * 2048], xnT_d[eb], incs=[(ld, 16)])
    if layer == 0:
        p.dma("sync", qs_c[:, :], T["l0_qscale"].ap()[:, :], incs=[(ld, 16)])
        p.dma("sync", ks_c[:, :], T["l0_kscale"].ap()[:, :], incs=[(ld, 16)])
        p.emit("vector", lambda e: e.tensor_tensor(s2c[:, :], qs_c[:, :], ks_c[:, :],
                                                   ALU.mult),
               waits=[(ld, None)], incs=[(ve, 1)])

    def proj_T(wsb, psum):
        for sc in range(4):
            def mmf(e, sc=sc):
                for c in range(E_BLKS):
                    inst = e.matmul(psum[:, :], wsb[:, c * 128:(c + 1) * 128],
                                    xnT_sb[:, c * 2048 + sc * 512: c * 2048 + (sc + 1) * 512],
                                    start=(c == 0), stop=(c == E_BLKS - 1))
                return inst
            p.emit("tensor", mmf, waits=[(ld, None), (gld, None), (se, None), (ve, None)],
                   incs=[(mm, 1)])
            p.emit("scalar",
                   lambda e, sc=sc: e.activation(fT[:, sc * 512:(sc + 1) * 512],
                                                 psum[:, :], AF.Identity),
                   waits=[(mm, None), (ve, None)], incs=[(se, 1)])

    def rms_rows(prow, pb, eps_col):
        p.emit("vector", lambda e: e.tensor_tensor(fsq[:, :], fT[:, :], fT[:, :], ALU.mult),
               waits=[(se, None), (mm, None)], incs=[(ve, 1)])
        for sc in range(4):
            p.emit("tensor",
                   lambda e, sc=sc: e.matmul(prow[:, :], C.red1_f32[:, :],
                                             fsq[:, sc * 512:(sc + 1) * 512],
                                             start=True, stop=True),
                   waits=[(ve, None), (se, None)], incs=[(mm, 1)])
            p.emit("scalar",
                   lambda e, sc=sc: e.activation(lnrow[0:1, sc * 512:(sc + 1) * 512],
                                                 prow[0:1, :], AF.Ln,
                                                 bias=eps_col, scale=C.v1(6)),
                   waits=[(mm, None)], incs=[(se, 1)])
        for sc in range(4):
            p.emit("tensor",
                   lambda e, sc=sc: e.matmul(pb[:, :], C.bcast1_f32[:, :],
                                             lnrow[0:1, sc * 512:(sc + 1) * 512],
                                             start=True, stop=True),
                   waits=[(se, None), (ve, None)], incs=[(mm, 1)])
            p.emit("scalar",
                   lambda e, sc=sc: e.activation(rstd2[:, sc * 512:(sc + 1) * 512],
                                                 pb[:, :], AF.Exp, scale=-0.5),
                   waits=[(mm, None), (ve, None), (se, None)], incs=[(se, 1)])

    if layer == 1:
        wvn = eT
        for c in range(E_BLKS):
            p.dma("gpsimd", wvn[:, c * 512:(c + 1) * 512], T["l1_wv_n"].ap()[c],
                  incs=[(gvn, 16)])
        with nc.psum_tensor(tag + "pvn", [128, 512], F32) as pv:
            for t in range(S_TILES):
                def mmv(e, t=t, pv=pv):
                    for c in range(E_BLKS):
                        inst = e.matmul(pv[:, :],
                                        xnT_sb[:, c * 2048 + t * 128: c * 2048 + (t + 1) * 128],
                                        wvn[:, c * 512:(c + 1) * 512],
                                        start=(c == 0), stop=(c == E_BLKS - 1))
                    return inst
                p.emit("tensor", mmv, waits=[(ld, None), (gvn, None), (ve, None)],
                       incs=[(mm, 1)])
                p.emit("vector",
                       lambda e, t=t, pv=pv: e.tensor_copy(v_nat[:, t * 512:(t + 1) * 512], pv[:, :]),
                       waits=[(mm, None)], incs=[(ve, 1)])
        for g in range(4):
            with nc.psum_tensor(f"{tag}kk{g}", [128, 512], F32) as pk, \
                 nc.psum_tensor(f"{tag}kr{g}", [1, 512], F32) as prow, \
                 nc.psum_tensor(f"{tag}kb{g}", [128, 512], F32) as pb:
                p.dma("gpsimd", wkl[:, :], T["l1_wk"].ap()[g].rearrange("p c j -> p (c j)"),
                      waits=[(mm, None)], incs=[(gld, 16)])
                proj_T(wkl, pk)
                rms_rows(prow, pb, C.v1(4))
                p.emit("vector",
                       lambda e, g=g: e.scalar_tensor_tensor(khatT[g][:, :], fT[:, :],
                                                             C.v(8), rstd2[:, :],
                                                             ALU.mult, ALU.mult),
                       waits=[(se, None)], incs=[(ve, 1)])
                p.dma("gpsimd", wvl[:, :], T["l1_wv"].ap()[g].rearrange("p c j -> p (c j)"),
                      waits=[(ve, None)], incs=[(gld, 16)])
                proj_T(wvl, pk)
                p.emit("vector", lambda e: e.tensor_tensor(fsq[:, :], fT[:, :], fT[:, :],
                                                           ALU.mult),
                       waits=[(se, None)], incs=[(ve, 1)])
                for sc in range(4):
                    p.emit("tensor",
                           lambda e, sc=sc, prow=prow: e.matmul(prow[:, :], C.red1_f32[:, :],
                                                     fsq[:, sc * 512:(sc + 1) * 512],
                                                     start=True, stop=True),
                           waits=[(ve, None), (se, None)], incs=[(mm, 1)])
                    p.emit("vector",
                           lambda e, sc=sc, prow=prow: e.tensor_scalar(
                               lnrow[0:1, sc * 512:(sc + 1) * 512], prow[0:1, :],
                               C.v1(7), None, ALU.max),
                           waits=[(mm, None)], incs=[(ve, 1)])
                    p.emit("scalar",
                           lambda e, sc=sc: e.activation(
                               lnrow[0:1, sc * 512:(sc + 1) * 512],
                               lnrow[0:1, sc * 512:(sc + 1) * 512], AF.Ln),
                           waits=[(ve, None)], incs=[(se, 1)])
                    p.emit("tensor",
                           lambda e, sc=sc, pb=pb: e.matmul(pb[:, :], C.bcast1_f32[:, :],
                                                     lnrow[0:1, sc * 512:(sc + 1) * 512],
                                                     start=True, stop=True),
                           waits=[(se, None)], incs=[(mm, 1)])
                    p.emit("scalar",
                           lambda e, sc=sc, pb=pb: e.activation(rstd2[:, sc * 512:(sc + 1) * 512],
                                                         pb[:, :], AF.Exp, scale=-0.5),
                           waits=[(mm, None), (ve, None)], incs=[(se, 1)])
                    p.emit("vector",
                           lambda e, sc=sc, g=g: e.tensor_tensor(
                               vT_bf[g][:, sc * 512:(sc + 1) * 512],
                               fT[:, sc * 512:(sc + 1) * 512],
                               rstd2[:, sc * 512:(sc + 1) * 512], ALU.mult),
                           waits=[(se, None)], incs=[(ve, 1)])

    for h in range(NH):
        with nc.psum_tensor(f"{tag}hk{h}", [128, 512], F32) as pk, \
             nc.psum_tensor(f"{tag}hr{h}", [1, 512], F32) as prow, \
             nc.psum_tensor(f"{tag}hb{h}", [128, 512], F32) as pb, \
             nc.psum_tensor(f"{tag}hv{h}", [128, 128], F32) as pv:
            if layer == 0:
                p.dma("gpsimd", wql[:, :], T["l0_wq"].ap()[h].rearrange("p c j -> p (c j)"),
                      waits=[(mm, None)], incs=[(gld, 16)])
                p.dma("gpsimd", wkl[:, :], T["l0_wk"].ap()[h].rearrange("p c j -> p (c j)"),
                      incs=[(gld, 16)])
                p.dma("gpsimd", wvl[:, :], T["l0_wv"].ap()[h].rearrange("p c j -> p (c j)"),
                      incs=[(gld, 16)])
                proj_T(wkl, pk)
                if dbg and h == 0:
                    p.dma("sync", T["d_fT"].ap()[:, :], fT[:, :],
                          waits=[(se, None)], incs=[(sto, 16)])
                    p.dma("sync", T["d_wql"].ap()[:, :], wkl[:, :],
                          waits=[(ld, None)], incs=[(sto, 16)])
                    p.dma("sync", T["d_xnT0"].ap()[:, :], xnT_sb[:, 0:2048],
                          waits=[(ld, None)], incs=[(sto, 16)])
                rms_rows(prow, pb, C.v1(3))
                if dbg and h == 0:
                    p.dma("sync", T["d_rstd2"].ap()[:, :], rstd2[:, :],
                          waits=[(se, None)], incs=[(sto, 16)])
                p.emit("vector",
                       lambda e: e.scalar_tensor_tensor(khatT[0][:, :], fT[:, :],
                                                        s2c[:, :], rstd2[:, :],
                                                        ALU.mult, ALU.mult),
                       waits=[(se, None), (ve, None)], incs=[(ve, 1)])
                for t in range(S_TILES):
                    def mmv(e, t=t, pv=pv):
                        for c in range(E_BLKS):
                            inst = e.matmul(pv[:, :],
                                            xnT_sb[:, c * 2048 + t * 128: c * 2048 + (t + 1) * 128],
                                            wvl[:, c * 128:(c + 1) * 128],
                                            start=(c == 0), stop=(c == E_BLKS - 1))
                        return inst
                    p.emit("tensor", mmv, waits=[(ld, None), (gld, None), (ve, None)],
                           incs=[(mm, 1)])
                    p.emit("vector",
                           lambda e, t=t, pv=pv: e.tensor_copy(v_nat[0][:, t * 128:(t + 1) * 128],
                                                        pv[:, :]),
                           waits=[(mm, None)], incs=[(ve, 1)])
                proj_T(wql, pk)
                rms_rows(prow, pb, C.v1(3))
                p.emit("vector",
                       lambda e: e.tensor_tensor(qhatT[:, :], fT[:, :], rstd2[:, :], ALU.mult),
                       waits=[(se, None)], incs=[(ve, 1)])
                kh = khatT[0]
                if dbg and h == 0:
                    p.dma("sync", T["d_qhatT"].ap()[:, :], qhatT[:, :],
                          waits=[(ve, None)], incs=[(sto, 16)])
                    p.dma("sync", T["d_khatT"].ap()[:, :], khatT[0][:, :],
                          waits=[(ve, None)], incs=[(sto, 16)])
                    p.dma("sync", T["d_v"].ap()[:, :], v_nat[0][:, :],
                          waits=[(ve, None)], incs=[(sto, 16)])
                    for _eng in ("tensor", "vector", "scalar"):
                        p.emit(_eng, lambda e: None, waits=[(sto, None)])
            else:
                g = h // 4
                p.dma("gpsimd", wql[:, :], T["l1_wq"].ap()[h].rearrange("p c j -> p (c j)"),
                      waits=[(mm, None)], incs=[(gld, 16)])
                proj_T(wql, pk)
                rms_rows(prow, pb, C.v1(4))
                p.emit("vector",
                       lambda e: e.tensor_tensor(qhatT[:, :], fT[:, :], rstd2[:, :], ALU.mult),
                       waits=[(se, None)], incs=[(ve, 1)])
                kh = khatT[g]

        if layer == 0:
            with nc.psum_tensor(f"{tag}sn{h}", [128, 2048], F32) as psn, \
                 nc.psum_tensor(f"{tag}mr{h}", [1, 128], F32) as pmr:
                for t in range(S_TILES):
                    def mmsn(e, t=t, psn=psn):
                        for kc in range(4):
                            inst = e.matmul(psn[:, kc * 512:(kc + 1) * 512],
                                            qhatT[:, t * 128:(t + 1) * 128],
                                            kh[:, kc * 512:(kc + 1) * 512],
                                            start=True, stop=True)
                        return inst
                    p.emit("tensor", mmsn, waits=[(ve, None), (se, None)], incs=[(mm, 1)])
                    p.emit("vector",
                           lambda e, t=t, psn=psn: e.reduce_max(cols[:, t:t + 1], psn[:, :],
                                                       axis=AX.X, negate=True),
                           waits=[(mm, None)], incs=[(ve, 1)])
                    p.emit("scalar",
                           lambda e, t=t, psn=psn: e.activation(e_nat[:, :], psn[:, :], AF.Exp,
                                                       bias=cols[:, t:t + 1],
                                                       accum_out=cols[:, 16 + t:17 + t]),
                           waits=[(ve, None)], incs=[(se, 1)])
                    p.emit("tensor",
                           lambda e, t=t, pmr=pmr: e.transpose(pmr[:, :], cols[:, t:t + 1],
                                                      C.idn_f32[:, :]),
                           waits=[(ve, None), (se, None), (C.ld, None)], incs=[(mm, 1)])
                    p.emit("vector",
                           lambda e, t=t, pmr=pmr: e.tensor_copy(
                               mrow[0:1, t * 128:(t + 1) * 128], pmr[0:1, :]),
                           waits=[(mm, None)], incs=[(ve, 1)])
                p.emit("vector",
                       lambda e: e.reciprocal(cols[:, 16:32], cols[:, 16:32]),
                       waits=[(se, None)], incs=[(ve, 1)])
                for t in range(S_TILES):
                    p.emit("tensor",
                           lambda e, t=t, pmr=pmr: e.transpose(pmr[:, :], cols[:, 16 + t:17 + t],
                                                      C.idn_f32[:, :]),
                           waits=[(ve, None), (C.ld, None)], incs=[(mm, 1)])
                    p.emit("vector",
                           lambda e, t=t, pmr=pmr: e.tensor_copy(
                               zrow[0:1, t * 128:(t + 1) * 128], pmr[0:1, :]),
                           waits=[(mm, None)], incs=[(ve, 1)])
            if dbg and h == 0:
                p.emit("vector", lambda e: e.tensor_copy(cols[:, 32:33], s2c[:, :]),
                       incs=[(ve, 1)])
                p.emit("vector", lambda e: e.tensor_copy(cols[:, 33:34], qs_c[:, :]),
                       incs=[(ve, 1)])
                p.dma("sync", T["d_cols"].ap()[:, :], cols[:, :],
                      waits=[(ve, None)], incs=[(sto, 16)])
                p.dma("sync", T["d_mrow"].ap()[:, :], mrow[:, :],
                      waits=[(ve, None)], incs=[(sto, 16)])
                p.dma("sync", T["d_zrow"].ap()[:, :], zrow[:, :],
                      waits=[(ve, None)], incs=[(sto, 16)])
                for _eng in ("tensor", "vector", "scalar"):
                    p.emit(_eng, lambda e: None, waits=[(sto, None)])

        with nc.psum_tensor(f"{tag}sT{h}", [128, 512], F32) as pT, \
             nc.psum_tensor(f"{tag}av{h}", [128, 512], F32) as pav, \
             nc.psum_tensor(f"{tag}zz{h}", [1, 512], F32) as pz, \
             nc.psum_tensor(f"{tag}dt{h}", [1, 512], F32) as pdot, \
             nc.psum_tensor(f"{tag}zb{h}", [128, 512], F32) as pzb:
            for qc in range(4):
                for kt in range(S_TILES):
                    def mmsT(e, kt=kt, qc=qc, pT=pT, kh=kh):
                        if layer == 0:
                            e.matmul(pT[:, :], kh[:, kt * 128:(kt + 1) * 128],
                                     qhatT[:, qc * 512:(qc + 1) * 512],
                                     start=True, stop=False)
                            return e.matmul(pT[:, :], C.ones_row_bf[:, :],
                                            mrow[0:1, qc * 512:(qc + 1) * 512],
                                            start=False, stop=True)
                        return e.matmul(pT[:, :], kh[:, kt * 128:(kt + 1) * 128],
                                        qhatT[:, qc * 512:(qc + 1) * 512],
                                        start=True, stop=True)
                    p.emit("tensor", mmsT,
                           waits=[(se, None)] + ([(ve, None)] if kt == 0 else []),
                           incs=[(mm, 1)])
                    p.emit("scalar",
                           lambda e, kt=kt, pT=pT: e.activation(eT[:, kt * 512:(kt + 1) * 512],
                                                         pT[:, :], AF.Exp),
                           waits=[(mm, None)], incs=[(se, 1)])

                def mmav(e, qc=qc, h=h, pav=pav):
                    for kt in range(S_TILES):
                        lhs = (v_nat[0][:, kt * 128:(kt + 1) * 128] if layer == 0 else
                               v_nat[:, kt * 512 + (h // 4) * 128:
                                     kt * 512 + (h // 4) * 128 + 128])
                        inst = e.matmul(pav[:, :], lhs, eT[:, kt * 512:(kt + 1) * 512],
                                        start=(kt == 0), stop=(kt == S_TILES - 1))
                    return inst
                p.emit("tensor", mmav, waits=[(se, None), (ve, None)], incs=[(mm, 1)])

                if layer == 0:
                    p.emit("tensor",
                           lambda e, qc=qc, pzb=pzb: e.matmul(pzb[:, :], C.bcast1_f32[:, :],
                                                     zrow[0:1, qc * 512:(qc + 1) * 512],
                                                     start=True, stop=True),
                           waits=[(ve, None)], incs=[(mm, 1)])
                    p.emit("vector",
                           lambda e, pzb=pzb: e.tensor_copy(zb[:, :], pzb[:, :]),
                           waits=[(mm, None)], incs=[(ve, 1)])
                    p.emit("vector",
                           lambda e, pav=pav: e.tensor_tensor(ao_p[:, :], pav[:, :], zb[:, :],
                                                     ALU.mult),
                           waits=[(sto, None), (ve, None)], incs=[(ve, 1)])
                else:
                    g = h // 4

                    def mmz(e, pz=pz):
                        for kt in range(S_TILES):
                            inst = e.matmul(pz[:, :], C.red1_bf[:, :],
                                            eT[:, kt * 512:(kt + 1) * 512],
                                            start=(kt == 0), stop=(kt == S_TILES - 1))
                        return inst
                    p.emit("tensor", mmz, waits=[(ve, None)], incs=[(mm, 1)])
                    p.emit("vector",
                           lambda e, g=g, qc=qc, pav=pav: e.tensor_tensor(
                               uv_f[:, :], pav[:, :],
                               vT_bf[g][:, qc * 512:(qc + 1) * 512], ALU.mult),
                           waits=[(mm, None)], incs=[(ve, 1)])
                    p.emit("scalar",
                           lambda e, pav=pav: e.activation(ut_f[:, :], pav[:, :], AF.Identity),
                           waits=[(mm, None), (ve, None)], incs=[(se, 1)])
                    p.emit("tensor",
                           lambda e, pdot=pdot: e.matmul(pdot[:, :], C.red1_f32[:, :], uv_f[:, :],
                                              start=True, stop=True),
                           waits=[(ve, None)], incs=[(mm, 1)])
                    p.emit("vector",
                           lambda e, qc=qc, pz=pz: e.reciprocal(
                               zrow[0:1, qc * 512:(qc + 1) * 512], pz[0:1, :]),
                           waits=[(mm, None)], incs=[(ve, 1)])
                    p.emit("vector",
                           lambda e, qc=qc, pdot=pdot: e.tensor_tensor(
                               g2row[0:1, qc * 512:(qc + 1) * 512], pdot[0:1, :],
                               zrow[0:1, qc * 512:(qc + 1) * 512], ALU.mult),
                           waits=[(mm, None), (ve, None)], incs=[(ve, 1)])
                    p.emit("tensor",
                           lambda e, qc=qc, pzb=pzb: e.matmul(pzb[:, :], C.bcast1_f32[:, :],
                                                     zrow[0:1, qc * 512:(qc + 1) * 512],
                                                     start=True, stop=True),
                           waits=[(ve, None)], incs=[(mm, 1)])
                    p.emit("vector", lambda e, pzb=pzb: e.tensor_copy(zb[:, :], pzb[:, :]),
                           waits=[(mm, None)], incs=[(ve, 1)])
                    p.emit("tensor",
                           lambda e, qc=qc, pzb=pzb: e.matmul(pzb[:, :], C.bcast1_f32[:, :],
                                                     g2row[0:1, qc * 512:(qc + 1) * 512],
                                                     start=True, stop=True),
                           waits=[(ve, None)], incs=[(mm, 1)])
                    p.emit("vector", lambda e, pzb=pzb: e.tensor_copy(g2b[:, :], pzb[:, :]),
                           waits=[(mm, None)], incs=[(ve, 1)])
                    p.emit("vector",
                           lambda e: e.tensor_tensor(ut_f[:, :], ut_f[:, :], zb[:, :],
                                                     ALU.mult),
                           waits=[(se, None), (ve, None)], incs=[(ve, 1)])
                    p.emit("vector",
                           lambda e, g=g, qc=qc: e.tensor_tensor(
                               uv_f[:, :], g2b[:, :],
                               vT_bf[g][:, qc * 512:(qc + 1) * 512], ALU.mult),
                           waits=[(ve, None)], incs=[(ve, 1)])
                    p.emit("vector",
                           lambda e: e.tensor_tensor(ao_p[:, :], ut_f[:, :], uv_f[:, :],
                                                     ALU.subtract),
                           waits=[(sto, None), (ve, None)], incs=[(ve, 1)])
                if dbg and h == 0 and qc == 0:
                    p.dma("sync", T["d_eT"].ap()[:, :], eT[:, :],
                          waits=[(se, None), (mm, None)], incs=[(sto, 16)])
                    for _eng in ("tensor", "vector", "scalar"):
                        p.emit(_eng, lambda e: None, waits=[(sto, None)])
                p.dma("sync", aoT_d[h, :, qc * 512:(qc + 1) * 512], ao_p[:, :],
                      waits=[(ve, None)], incs=[(sto, 16)])

    wo_d = T["l0_wo"].ap() if layer == 0 else T["l1_wo"].ap()
    for tg in range(4):
        for eh in range(2):
            with ExitStack() as ps:
                pw = [ps.enter_context(nc.psum_tensor(f"{tag}pw{tg}{eh}_{i}", [128, 512], F32))
                      for i in range(8)]
                for c in range(16):
                    p.dma("gpsimd", wop[:, :], wo_d[c, :, eh * 1024:(eh + 1) * 1024],
                          waits=[(mm, None)], incs=[(glw, 16)])
                    p.dma("sync", aop_sb[:, :], aoT_d[c, :, tg * 512:(tg + 1) * 512],
                          waits=[(mm, None)], incs=[(aold, 16)])

                    def mmw(e, c=c, pw=pw):
                        for i in range(4):
                            for ec in range(2):
                                inst = e.matmul(pw[i * 2 + ec][:, :],
                                                aop_sb[:, i * 128:(i + 1) * 128],
                                                wop[:, ec * 512:(ec + 1) * 512],
                                                start=(c == 0), stop=(c == 15))
                        return inst
                    p.emit("tensor", mmw,
                           waits=[(aold, None), (glw, None)] + ([(ve, None)] if c == 0 else []),
                           incs=[(mm, 1)])
                for i in range(4):
                    t = tg * 4 + i
                    p.dma("sync", xbh, xbase_d[t, :, eh * 1024:(eh + 1) * 1024],
                          waits=[(ve, None)], incs=[(xbld, 16)])
                    for ec in range(2):
                        p.emit("vector",
                               lambda e, i=i, ec=ec, pw=pw: e.tensor_tensor(
                                   fsq[:, 1024 + ec * 512:1024 + (ec + 1) * 512],
                                   fsq[:, ec * 512:(ec + 1) * 512],
                                   pw[i * 2 + ec][:, :], ALU.add),
                               waits=[(mm, None), (xbld, None), (sto, None)], incs=[(ve, 1)])
                    p.dma("sync", xout_d[t, :, eh * 1024:(eh + 1) * 1024], xoh,
                          waits=[(ve, None)], incs=[(sto, 16)])
    return sto


def _mk_dram(nc, specs):
    T = {}
    for name, shape, dt, kind in specs:
        T[name] = nc.dram_tensor(name, list(shape), dt, kind=kind)
    return T


_FFN_L0_W = [
    ("l0_wg", (64, 128, 16, 128), I8, "ExternalInput"),
    ("l0_wu", (64, 128, 16, 128), I8, "ExternalInput"),
    ("l0_wd", (64, 128, 2048), I8, "ExternalInput"),
    ("l0_sg", (128, 64), F32, "ExternalInput"),
    ("l0_su", (128, 64), F32, "ExternalInput"),
    ("l0_sdown", (1, 2048), F32, "ExternalInput"),
]
_FFN_L1_W = [
    ("l1_w1", (64, 128, 16, 128), F32, "ExternalInput"),
    ("l1_w3", (64, 128, 16, 128), F32, "ExternalInput"),
    ("l1_w2t", (64, 128, 2048), F32, "ExternalInput"),
    ("h128", (128, 128), BF16, "ExternalInput"),
    ("h2x64", (128, 128), BF16, "ExternalInput"),
    ("l1_w2a", (8192, 2048), BF16, "Internal"),
    ("l1_w2pp", (64, 128, 2048), BF16, "Internal"),
]


_ATTN_L0_W = [
    ("l0_wq", (16, 128, 16, 128), F32, "ExternalInput"),
    ("l0_wk", (16, 128, 16, 128), F32, "ExternalInput"),
    ("l0_wv", (16, 128, 16, 128), F32, "ExternalInput"),
    ("l0_wo", (16, 128, 2048), F32, "ExternalInput"),
    ("l0_qscale", (128, 1), F32, "ExternalInput"),
    ("l0_kscale", (128, 1), F32, "ExternalInput"),
]
_ATTN_L1_W = [
    ("l1_wq", (16, 128, 16, 128), F32, "ExternalInput"),
    ("l1_wk", (4, 128, 16, 128), F32, "ExternalInput"),
    ("l1_wv", (4, 128, 16, 128), F32, "ExternalInput"),
    ("l1_wv_n", (16, 128, 512), F32, "ExternalInput"),
    ("l1_wo", (16, 128, 2048), F32, "ExternalInput"),
]


def build_attn_nc(layer, dbg=False):
    nc = bass.Bass("TRN2", target_bir_lowering=False)
    specs = [
        ("xnT", (16, 128, 2048), BF16, "ExternalInput"),
        ("x_base", (16, 128, 2048), F32, "ExternalInput"),
        ("idn_bf", (128, 128), BF16, "ExternalInput"),
        ("idn_f32", (128, 128), F32, "ExternalInput"),
        ("x_out", (16, 128, 2048), F32, "ExternalOutput"),
        ("aoT", (16, 128, 2048), BF16, "ExternalOutput" if dbg else "Internal"),
    ]
    if dbg:
        specs += [
            ("d_qhatT", (128, 2048), BF16, "ExternalOutput"),
            ("d_fT", (128, 2048), F32, "ExternalOutput"),
            ("d_wql", (128, 2048), BF16, "ExternalOutput"),
            ("d_xnT0", (128, 2048), BF16, "ExternalOutput"),
            ("d_rstd2", (128, 2048), F32, "ExternalOutput"),
            ("d_khatT", (128, 2048), BF16, "ExternalOutput"),
            ("d_v", (128, 2048), BF16, "ExternalOutput"),
            ("d_cols", (128, 34), F32, "ExternalOutput"),
            ("d_mrow", (1, 2048), BF16, "ExternalOutput"),
            ("d_zrow", (1, 2048), F32, "ExternalOutput"),
            ("d_eT", (128, 8192), BF16, "ExternalOutput"),
        ]
    specs += _ATTN_L0_W if layer == 0 else _ATTN_L1_W
    with ExitStack() as ctx:
        T = _mk_dram(nc, specs)
        p = Prog(nc, ctx)
        C = Consts(nc, ctx, p, T)
        sto = emit_attn(p, C, layer, T, tag=f"a{layer}_", dbg=dbg)
        p.emit("sync", lambda e: None, waits=[(sto, None)])
        p.finalize()
    return nc


def build_ffn_nc(layer):
    nc = bass.Bass("TRN2", target_bir_lowering=False)
    specs = [
        ("x_in", (16, 128, 2048), F32, "ExternalInput"),
        ("idn_bf", (128, 128), BF16, "ExternalInput"),
        ("idn_f32", (128, 128), F32, "ExternalInput"),
        ("xnT", (16, 128, 2048), BF16, "ExternalOutput"),
        ("x_acc", (16, 128, 2048), F32, "ExternalOutput"),
    ]
    specs += _FFN_L0_W if layer == 0 else _FFN_L1_W
    with ExitStack() as ctx:
        T = _mk_dram(nc, specs)
        p = Prog(nc, ctx)
        C = Consts(nc, ctx, p, T)
        if layer == 1:
            w2done = emit_w2pp(p, C, T)
            for eng in ENGINES:
                p.emit(eng, lambda e: None, waits=[(w2done, None)])
        sto = emit_ffn(p, C, layer, T, tag=f"f{layer}_")
        p.emit("sync", lambda e: None, waits=[(sto, None)])
        p.finalize()
    return nc


def build_fused_nc():
    nc = bass.Bass("TRN2", target_bir_lowering=False)
    specs = [
        ("x_in", (16, 128, 2048), F32, "ExternalInput"),
        ("idn_bf", (128, 128), BF16, "ExternalInput"),
        ("idn_f32", (128, 128), F32, "ExternalInput"),
        ("h128", (128, 128), BF16, "ExternalInput"),
        ("h2x64", (128, 128), BF16, "ExternalInput"),
        ("out", (16, 128, 2048), F32, "ExternalOutput"),
        ("xnT_s", (16, 128, 2048), BF16, "Internal"),
        ("xacc_s", (16, 128, 2048), F32, "Internal"),
        ("x1_s", (16, 128, 2048), F32, "Internal"),
        ("aoT_s", (16, 128, 2048), BF16, "Internal"),
        ("l1_w2a", (8192, 2048), BF16, "Internal"),
        ("l1_w2pp", (64, 128, 2048), BF16, "Internal"),
    ]
    specs += _FFN_L0_W + _ATTN_L0_W
    specs += [t for t in _FFN_L1_W if t[0] in ("l1_w1", "l1_w3", "l1_w2t")]
    specs += _ATTN_L1_W
    with ExitStack() as ctx:
        T = _mk_dram(nc, specs)
        p = Prog(nc, ctx)
        C = Consts(nc, ctx, p, T)

        def phase(emitter):
            with ExitStack() as pc:
                p.alloc_ctx = pc
                emitter()
            p.alloc_ctx = ctx
            p.barrier()

        Tf0 = dict(T, x_in=T["x_in"], xnT=T["xnT_s"], x_acc=T["xacc_s"])
        Ta0 = dict(T, xnT=T["xnT_s"], x_base=T["xacc_s"], x_out=T["x1_s"], aoT=T["aoT_s"])
        Tf1 = dict(T, x_in=T["x1_s"], xnT=T["xnT_s"], x_acc=T["xacc_s"])
        Ta1 = dict(T, xnT=T["xnT_s"], x_base=T["xacc_s"], x_out=T["out"], aoT=T["aoT_s"])

        phase(lambda: emit_w2pp(p, C, T))
        phase(lambda: emit_ffn(p, C, 0, Tf0, tag="f0_"))
        phase(lambda: emit_attn(p, C, 0, Ta0, tag="a0_"))
        phase(lambda: emit_ffn(p, C, 1, Tf1, tag="f1_"))
        phase(lambda: emit_attn(p, C, 1, Ta1, tag="a1_"))
        p.finalize()
    return nc


_BUILD_CACHE = {}


def _get_nc(key, builder):
    if key not in _BUILD_CACHE:
        _BUILD_CACHE[key] = builder()
    return _BUILD_CACHE[key]


def kernel_fused(**inputs):
    w = prep_weights(inputs)
    c = host_consts()
    x = inputs["L0_attn_x"]
    nc = _get_nc("fused", build_fused_nc)
    base = {"idn_bf": c["idn_bf"], "idn_f32": c["idn_f32"],
            "h128": c["h128"], "h2x64": c["h2x64"]}
    for k in ("l0_wg", "l0_wu", "l0_wd", "l0_sg", "l0_su", "l0_sdown",
              "l0_wq", "l0_wk", "l0_wv", "l0_wo", "l0_qscale", "l0_kscale",
              "l1_w1", "l1_w3", "l1_w2t",
              "l1_wq", "l1_wk", "l1_wv", "l1_wv_n", "l1_wo"):
        base[k] = w[k]
    maps = [dict(base, x_in=np.ascontiguousarray(x[i].reshape(16, 128, 2048)))
            for i in range(NCORES)]
    res = run_bass_kernel_spmd(nc, maps, core_ids=list(range(NCORES)))
    out = np.empty((NCORES, 2048, 2048), dtype=np.float32)
    for i in range(NCORES):
        out[i] = res.results[i]["out"].reshape(2048, 2048)
    return out


def kernel(**inputs):
    return kernel_fused(**inputs)


def kernel_unfused(**inputs):
    w = prep_weights(inputs)
    c = host_consts()
    x = inputs["L0_attn_x"]
    consts = {"idn_bf": c["idn_bf"], "idn_f32": c["idn_f32"]}

    def shard(arr):
        return [np.ascontiguousarray(arr[cid].reshape(16, 128, 2048))
                for cid in range(NCORES)]

    core_ids = list(range(NCORES))

    nc = _get_nc("ffn0", lambda: build_ffn_nc(0))
    base = dict(consts)
    for k in ("l0_wg", "l0_wu", "l0_wd", "l0_sg", "l0_su", "l0_sdown"):
        base[k] = w[k]
    xs = shard(x)
    maps = [dict(base, x_in=xs[i]) for i in range(NCORES)]
    res = run_bass_kernel_spmd(nc, maps, core_ids=core_ids)
    xnT = [r["xnT"] for r in res.results]
    xacc = [r["x_acc"] for r in res.results]

    nc = _get_nc("attn0", lambda: build_attn_nc(0))
    base = dict(consts)
    for k in ("l0_wq", "l0_wk", "l0_wv", "l0_wo", "l0_qscale", "l0_kscale"):
        base[k] = w[k]
    maps = [dict(base, xnT=xnT[i], x_base=xacc[i]) for i in range(NCORES)]
    res = run_bass_kernel_spmd(nc, maps, core_ids=core_ids)
    x1 = [r["x_out"] for r in res.results]

    nc = _get_nc("ffn1", lambda: build_ffn_nc(1))
    base = dict(consts)
    for k in ("l1_w1", "l1_w3", "l1_w2t"):
        base[k] = w[k]
    base["h128"] = c["h128"]
    base["h2x64"] = c["h2x64"]
    maps = [dict(base, x_in=x1[i]) for i in range(NCORES)]
    res = run_bass_kernel_spmd(nc, maps, core_ids=core_ids)
    xnT = [r["xnT"] for r in res.results]
    xacc = [r["x_acc"] for r in res.results]

    nc = _get_nc("attn1", lambda: build_attn_nc(1))
    base = dict(consts)
    for k in ("l1_wq", "l1_wk", "l1_wv", "l1_wv_n", "l1_wo"):
        base[k] = w[k]
    maps = [dict(base, xnT=xnT[i], x_base=xacc[i]) for i in range(NCORES)]
    res = run_bass_kernel_spmd(nc, maps, core_ids=core_ids)

    out = np.empty((NCORES, 2048, 2048), dtype=np.float32)
    for i in range(NCORES):
        out[i] = res.results[i]["x_out"].reshape(2048, 2048)
    return out


# version 07
# speedup vs baseline: 1.4266x; 1.4266x over previous
import numpy as np
import ml_dtypes
from contextlib import ExitStack

import concourse.bass as bass
import concourse.mybir as mybir
from concourse.bass_utils import run_bass_kernel_spmd

F32 = mybir.dt.float32
BF16 = mybir.dt.bfloat16
I8 = mybir.dt.int8
AF = mybir.ActivationFunctionType
ALU = mybir.AluOpType
AX = mybir.AxisListType

NCORES = 8
S_TILES = 16
E_BLKS = 16
O_TILES = 64
CHUNK = 512
N_CHUNKS = 4

RMS_EPS_L0 = 1e-6
RMS_EPS_L1 = float(np.finfo(np.float32).eps)
LN_EPS = 1e-5
FWHT_SCALE = float(1.0 / np.sqrt(8192.0))
SM_SCALE_L1 = float(128.0 ** -0.5)

ENGINES = ("sync", "gpsimd", "scalar", "vector", "tensor")


class Sem:
    def __init__(self, nc, ctx, name):
        self.h = ctx.enter_context(nc.semaphore(name))
        self.n = 0


class Prog:

    def __init__(self, nc, ctx):
        self.nc = nc
        self.ctx = ctx
        self.alloc_ctx = ctx
        self.q = {e: [] for e in ENGINES}
        self._sems = {}

    def barrier(self):
        waits = [(s, None) for s in self._sems.values() if s.n > 0]
        for eng in ENGINES:
            self.emit(eng, lambda e: None, waits=waits)

    def sem(self, name):
        if name not in self._sems:
            self._sems[name] = Sem(self.nc, self.ctx, name)
        return self._sems[name]

    def emit(self, eng, fn, waits=(), incs=()):
        wlist = [(s.h, (s.n if v is None else v)) for (s, v) in waits]
        ilist = []
        for s, k in incs:
            s.n += k
            ilist.append((s.h, k))

        def run(e):
            for h, v in wlist:
                if v > 0:
                    e.wait_ge(h, v)
            inst = fn(e)
            for h, k in ilist:
                assert inst is not None
                inst = inst.then_inc(h, k)
        self.q[eng].append(run)

    def dma(self, eng, out_ap, in_ap, waits=(), incs=()):
        self.emit(eng, lambda e: e.dma_start(out_ap, in_ap), waits, incs)

    def finalize(self):
        with self.nc.Block() as b:
            for name in ENGINES:
                fns = self.q[name]
                if not fns:
                    continue
                getattr(b, name)(lambda e, fns=fns: [f(e) for f in fns])


def _c(a):
    return np.ascontiguousarray(a)


def prep_weights(ins):
    w = {}
    for nm in ("wq", "wk", "wv"):
        a = ins[f"L0_attn_{nm}"]
        w[f"l0_{nm}"] = _c(a.reshape(16, 128, 16, 128).transpose(2, 1, 0, 3))
    w["l0_wo"] = _c(ins["L0_attn_wo"].reshape(16, 128, 2048))
    w["l0_qscale"] = _c(ins["L0_attn_q_scale"].reshape(128, 1))
    w["l0_kscale"] = _c(ins["L0_attn_k_scale"].reshape(128, 1))
    for nm, key in (("wg", "L0_ffn_w_gate"), ("wu", "L0_ffn_w_up")):
        a = ins[key]
        w[f"l0_{nm}"] = _c(a.reshape(16, 128, 64, 128).transpose(2, 1, 0, 3))
    w["l0_wd"] = _c(ins["L0_ffn_w_down"].reshape(64, 128, 2048))
    w["l0_sg"] = _c(ins["L0_ffn_s_gate"].reshape(64, 128).T)
    w["l0_su"] = _c(ins["L0_ffn_s_up"].reshape(64, 128).T)
    w["l0_sdown"] = _c(ins["L0_ffn_s_down"].reshape(1, 2048))
    a = ins["L1_attn_Wq"]
    w["l1_wq"] = _c(a.reshape(16, 128, 16, 128).transpose(0, 3, 2, 1))
    for nm in ("Wk", "Wv"):
        a = ins[f"L1_attn_{nm}"]
        w[f"l1_{nm.lower()}"] = _c(a.reshape(4, 128, 16, 128).transpose(0, 3, 2, 1))
    w["l1_wv_n"] = _c(ins["L1_attn_Wv"].T.reshape(16, 128, 512))
    w["l1_wo"] = _c(ins["L1_attn_Wo"].T.reshape(16, 128, 2048))
    for nm, key in (("w1", "L1_ffn_W1"), ("w3", "L1_ffn_W3")):
        a = ins[key]
        w[f"l1_{nm}"] = _c(a.reshape(64, 128, 16, 128).transpose(0, 3, 2, 1))
    w["l1_w2t"] = _c(ins["L1_ffn_W2"].T.reshape(64, 128, 2048))
    return w


def host_consts():
    c = {}
    c["idn_bf"] = np.eye(128, dtype=ml_dtypes.bfloat16)
    c["idn_f32"] = np.eye(128, dtype=np.float32)

    def had(n):
        h = np.array([[1.0]])
        while h.shape[0] < n:
            h = np.block([[h, h], [h, -h]])
        return h
    c["h128"] = had(128).astype(ml_dtypes.bfloat16)
    h64 = had(64)
    z = np.zeros((64, 64))
    c["h2x64"] = np.block([[h64, z], [z, h64]]).astype(ml_dtypes.bfloat16)
    return c


class Consts:
    def __init__(self, nc, ctx, p, T):
        sb = lambda name, shape, dt: ctx.enter_context(nc.sbuf_tensor(name, shape, dt))
        self.idn_bf = sb("c_idn_bf", [128, 128], BF16)
        self.idn_f32 = sb("c_idn_f32", [128, 128], F32)
        self.bcast1_f32 = sb("c_bcast1", [1, 128], F32)
        self.ones_row_bf = sb("c_ones_row_bf", [1, 128], BF16)
        self.red1_f32 = sb("c_red1_f", [128, 1], F32)
        self.red1_bf = sb("c_red1_b", [128, 1], BF16)
        self.vals = sb("c_vals", [128, 12], F32)
        ld = p.sem("c_ld")
        p.dma("sync", self.idn_bf[:, :], T["idn_bf"].ap()[:, :], incs=[(ld, 16)])
        p.dma("sync", self.idn_f32[:, :], T["idn_f32"].ap()[:, :], incs=[(ld, 16)])
        g = p.sem("c_set")
        for col, v in ((0, -1.0 / 2048.0), (1, 1.0 / 2048.0), (2, LN_EPS),
                       (3, RMS_EPS_L0), (4, RMS_EPS_L1), (5, FWHT_SCALE),
                       (6, 1.0 / 128.0), (7, 1e-24), (8, SM_SCALE_L1)):
            p.emit("vector", lambda e, col=col, v=v: e.memset(self.vals[:, col:col + 1], v),
                   incs=[(g, 1)])
        for t in (self.bcast1_f32, self.ones_row_bf, self.red1_f32, self.red1_bf):
            p.emit("vector", lambda e, t=t: e.memset(t[:, :], 1.0), incs=[(g, 1)])
        self.ld = ld
        self.ready = g

    def v(self, col):
        return self.vals[:, col:col + 1]

    def v1(self, col):
        return self.vals[0:1, col:col + 1]



def emit_ffn(p, C, layer, T, tag):
    nc = p.nc
    ctx = p.alloc_ctx
    sb = lambda name, shape, dt: ctx.enter_context(nc.sbuf_tensor(tag + name, shape, dt))

    x_in = T["x_in"].ap()
    xnT_d = T["xnT"].ap()
    xacc_d = T["x_acc"].ap()
    if layer == 0:
        wg_d, wu_d, wd_d = T["l0_wg"].ap(), T["l0_wu"].ap(), T["l0_wd"].ap()
    else:
        wg_d, wu_d, wd_d = T["l1_w1"].ap(), T["l1_w3"].ap(), T["l1_w2pp"].ap()

    xbuf = sb("xbuf", [128, 2048], F32)
    xsq = sb("xsq", [128, 2048], F32)
    st = sb("stats", [128, 8], F32)
    xn_bf = sb("xn_bf", [128, 2048], BF16)
    xnT_sb = sb("xnT_sb", [128, E_BLKS * CHUNK], BF16)
    hidT = sb("hidT", [128, O_TILES * CHUNK], BF16)
    wgl = [sb(f"wgl{b}", [128, 2048], BF16) for b in range(2)]
    wul = [sb(f"wul{b}", [128, 2048], BF16) for b in range(2)]
    wdp = [sb(f"wdp{b}", [128, 4096], BF16) for b in range(2)]
    silu_s = sb("silu_s", [128, CHUNK], F32)
    up_s = sb("up_s", [128, CHUNK], F32)
    xacc = [sb(f"xacc{i}", [128, 2048], F32) for i in range(4)]
    if layer == 0:
        sg_sb = sb("sg", [128, 64], F32)
        su_sb = sb("su", [128, 64], F32)
        sdb = sb("sdb", [128, 2048], F32)

    xld = p.sem(tag + "xld")
    ald = p.sem(tag + "ald")
    wld = p.sem(tag + "wld")
    ggu = [p.sem(tag + f"ggu{b}") for b in range(2)]
    gwd = [p.sem(tag + f"gwd{b}") for b in range(2)]
    lnv = p.sem(tag + "lnv")
    trp = p.sem(tag + "trp")
    xnv = p.sem(tag + "xnv")
    mmg = p.sem(tag + "mmg")
    ev = p.sem(tag + "ev")
    mmd = p.sem(tag + "mmd")
    dev = p.sem(tag + "dev")
    sto = p.sem(tag + "sto")

    if layer == 0:
        p.dma("sync", sg_sb[:, :], T["l0_sg"].ap()[:, :], incs=[(wld, 16)])
        p.dma("sync", su_sb[:, :], T["l0_su"].ap()[:, :], incs=[(wld, 16)])
        sdrow = sb("sdrow", [1, 2048], F32)
        p.dma("sync", sdrow[:, :], T["l0_sdown"].ap()[:, :], incs=[(wld, 16)])
        with nc.psum_tensor(tag + "pbc", [128, 512], F32) as pbc:
            for i in range(4):
                p.emit("tensor",
                       lambda e, i=i, pbc=pbc: e.matmul(pbc[:, :], C.bcast1_f32[:, :],
                                               sdrow[:, i * 512:(i + 1) * 512],
                                               start=True, stop=True),
                       waits=[(wld, None), (C.ready, None), (ev, None)], incs=[(mmg, 1)])
                p.emit("vector",
                       lambda e, i=i, pbc=pbc: e.tensor_copy(sdb[:, i * 512:(i + 1) * 512], pbc[:, :]),
                       waits=[(mmg, None)], incs=[(ev, 1)])

    gu_ev_done = {}
    gu_mm_done = {}
    dq_mm_done = {}
    for q in range(N_CHUNKS):
        tiles = [q * 4 + i for i in range(4)]
        with nc.psum_tensor(f"{tag}ptr{q}", [128, 512], BF16) as ptr, \
             nc.psum_tensor(f"{tag}pg{q}a", [128, 512], F32) as pg0, \
             nc.psum_tensor(f"{tag}pg{q}b", [128, 512], F32) as pg1, \
             nc.psum_tensor(f"{tag}pu{q}a", [128, 512], F32) as pu0, \
             nc.psum_tensor(f"{tag}pu{q}b", [128, 512], F32) as pu1:
            pg = [pg0, pg1]
            pu = [pu0, pu1]
            for i, t in enumerate(tiles):
                p.dma("sync", xbuf[:, :], x_in[t], waits=[(lnv, None)], incs=[(xld, 16)])
                p.emit("vector", lambda e: e.reduce_sum(st[:, 0:1], xbuf[:, :], axis=AX.X),
                       waits=[(xld, None)], incs=[(lnv, 1)])
                p.emit("scalar", lambda e: e.activation(xsq[:, :], xbuf[:, :], AF.Square),
                       waits=[(xld, None), (lnv, None)], incs=[(lnv, 1)])
                p.emit("vector", lambda e: e.reduce_sum(st[:, 1:2], xsq[:, :], axis=AX.X),
                       waits=[(lnv, None)], incs=[(lnv, 1)])
                p.emit("vector", lambda e: e.tensor_scalar(st[:, 2:3], st[:, 0:1],
                                                           C.v(0), None, ALU.mult),
                       waits=[(lnv, None)], incs=[(lnv, 1)])
                p.emit("vector", lambda e: e.tensor_tensor(st[:, 3:4], st[:, 2:3], st[:, 2:3],
                                                           ALU.mult),
                       waits=[(lnv, None)], incs=[(lnv, 1)])
                p.emit("vector", lambda e: e.scalar_tensor_tensor(st[:, 4:5], st[:, 1:2],
                                                                  C.v(1), st[:, 3:4],
                                                                  ALU.mult, ALU.subtract),
                       waits=[(lnv, None)], incs=[(lnv, 1)])
                p.emit("scalar", lambda e: e.activation(st[:, 5:6], st[:, 4:5], AF.Ln,
                                                        bias=C.v(2)),
                       waits=[(lnv, None)], incs=[(lnv, 1)])
                p.emit("scalar", lambda e: e.activation(st[:, 6:7], st[:, 5:6], AF.Exp,
                                                        scale=-0.5),
                       waits=[(lnv, None)], incs=[(lnv, 1)])
                p.emit("vector", lambda e: e.tensor_scalar(xn_bf[:, :], xbuf[:, :],
                                                           st[:, 2:3], st[:, 6:7],
                                                           ALU.add, ALU.mult),
                       waits=[(lnv, None), (trp, None)], incs=[(lnv, 1)])
                for gblk in range(4):
                    for j in range(4):
                        eb = gblk * 4 + j
                        w = [(lnv, None), (C.ld, None), (xnv, None), (dev, None),
                             (ev, None)] if j == 0 else []
                        p.emit("tensor",
                               lambda e, eb=eb, j=j, ptr=ptr: e.transpose(
                                   ptr[:, j * 128:(j + 1) * 128],
                                   xn_bf[:, eb * 128:(eb + 1) * 128],
                                   C.idn_bf[:, :]),
                               waits=w, incs=([(trp, 1)] if j == 3 else []))
                    dst = xnT_sb[:, :].rearrange("p (eb s) -> p eb s", eb=E_BLKS)
                    p.emit("vector",
                           lambda e, gblk=gblk, i=i, dst=dst, ptr=ptr: e.tensor_copy(
                               dst[:, gblk * 4:gblk * 4 + 4, i * 128:(i + 1) * 128],
                               ptr[:, :].rearrange("p (j s) -> p j s", j=4)),
                           waits=[(trp, None), (sto, None)], incs=[(xnv, 1)])
            for eb in range(E_BLKS):
                p.dma("sync", xnT_d[eb, :, q * CHUNK:(q + 1) * CHUNK],
                      xnT_sb[:, eb * CHUNK:(eb + 1) * CHUNK],
                      waits=[(xnv, None)], incs=[(sto, 16)])

            for ot in range(O_TILES):
                b = ot & 1
                gq = q * O_TILES + ot
                wg_w = [(mmg, gu_mm_done.get(gq - 2, 0))]
                p.dma("gpsimd", wgl[b][:, :], wg_d[ot].rearrange("p c j -> p (c j)"),
                      waits=wg_w, incs=[(ggu[b], 16)])
                p.dma("gpsimd", wul[b][:, :], wu_d[ot].rearrange("p c j -> p (c j)"),
                      incs=[(ggu[b], 16)])

                def mm_gate(e, pg=pg, b=b):
                    for c in range(E_BLKS):
                        inst = e.matmul(pg[b][:, :], wgl[b][:, c * 128:(c + 1) * 128],
                                        xnT_sb[:, c * CHUNK:(c + 1) * CHUNK],
                                        start=(c == 0), stop=(c == E_BLKS - 1))
                    return inst
                g_w = [(ggu[b], None), (xnv, None), (dev, None)]
                if gq >= 2:
                    g_w.append((ev, gu_ev_done[gq - 2]))
                else:
                    g_w.append((ev, None))
                p.emit("tensor", mm_gate, waits=g_w, incs=[(mmg, 1)])

                def mm_up(e, pu=pu, b=b):
                    for c in range(E_BLKS):
                        inst = e.matmul(pu[b][:, :], wul[b][:, c * 128:(c + 1) * 128],
                                        xnT_sb[:, c * CHUNK:(c + 1) * CHUNK],
                                        start=(c == 0), stop=(c == E_BLKS - 1))
                    return inst
                p.emit("tensor", mm_up, incs=[(mmg, 1)])

                if layer == 0:
                    p.emit("scalar",
                           lambda e, ot=ot, pg=pg, b=b: e.activation(
                               silu_s[:, :], pg[b][:, :], AF.Silu,
                               scale=sg_sb[:, ot:ot + 1]),
                           waits=[(mmg, None), (ev, None), (wld, None)], incs=[(ev, 1)])
                    p.emit("vector",
                           lambda e, ot=ot, pu=pu, b=b: e.tensor_scalar(
                               up_s[:, :], pu[b][:, :],
                               su_sb[:, ot:ot + 1], None, ALU.mult),
                           waits=[(mmg, None), (wld, None)], incs=[(ev, 1)])
                    p.emit("vector",
                           lambda e, ot=ot: e.tensor_tensor(
                               hidT[:, ot * CHUNK:(ot + 1) * CHUNK],
                               silu_s[:, :], up_s[:, :], ALU.mult),
                           waits=[(ev, None)], incs=[(ev, 1)])
                else:
                    p.emit("scalar",
                           lambda e, pg=pg, b=b: e.activation(silu_s[:, :], pg[b][:, :],
                                                              AF.Silu),
                           waits=[(mmg, None), (ev, None)], incs=[(ev, 1)])
                    p.emit("vector",
                           lambda e, ot=ot, pu=pu, b=b: e.tensor_tensor(
                               hidT[:, ot * CHUNK:(ot + 1) * CHUNK],
                               silu_s[:, :], pu[b][:, :], ALU.mult),
                           waits=[(mmg, None), (ev, None)], incs=[(ev, 1)])
                gu_ev_done[gq] = ev.n
                gu_mm_done[gq] = mmg.n

        for i, t in enumerate(tiles):
            p.dma("sync", xacc[i][:, :], x_in[t], waits=[(dev, None)], incs=[(ald, 16)])
        with ExitStack() as ps:
            pd = [ps.enter_context(nc.psum_tensor(f"{tag}pd{q}_{i}", [128, 512], F32))
                  for i in range(8)]
            for dh in range(2):
                for qt in range(O_TILES // 4):
                    b = qt & 1
                    gqt = (q * 2 + dh) * (O_TILES // 4) + qt
                    d_w = [(mmd, dq_mm_done.get(gqt - 2, 0))]
                    p.dma("gpsimd", wdp[b][:, :].rearrange("p (t d) -> p t d", t=4),
                          wd_d[4 * qt:4 * qt + 4, :, dh * 1024:(dh + 1) * 1024]
                          .rearrange("t p d -> p t d"),
                          waits=d_w, incs=[(gwd[b], 16)])
                    for j in range(4):
                        tt = 4 * qt + j

                        def mm_down(e, tt=tt, j=j, b=b, pd=pd):
                            for i in range(4):
                                for dc in range(2):
                                    inst = e.matmul(
                                        pd[i * 2 + dc][:, :],
                                        hidT[:, tt * CHUNK + i * 128: tt * CHUNK + (i + 1) * 128],
                                        wdp[b][:, j * 1024 + dc * 512: j * 1024 + (dc + 1) * 512],
                                        start=(tt == 0), stop=(tt == O_TILES - 1))
                            return inst
                        mw = [(gwd[b], None)]
                        if tt == 0:
                            mw += [(ev, None), (dev, None)]
                        p.emit("tensor", mm_down, waits=mw, incs=[(mmd, 1)])
                    dq_mm_done[gqt] = mmd.n
                for i in range(4):
                    for dc in range(2):
                        col = dh * 1024 + dc * 512
                        if layer == 0:
                            p.emit("vector",
                                   lambda e, i=i, dc=dc, col=col, pd=pd: e.tensor_tensor(
                                       up_s[:, :], pd[i * 2 + dc][:, :],
                                       sdb[:, col:col + 512], ALU.mult),
                                   waits=[(mmd, None), (ald, None), (dev, None)],
                                   incs=[(dev, 1)])
                            p.emit("vector",
                                   lambda e, i=i, col=col: e.tensor_tensor(
                                       xacc[i][:, col:col + 512], xacc[i][:, col:col + 512],
                                       up_s[:, :], ALU.add),
                                   waits=[(dev, None)], incs=[(dev, 1)])
                        else:
                            p.emit("vector",
                                   lambda e, i=i, dc=dc, col=col, pd=pd: e.tensor_tensor(
                                       xacc[i][:, col:col + 512], xacc[i][:, col:col + 512],
                                       pd[i * 2 + dc][:, :], ALU.add),
                                   waits=[(mmd, None), (ald, None)], incs=[(dev, 1)])
        for i, t in enumerate(tiles):
            p.dma("sync", xacc_d[t], xacc[i][:, :], waits=[(dev, None)], incs=[(sto, 16)])

    return sto



def emit_w2pp(p, C, T, tag="w2_"):
    nc = p.nc
    ctx = p.alloc_ctx
    sb = lambda name, shape, dt: ctx.enter_context(nc.sbuf_tensor(tag + name, shape, dt))
    w2t = T["l1_w2t"].ap()
    w2a = T["l1_w2a"].ap()
    w2pp_flat = T["l1_w2pp"].ap().rearrange("t p d -> (t p) d")

    h128 = sb("h128", [128, 128], BF16)
    h2x64 = sb("h2x64", [128, 128], BF16)
    win = sb("win", [128, 2048], BF16)
    wout = sb("wout", [128, 2048], BF16)
    ld = p.sem(tag + "ld")
    gld = p.sem(tag + "gld")
    mm = p.sem(tag + "mm")
    evs = p.sem(tag + "ev")
    sto = p.sem(tag + "sto")
    p.dma("sync", h128[:, :], T["h128"].ap()[:, :], incs=[(ld, 16)])
    p.dma("sync", h2x64[:, :], T["h2x64"].ap()[:, :], incs=[(ld, 16)])

    w2a_v = w2a.rearrange("(a b tt) d -> a b tt d", a=64, b=2)
    w2a_g = w2a.rearrange("(g q) d -> g q d", g=64)
    w2pp_v = w2pp_flat.rearrange("(tau gb) d -> tau gb d", gb=128)

    with nc.psum_tensor(tag + "ps", [128, 512], F32) as ps:
        for t in range(64):
            p.dma("gpsimd", win[:, :], w2t[t], waits=[(sto, None)], incs=[(gld, 16)])
            for dchunk in range(4):
                p.emit("tensor",
                       lambda e, dchunk=dchunk: e.matmul(
                           ps[:, :], h128[:, :], win[:, dchunk * 512:(dchunk + 1) * 512],
                           start=True, stop=True),
                       waits=[(ld, None), (gld, None), (evs, None)], incs=[(mm, 1)])
                p.emit("vector",
                       lambda e, dchunk=dchunk: e.tensor_copy(
                           wout[:, dchunk * 512:(dchunk + 1) * 512], ps[:, :]),
                       waits=[(mm, None)], incs=[(evs, 1)])
            p.dma("sync", w2a_v[:, :, t, :].rearrange("a b d -> (a b) d"), wout[:, :],
                  waits=[(evs, None)], incs=[(sto, 16)])
        for g in range(64):
            p.dma("gpsimd", win[:, :], w2a_g[g], waits=[(sto, None)], incs=[(gld, 16)])
            for dchunk in range(4):
                p.emit("tensor",
                       lambda e, dchunk=dchunk: e.matmul(
                           ps[:, :], h2x64[:, :], win[:, dchunk * 512:(dchunk + 1) * 512],
                           start=True, stop=True),
                       waits=[(ld, None), (gld, None), (evs, None)], incs=[(mm, 1)])
                p.emit("vector",
                       lambda e, dchunk=dchunk: e.tensor_scalar(
                           wout[:, dchunk * 512:(dchunk + 1) * 512], ps[:, :],
                           C.v(5), None, ALU.mult),
                       waits=[(mm, None), (C.ready, None)], incs=[(evs, 1)])
            for beta in range(2):
                p.dma("sync", w2pp_v[:, g * 2 + beta, :], wout[beta * 64:(beta + 1) * 64, :],
                      waits=[(evs, None)], incs=[(sto, 16)])
    return sto



def emit_attn(p, C, layer, T, tag, dbg=False):
    nc = p.nc
    ctx = p.alloc_ctx
    sb = lambda name, shape, dt: ctx.enter_context(nc.sbuf_tensor(tag + name, shape, dt))

    xnT_d = T["xnT"].ap()
    xbase_d = T["x_base"].ap()
    xout_d = T["x_out"].ap()
    aoT_d = T["aoT"].ap()
    NH = 16

    xnT_sb = sb("xnT", [128, E_BLKS * 2048], BF16)
    wql = sb("wql", [128, 2048], BF16)
    wkl = sb("wkl", [128, 2048], BF16)
    wvl = sb("wvl", [128, 2048], BF16)
    fT = sb("fT", [128, 2048], F32)
    fsq = sb("fsq", [128, 2048], F32)
    rstd2 = sb("rstd2", [128, 2048], F32)
    lnrow = sb("lnrow", [1, 2048], F32)
    qhatT = sb("qhatT", [128, 2048], BF16)
    eT = sb("eT", [128, 16 * 512], BF16)
    cols = sb("cols", [128, 34], F32)
    zrow = sb("zrow", [1, 2048], F32)
    zb = sb("zb", [128, 512], F32)
    ao_p = sb("ao_p", [128, 512], BF16)
    xbh = fsq[:, 0:1024]
    xoh = fsq[:, 1024:2048]
    aop_sb = sb("aop", [128, 512], BF16)
    wop = sb("wop", [128, 1024], BF16)
    if layer == 0:
        e_nat = sb("e_nat", [128, 2048], BF16)
        mrow = sb("mrow", [1, 2048], BF16)
        khatT = [sb("khatT", [128, 2048], BF16)]
        v_nat = [sb("v", [128, 2048], BF16)]
        qs_c = sb("qs_c", [128, 1], F32)
        ks_c = sb("ks_c", [128, 1], F32)
        s2c = sb("s2c", [128, 1], F32)
    else:
        khatT = [sb(f"khatT{g}", [128, 2048], BF16) for g in range(4)]
        vT_bf = [sb(f"vT{g}", [128, 2048], BF16) for g in range(4)]
        v_nat = sb("v", [128, 16 * 512], BF16)
        ut_f = sb("ut_f", [128, 512], F32)
        uv_f = sb("uv_f", [128, 512], F32)
        g2b = sb("g2b", [128, 512], F32)
        g2row = sb("g2row", [1, 2048], F32)

    ld = p.sem(tag + "ld")
    aold = p.sem(tag + "aold")
    xbld = p.sem(tag + "xbld")
    gld = p.sem(tag + "gld")
    gvn = p.sem(tag + "gvn")
    glw = p.sem(tag + "glw")
    mm = p.sem(tag + "mm")
    se = p.sem(tag + "se")
    ve = p.sem(tag + "ve")
    sto = p.sem(tag + "sto")

    for eb in range(E_BLKS):
        p.dma("sync", xnT_sb[:, eb * 2048:(eb + 1) * 2048], xnT_d[eb], incs=[(ld, 16)])
    if layer == 0:
        p.dma("sync", qs_c[:, :], T["l0_qscale"].ap()[:, :], incs=[(ld, 16)])
        p.dma("sync", ks_c[:, :], T["l0_kscale"].ap()[:, :], incs=[(ld, 16)])
        p.emit("vector", lambda e: e.tensor_tensor(s2c[:, :], qs_c[:, :], ks_c[:, :],
                                                   ALU.mult),
               waits=[(ld, None)], incs=[(ve, 1)])

    def proj_T(wsb, psum):
        for sc in range(4):
            def mmf(e, sc=sc):
                for c in range(E_BLKS):
                    inst = e.matmul(psum[:, :], wsb[:, c * 128:(c + 1) * 128],
                                    xnT_sb[:, c * 2048 + sc * 512: c * 2048 + (sc + 1) * 512],
                                    start=(c == 0), stop=(c == E_BLKS - 1))
                return inst
            p.emit("tensor", mmf, waits=[(ld, None), (gld, None), (se, None), (ve, None)],
                   incs=[(mm, 1)])
            p.emit("scalar",
                   lambda e, sc=sc: e.activation(fT[:, sc * 512:(sc + 1) * 512],
                                                 psum[:, :], AF.Identity),
                   waits=[(mm, None), (ve, None)], incs=[(se, 1)])

    def rms_rows(prow, pb, eps_col):
        p.emit("vector", lambda e: e.tensor_tensor(fsq[:, :], fT[:, :], fT[:, :], ALU.mult),
               waits=[(se, None), (mm, None)], incs=[(ve, 1)])
        for sc in range(4):
            p.emit("tensor",
                   lambda e, sc=sc: e.matmul(prow[:, :], C.red1_f32[:, :],
                                             fsq[:, sc * 512:(sc + 1) * 512],
                                             start=True, stop=True),
                   waits=[(ve, None), (se, None)], incs=[(mm, 1)])
            p.emit("scalar",
                   lambda e, sc=sc: e.activation(lnrow[0:1, sc * 512:(sc + 1) * 512],
                                                 prow[0:1, :], AF.Ln,
                                                 bias=eps_col, scale=C.v1(6)),
                   waits=[(mm, None)], incs=[(se, 1)])
        for sc in range(4):
            p.emit("tensor",
                   lambda e, sc=sc: e.matmul(pb[:, :], C.bcast1_f32[:, :],
                                             lnrow[0:1, sc * 512:(sc + 1) * 512],
                                             start=True, stop=True),
                   waits=[(se, None), (ve, None)], incs=[(mm, 1)])
            p.emit("scalar",
                   lambda e, sc=sc: e.activation(rstd2[:, sc * 512:(sc + 1) * 512],
                                                 pb[:, :], AF.Exp, scale=-0.5),
                   waits=[(mm, None), (ve, None), (se, None)], incs=[(se, 1)])

    if layer == 1:
        wvn = eT
        for c in range(E_BLKS):
            p.dma("gpsimd", wvn[:, c * 512:(c + 1) * 512], T["l1_wv_n"].ap()[c],
                  incs=[(gvn, 16)])
        with nc.psum_tensor(tag + "pvn", [128, 512], F32) as pv:
            for t in range(S_TILES):
                def mmv(e, t=t, pv=pv):
                    for c in range(E_BLKS):
                        inst = e.matmul(pv[:, :],
                                        xnT_sb[:, c * 2048 + t * 128: c * 2048 + (t + 1) * 128],
                                        wvn[:, c * 512:(c + 1) * 512],
                                        start=(c == 0), stop=(c == E_BLKS - 1))
                    return inst
                p.emit("tensor", mmv, waits=[(ld, None), (gvn, None), (ve, None)],
                       incs=[(mm, 1)])
                p.emit("vector",
                       lambda e, t=t, pv=pv: e.tensor_copy(v_nat[:, t * 512:(t + 1) * 512], pv[:, :]),
                       waits=[(mm, None)], incs=[(ve, 1)])
        for g in range(4):
            with nc.psum_tensor(f"{tag}kk{g}", [128, 512], F32) as pk, \
                 nc.psum_tensor(f"{tag}kr{g}", [1, 512], F32) as prow, \
                 nc.psum_tensor(f"{tag}kb{g}", [128, 512], F32) as pb:
                p.dma("gpsimd", wkl[:, :], T["l1_wk"].ap()[g].rearrange("p c j -> p (c j)"),
                      waits=[(mm, None)], incs=[(gld, 16)])
                proj_T(wkl, pk)
                rms_rows(prow, pb, C.v1(4))
                p.emit("vector",
                       lambda e, g=g: e.scalar_tensor_tensor(khatT[g][:, :], fT[:, :],
                                                             C.v(8), rstd2[:, :],
                                                             ALU.mult, ALU.mult),
                       waits=[(se, None)], incs=[(ve, 1)])
                p.dma("gpsimd", wvl[:, :], T["l1_wv"].ap()[g].rearrange("p c j -> p (c j)"),
                      waits=[(ve, None)], incs=[(gld, 16)])
                proj_T(wvl, pk)
                p.emit("vector", lambda e: e.tensor_tensor(fsq[:, :], fT[:, :], fT[:, :],
                                                           ALU.mult),
                       waits=[(se, None)], incs=[(ve, 1)])
                for sc in range(4):
                    p.emit("tensor",
                           lambda e, sc=sc, prow=prow: e.matmul(prow[:, :], C.red1_f32[:, :],
                                                     fsq[:, sc * 512:(sc + 1) * 512],
                                                     start=True, stop=True),
                           waits=[(ve, None), (se, None)], incs=[(mm, 1)])
                    p.emit("vector",
                           lambda e, sc=sc, prow=prow: e.tensor_scalar(
                               lnrow[0:1, sc * 512:(sc + 1) * 512], prow[0:1, :],
                               C.v1(7), None, ALU.max),
                           waits=[(mm, None)], incs=[(ve, 1)])
                    p.emit("scalar",
                           lambda e, sc=sc: e.activation(
                               lnrow[0:1, sc * 512:(sc + 1) * 512],
                               lnrow[0:1, sc * 512:(sc + 1) * 512], AF.Ln),
                           waits=[(ve, None)], incs=[(se, 1)])
                    p.emit("tensor",
                           lambda e, sc=sc, pb=pb: e.matmul(pb[:, :], C.bcast1_f32[:, :],
                                                     lnrow[0:1, sc * 512:(sc + 1) * 512],
                                                     start=True, stop=True),
                           waits=[(se, None)], incs=[(mm, 1)])
                    p.emit("scalar",
                           lambda e, sc=sc, pb=pb: e.activation(rstd2[:, sc * 512:(sc + 1) * 512],
                                                         pb[:, :], AF.Exp, scale=-0.5),
                           waits=[(mm, None), (ve, None)], incs=[(se, 1)])
                    p.emit("vector",
                           lambda e, sc=sc, g=g: e.tensor_tensor(
                               vT_bf[g][:, sc * 512:(sc + 1) * 512],
                               fT[:, sc * 512:(sc + 1) * 512],
                               rstd2[:, sc * 512:(sc + 1) * 512], ALU.mult),
                           waits=[(se, None)], incs=[(ve, 1)])

    for h in range(NH):
        with nc.psum_tensor(f"{tag}hk{h}", [128, 512], F32) as pk, \
             nc.psum_tensor(f"{tag}hr{h}", [1, 512], F32) as prow, \
             nc.psum_tensor(f"{tag}hb{h}", [128, 512], F32) as pb, \
             nc.psum_tensor(f"{tag}hv{h}", [128, 128], F32) as pv:
            if layer == 0:
                p.dma("gpsimd", wql[:, :], T["l0_wq"].ap()[h].rearrange("p c j -> p (c j)"),
                      waits=[(mm, None)], incs=[(gld, 16)])
                p.dma("gpsimd", wkl[:, :], T["l0_wk"].ap()[h].rearrange("p c j -> p (c j)"),
                      incs=[(gld, 16)])
                p.dma("gpsimd", wvl[:, :], T["l0_wv"].ap()[h].rearrange("p c j -> p (c j)"),
                      incs=[(gld, 16)])
                proj_T(wkl, pk)
                if dbg and h == 0:
                    p.dma("sync", T["d_fT"].ap()[:, :], fT[:, :],
                          waits=[(se, None)], incs=[(sto, 16)])
                    p.dma("sync", T["d_wql"].ap()[:, :], wkl[:, :],
                          waits=[(ld, None)], incs=[(sto, 16)])
                    p.dma("sync", T["d_xnT0"].ap()[:, :], xnT_sb[:, 0:2048],
                          waits=[(ld, None)], incs=[(sto, 16)])
                rms_rows(prow, pb, C.v1(3))
                if dbg and h == 0:
                    p.dma("sync", T["d_rstd2"].ap()[:, :], rstd2[:, :],
                          waits=[(se, None)], incs=[(sto, 16)])
                p.emit("vector",
                       lambda e: e.scalar_tensor_tensor(khatT[0][:, :], fT[:, :],
                                                        s2c[:, :], rstd2[:, :],
                                                        ALU.mult, ALU.mult),
                       waits=[(se, None), (ve, None)], incs=[(ve, 1)])
                for t in range(S_TILES):
                    def mmv(e, t=t, pv=pv):
                        for c in range(E_BLKS):
                            inst = e.matmul(pv[:, :],
                                            xnT_sb[:, c * 2048 + t * 128: c * 2048 + (t + 1) * 128],
                                            wvl[:, c * 128:(c + 1) * 128],
                                            start=(c == 0), stop=(c == E_BLKS - 1))
                        return inst
                    p.emit("tensor", mmv, waits=[(ld, None), (gld, None), (ve, None)],
                           incs=[(mm, 1)])
                    p.emit("vector",
                           lambda e, t=t, pv=pv: e.tensor_copy(v_nat[0][:, t * 128:(t + 1) * 128],
                                                        pv[:, :]),
                           waits=[(mm, None)], incs=[(ve, 1)])
                proj_T(wql, pk)
                rms_rows(prow, pb, C.v1(3))
                p.emit("vector",
                       lambda e: e.tensor_tensor(qhatT[:, :], fT[:, :], rstd2[:, :], ALU.mult),
                       waits=[(se, None)], incs=[(ve, 1)])
                kh = khatT[0]
                if dbg and h == 0:
                    p.dma("sync", T["d_qhatT"].ap()[:, :], qhatT[:, :],
                          waits=[(ve, None)], incs=[(sto, 16)])
                    p.dma("sync", T["d_khatT"].ap()[:, :], khatT[0][:, :],
                          waits=[(ve, None)], incs=[(sto, 16)])
                    p.dma("sync", T["d_v"].ap()[:, :], v_nat[0][:, :],
                          waits=[(ve, None)], incs=[(sto, 16)])
                    for _eng in ("tensor", "vector", "scalar"):
                        p.emit(_eng, lambda e: None, waits=[(sto, None)])
            else:
                g = h // 4
                p.dma("gpsimd", wql[:, :], T["l1_wq"].ap()[h].rearrange("p c j -> p (c j)"),
                      waits=[(mm, None)], incs=[(gld, 16)])
                proj_T(wql, pk)
                rms_rows(prow, pb, C.v1(4))
                p.emit("vector",
                       lambda e: e.tensor_tensor(qhatT[:, :], fT[:, :], rstd2[:, :], ALU.mult),
                       waits=[(se, None)], incs=[(ve, 1)])
                kh = khatT[g]

        if layer == 0:
            with nc.psum_tensor(f"{tag}sn{h}", [128, 2048], F32) as psn, \
                 nc.psum_tensor(f"{tag}mr{h}", [1, 128], F32) as pmr:
                for t in range(S_TILES):
                    def mmsn(e, t=t, psn=psn):
                        for kc in range(4):
                            inst = e.matmul(psn[:, kc * 512:(kc + 1) * 512],
                                            qhatT[:, t * 128:(t + 1) * 128],
                                            kh[:, kc * 512:(kc + 1) * 512],
                                            start=True, stop=True)
                        return inst
                    p.emit("tensor", mmsn, waits=[(ve, None), (se, None)], incs=[(mm, 1)])
                    p.emit("vector",
                           lambda e, t=t, psn=psn: e.reduce_max(cols[:, t:t + 1], psn[:, :],
                                                       axis=AX.X, negate=True),
                           waits=[(mm, None)], incs=[(ve, 1)])
                    p.emit("scalar",
                           lambda e, t=t, psn=psn: e.activation(e_nat[:, :], psn[:, :], AF.Exp,
                                                       bias=cols[:, t:t + 1],
                                                       accum_out=cols[:, 16 + t:17 + t]),
                           waits=[(ve, None)], incs=[(se, 1)])
                    p.emit("tensor",
                           lambda e, t=t, pmr=pmr: e.transpose(pmr[:, :], cols[:, t:t + 1],
                                                      C.idn_f32[:, :]),
                           waits=[(ve, None), (se, None), (C.ld, None)], incs=[(mm, 1)])
                    p.emit("vector",
                           lambda e, t=t, pmr=pmr: e.tensor_copy(
                               mrow[0:1, t * 128:(t + 1) * 128], pmr[0:1, :]),
                           waits=[(mm, None)], incs=[(ve, 1)])
                p.emit("vector",
                       lambda e: e.reciprocal(cols[:, 16:32], cols[:, 16:32]),
                       waits=[(se, None)], incs=[(ve, 1)])
                for t in range(S_TILES):
                    p.emit("tensor",
                           lambda e, t=t, pmr=pmr: e.transpose(pmr[:, :], cols[:, 16 + t:17 + t],
                                                      C.idn_f32[:, :]),
                           waits=[(ve, None), (C.ld, None)], incs=[(mm, 1)])
                    p.emit("vector",
                           lambda e, t=t, pmr=pmr: e.tensor_copy(
                               zrow[0:1, t * 128:(t + 1) * 128], pmr[0:1, :]),
                           waits=[(mm, None)], incs=[(ve, 1)])
            if dbg and h == 0:
                p.emit("vector", lambda e: e.tensor_copy(cols[:, 32:33], s2c[:, :]),
                       incs=[(ve, 1)])
                p.emit("vector", lambda e: e.tensor_copy(cols[:, 33:34], qs_c[:, :]),
                       incs=[(ve, 1)])
                p.dma("sync", T["d_cols"].ap()[:, :], cols[:, :],
                      waits=[(ve, None)], incs=[(sto, 16)])
                p.dma("sync", T["d_mrow"].ap()[:, :], mrow[:, :],
                      waits=[(ve, None)], incs=[(sto, 16)])
                p.dma("sync", T["d_zrow"].ap()[:, :], zrow[:, :],
                      waits=[(ve, None)], incs=[(sto, 16)])
                for _eng in ("tensor", "vector", "scalar"):
                    p.emit(_eng, lambda e: None, waits=[(sto, None)])

        with nc.psum_tensor(f"{tag}sT{h}", [128, 512], F32) as pT, \
             nc.psum_tensor(f"{tag}av{h}", [128, 512], F32) as pav, \
             nc.psum_tensor(f"{tag}zz{h}", [1, 512], F32) as pz, \
             nc.psum_tensor(f"{tag}dt{h}", [1, 512], F32) as pdot, \
             nc.psum_tensor(f"{tag}zb{h}", [128, 512], F32) as pzb:
            for qc in range(4):
                for kt in range(S_TILES):
                    def mmsT(e, kt=kt, qc=qc, pT=pT, kh=kh):
                        if layer == 0:
                            e.matmul(pT[:, :], kh[:, kt * 128:(kt + 1) * 128],
                                     qhatT[:, qc * 512:(qc + 1) * 512],
                                     start=True, stop=False)
                            return e.matmul(pT[:, :], C.ones_row_bf[:, :],
                                            mrow[0:1, qc * 512:(qc + 1) * 512],
                                            start=False, stop=True)
                        return e.matmul(pT[:, :], kh[:, kt * 128:(kt + 1) * 128],
                                        qhatT[:, qc * 512:(qc + 1) * 512],
                                        start=True, stop=True)
                    p.emit("tensor", mmsT,
                           waits=[(se, None)] + ([(ve, None)] if kt == 0 else []),
                           incs=[(mm, 1)])
                    p.emit("scalar",
                           lambda e, kt=kt, pT=pT: e.activation(eT[:, kt * 512:(kt + 1) * 512],
                                                         pT[:, :], AF.Exp),
                           waits=[(mm, None)], incs=[(se, 1)])

                def mmav(e, qc=qc, h=h, pav=pav):
                    for kt in range(S_TILES):
                        lhs = (v_nat[0][:, kt * 128:(kt + 1) * 128] if layer == 0 else
                               v_nat[:, kt * 512 + (h // 4) * 128:
                                     kt * 512 + (h // 4) * 128 + 128])
                        inst = e.matmul(pav[:, :], lhs, eT[:, kt * 512:(kt + 1) * 512],
                                        start=(kt == 0), stop=(kt == S_TILES - 1))
                    return inst
                p.emit("tensor", mmav, waits=[(se, None), (ve, None)], incs=[(mm, 1)])

                if layer == 0:
                    p.emit("tensor",
                           lambda e, qc=qc, pzb=pzb: e.matmul(pzb[:, :], C.bcast1_f32[:, :],
                                                     zrow[0:1, qc * 512:(qc + 1) * 512],
                                                     start=True, stop=True),
                           waits=[(ve, None)], incs=[(mm, 1)])
                    p.emit("vector",
                           lambda e, pzb=pzb: e.tensor_copy(zb[:, :], pzb[:, :]),
                           waits=[(mm, None)], incs=[(ve, 1)])
                    p.emit("vector",
                           lambda e, pav=pav: e.tensor_tensor(ao_p[:, :], pav[:, :], zb[:, :],
                                                     ALU.mult),
                           waits=[(sto, None), (ve, None)], incs=[(ve, 1)])
                else:
                    g = h // 4

                    def mmz(e, pz=pz):
                        for kt in range(S_TILES):
                            inst = e.matmul(pz[:, :], C.red1_bf[:, :],
                                            eT[:, kt * 512:(kt + 1) * 512],
                                            start=(kt == 0), stop=(kt == S_TILES - 1))
                        return inst
                    p.emit("tensor", mmz, waits=[(ve, None)], incs=[(mm, 1)])
                    p.emit("vector",
                           lambda e, g=g, qc=qc, pav=pav: e.tensor_tensor(
                               uv_f[:, :], pav[:, :],
                               vT_bf[g][:, qc * 512:(qc + 1) * 512], ALU.mult),
                           waits=[(mm, None)], incs=[(ve, 1)])
                    p.emit("scalar",
                           lambda e, pav=pav: e.activation(ut_f[:, :], pav[:, :], AF.Identity),
                           waits=[(mm, None), (ve, None)], incs=[(se, 1)])
                    p.emit("tensor",
                           lambda e, pdot=pdot: e.matmul(pdot[:, :], C.red1_f32[:, :], uv_f[:, :],
                                              start=True, stop=True),
                           waits=[(ve, None)], incs=[(mm, 1)])
                    p.emit("vector",
                           lambda e, qc=qc, pz=pz: e.reciprocal(
                               zrow[0:1, qc * 512:(qc + 1) * 512], pz[0:1, :]),
                           waits=[(mm, None)], incs=[(ve, 1)])
                    p.emit("vector",
                           lambda e, qc=qc, pdot=pdot: e.tensor_tensor(
                               g2row[0:1, qc * 512:(qc + 1) * 512], pdot[0:1, :],
                               zrow[0:1, qc * 512:(qc + 1) * 512], ALU.mult),
                           waits=[(mm, None), (ve, None)], incs=[(ve, 1)])
                    p.emit("tensor",
                           lambda e, qc=qc, pzb=pzb: e.matmul(pzb[:, :], C.bcast1_f32[:, :],
                                                     zrow[0:1, qc * 512:(qc + 1) * 512],
                                                     start=True, stop=True),
                           waits=[(ve, None)], incs=[(mm, 1)])
                    p.emit("vector", lambda e, pzb=pzb: e.tensor_copy(zb[:, :], pzb[:, :]),
                           waits=[(mm, None)], incs=[(ve, 1)])
                    p.emit("tensor",
                           lambda e, qc=qc, pzb=pzb: e.matmul(pzb[:, :], C.bcast1_f32[:, :],
                                                     g2row[0:1, qc * 512:(qc + 1) * 512],
                                                     start=True, stop=True),
                           waits=[(ve, None)], incs=[(mm, 1)])
                    p.emit("vector", lambda e, pzb=pzb: e.tensor_copy(g2b[:, :], pzb[:, :]),
                           waits=[(mm, None)], incs=[(ve, 1)])
                    p.emit("vector",
                           lambda e: e.tensor_tensor(ut_f[:, :], ut_f[:, :], zb[:, :],
                                                     ALU.mult),
                           waits=[(se, None), (ve, None)], incs=[(ve, 1)])
                    p.emit("vector",
                           lambda e, g=g, qc=qc: e.tensor_tensor(
                               uv_f[:, :], g2b[:, :],
                               vT_bf[g][:, qc * 512:(qc + 1) * 512], ALU.mult),
                           waits=[(ve, None)], incs=[(ve, 1)])
                    p.emit("vector",
                           lambda e: e.tensor_tensor(ao_p[:, :], ut_f[:, :], uv_f[:, :],
                                                     ALU.subtract),
                           waits=[(sto, None), (ve, None)], incs=[(ve, 1)])
                if dbg and h == 0 and qc == 0:
                    p.dma("sync", T["d_eT"].ap()[:, :], eT[:, :],
                          waits=[(se, None), (mm, None)], incs=[(sto, 16)])
                    for _eng in ("tensor", "vector", "scalar"):
                        p.emit(_eng, lambda e: None, waits=[(sto, None)])
                p.dma("sync", aoT_d[h, :, qc * 512:(qc + 1) * 512], ao_p[:, :],
                      waits=[(ve, None)], incs=[(sto, 16)])

    wo_d = T["l0_wo"].ap() if layer == 0 else T["l1_wo"].ap()
    for tg in range(4):
        for eh in range(2):
            with ExitStack() as ps:
                pw = [ps.enter_context(nc.psum_tensor(f"{tag}pw{tg}{eh}_{i}", [128, 512], F32))
                      for i in range(8)]
                for c in range(16):
                    p.dma("gpsimd", wop[:, :], wo_d[c, :, eh * 1024:(eh + 1) * 1024],
                          waits=[(mm, None)], incs=[(glw, 16)])
                    p.dma("sync", aop_sb[:, :], aoT_d[c, :, tg * 512:(tg + 1) * 512],
                          waits=[(mm, None)], incs=[(aold, 16)])

                    def mmw(e, c=c, pw=pw):
                        for i in range(4):
                            for ec in range(2):
                                inst = e.matmul(pw[i * 2 + ec][:, :],
                                                aop_sb[:, i * 128:(i + 1) * 128],
                                                wop[:, ec * 512:(ec + 1) * 512],
                                                start=(c == 0), stop=(c == 15))
                        return inst
                    p.emit("tensor", mmw,
                           waits=[(aold, None), (glw, None)] + ([(ve, None)] if c == 0 else []),
                           incs=[(mm, 1)])
                for i in range(4):
                    t = tg * 4 + i
                    p.dma("sync", xbh, xbase_d[t, :, eh * 1024:(eh + 1) * 1024],
                          waits=[(ve, None)], incs=[(xbld, 16)])
                    for ec in range(2):
                        p.emit("vector",
                               lambda e, i=i, ec=ec, pw=pw: e.tensor_tensor(
                                   fsq[:, 1024 + ec * 512:1024 + (ec + 1) * 512],
                                   fsq[:, ec * 512:(ec + 1) * 512],
                                   pw[i * 2 + ec][:, :], ALU.add),
                               waits=[(mm, None), (xbld, None), (sto, None)], incs=[(ve, 1)])
                    p.dma("sync", xout_d[t, :, eh * 1024:(eh + 1) * 1024], xoh,
                          waits=[(ve, None)], incs=[(sto, 16)])
    return sto


def _mk_dram(nc, specs):
    T = {}
    for name, shape, dt, kind in specs:
        T[name] = nc.dram_tensor(name, list(shape), dt, kind=kind)
    return T


_FFN_L0_W = [
    ("l0_wg", (64, 128, 16, 128), I8, "ExternalInput"),
    ("l0_wu", (64, 128, 16, 128), I8, "ExternalInput"),
    ("l0_wd", (64, 128, 2048), I8, "ExternalInput"),
    ("l0_sg", (128, 64), F32, "ExternalInput"),
    ("l0_su", (128, 64), F32, "ExternalInput"),
    ("l0_sdown", (1, 2048), F32, "ExternalInput"),
]
_FFN_L1_W = [
    ("l1_w1", (64, 128, 16, 128), F32, "ExternalInput"),
    ("l1_w3", (64, 128, 16, 128), F32, "ExternalInput"),
    ("l1_w2t", (64, 128, 2048), F32, "ExternalInput"),
    ("h128", (128, 128), BF16, "ExternalInput"),
    ("h2x64", (128, 128), BF16, "ExternalInput"),
    ("l1_w2a", (8192, 2048), BF16, "Internal"),
    ("l1_w2pp", (64, 128, 2048), BF16, "Internal"),
]


_ATTN_L0_W = [
    ("l0_wq", (16, 128, 16, 128), F32, "ExternalInput"),
    ("l0_wk", (16, 128, 16, 128), F32, "ExternalInput"),
    ("l0_wv", (16, 128, 16, 128), F32, "ExternalInput"),
    ("l0_wo", (16, 128, 2048), F32, "ExternalInput"),
    ("l0_qscale", (128, 1), F32, "ExternalInput"),
    ("l0_kscale", (128, 1), F32, "ExternalInput"),
]
_ATTN_L1_W = [
    ("l1_wq", (16, 128, 16, 128), F32, "ExternalInput"),
    ("l1_wk", (4, 128, 16, 128), F32, "ExternalInput"),
    ("l1_wv", (4, 128, 16, 128), F32, "ExternalInput"),
    ("l1_wv_n", (16, 128, 512), F32, "ExternalInput"),
    ("l1_wo", (16, 128, 2048), F32, "ExternalInput"),
]


def build_attn_nc(layer, dbg=False):
    nc = bass.Bass("TRN2", target_bir_lowering=False)
    specs = [
        ("xnT", (16, 128, 2048), BF16, "ExternalInput"),
        ("x_base", (16, 128, 2048), F32, "ExternalInput"),
        ("idn_bf", (128, 128), BF16, "ExternalInput"),
        ("idn_f32", (128, 128), F32, "ExternalInput"),
        ("x_out", (16, 128, 2048), F32, "ExternalOutput"),
        ("aoT", (16, 128, 2048), BF16, "ExternalOutput" if dbg else "Internal"),
    ]
    if dbg:
        specs += [
            ("d_qhatT", (128, 2048), BF16, "ExternalOutput"),
            ("d_fT", (128, 2048), F32, "ExternalOutput"),
            ("d_wql", (128, 2048), BF16, "ExternalOutput"),
            ("d_xnT0", (128, 2048), BF16, "ExternalOutput"),
            ("d_rstd2", (128, 2048), F32, "ExternalOutput"),
            ("d_khatT", (128, 2048), BF16, "ExternalOutput"),
            ("d_v", (128, 2048), BF16, "ExternalOutput"),
            ("d_cols", (128, 34), F32, "ExternalOutput"),
            ("d_mrow", (1, 2048), BF16, "ExternalOutput"),
            ("d_zrow", (1, 2048), F32, "ExternalOutput"),
            ("d_eT", (128, 8192), BF16, "ExternalOutput"),
        ]
    specs += _ATTN_L0_W if layer == 0 else _ATTN_L1_W
    with ExitStack() as ctx:
        T = _mk_dram(nc, specs)
        p = Prog(nc, ctx)
        C = Consts(nc, ctx, p, T)
        sto = emit_attn(p, C, layer, T, tag=f"a{layer}_", dbg=dbg)
        p.emit("sync", lambda e: None, waits=[(sto, None)])
        p.finalize()
    return nc


def build_ffn_nc(layer):
    nc = bass.Bass("TRN2", target_bir_lowering=False)
    specs = [
        ("x_in", (16, 128, 2048), F32, "ExternalInput"),
        ("idn_bf", (128, 128), BF16, "ExternalInput"),
        ("idn_f32", (128, 128), F32, "ExternalInput"),
        ("xnT", (16, 128, 2048), BF16, "ExternalOutput"),
        ("x_acc", (16, 128, 2048), F32, "ExternalOutput"),
    ]
    specs += _FFN_L0_W if layer == 0 else _FFN_L1_W
    with ExitStack() as ctx:
        T = _mk_dram(nc, specs)
        p = Prog(nc, ctx)
        C = Consts(nc, ctx, p, T)
        if layer == 1:
            w2done = emit_w2pp(p, C, T)
            for eng in ENGINES:
                p.emit(eng, lambda e: None, waits=[(w2done, None)])
        sto = emit_ffn(p, C, layer, T, tag=f"f{layer}_")
        p.emit("sync", lambda e: None, waits=[(sto, None)])
        p.finalize()
    return nc


def build_fused_nc():
    nc = bass.Bass("TRN2", target_bir_lowering=False)
    specs = [
        ("x_in", (16, 128, 2048), F32, "ExternalInput"),
        ("idn_bf", (128, 128), BF16, "ExternalInput"),
        ("idn_f32", (128, 128), F32, "ExternalInput"),
        ("h128", (128, 128), BF16, "ExternalInput"),
        ("h2x64", (128, 128), BF16, "ExternalInput"),
        ("out", (16, 128, 2048), F32, "ExternalOutput"),
        ("xnT_s", (16, 128, 2048), BF16, "Internal"),
        ("xacc_s", (16, 128, 2048), F32, "Internal"),
        ("x1_s", (16, 128, 2048), F32, "Internal"),
        ("aoT_s", (16, 128, 2048), BF16, "Internal"),
        ("l1_w2a", (8192, 2048), BF16, "Internal"),
        ("l1_w2pp", (64, 128, 2048), BF16, "Internal"),
    ]
    specs += _FFN_L0_W + _ATTN_L0_W
    specs += [t for t in _FFN_L1_W if t[0] in ("l1_w1", "l1_w3", "l1_w2t")]
    specs += _ATTN_L1_W
    with ExitStack() as ctx:
        T = _mk_dram(nc, specs)
        p = Prog(nc, ctx)
        C = Consts(nc, ctx, p, T)

        def phase(emitter):
            with ExitStack() as pc:
                p.alloc_ctx = pc
                emitter()
            p.alloc_ctx = ctx
            p.barrier()

        Tf0 = dict(T, x_in=T["x_in"], xnT=T["xnT_s"], x_acc=T["xacc_s"])
        Ta0 = dict(T, xnT=T["xnT_s"], x_base=T["xacc_s"], x_out=T["x1_s"], aoT=T["aoT_s"])
        Tf1 = dict(T, x_in=T["x1_s"], xnT=T["xnT_s"], x_acc=T["xacc_s"])
        Ta1 = dict(T, xnT=T["xnT_s"], x_base=T["xacc_s"], x_out=T["out"], aoT=T["aoT_s"])

        phase(lambda: emit_w2pp(p, C, T))
        phase(lambda: emit_ffn(p, C, 0, Tf0, tag="f0_"))
        phase(lambda: emit_attn(p, C, 0, Ta0, tag="a0_"))
        phase(lambda: emit_ffn(p, C, 1, Tf1, tag="f1_"))
        phase(lambda: emit_attn(p, C, 1, Ta1, tag="a1_"))
        p.finalize()
    return nc


_BUILD_CACHE = {}


def _get_nc(key, builder):
    if key not in _BUILD_CACHE:
        _BUILD_CACHE[key] = builder()
    return _BUILD_CACHE[key]


def kernel_fused(**inputs):
    w = prep_weights(inputs)
    c = host_consts()
    x = inputs["L0_attn_x"]
    nc = _get_nc("fused", build_fused_nc)
    base = {"idn_bf": c["idn_bf"], "idn_f32": c["idn_f32"],
            "h128": c["h128"], "h2x64": c["h2x64"]}
    for k in ("l0_wg", "l0_wu", "l0_wd", "l0_sg", "l0_su", "l0_sdown",
              "l0_wq", "l0_wk", "l0_wv", "l0_wo", "l0_qscale", "l0_kscale",
              "l1_w1", "l1_w3", "l1_w2t",
              "l1_wq", "l1_wk", "l1_wv", "l1_wv_n", "l1_wo"):
        base[k] = w[k]
    maps = [dict(base, x_in=np.ascontiguousarray(x[i].reshape(16, 128, 2048)))
            for i in range(NCORES)]
    res = run_bass_kernel_spmd(nc, maps, core_ids=list(range(NCORES)))
    out = np.empty((NCORES, 2048, 2048), dtype=np.float32)
    for i in range(NCORES):
        out[i] = res.results[i]["out"].reshape(2048, 2048)
    return out


def kernel(**inputs):
    return kernel_fused(**inputs)


def kernel_unfused(**inputs):
    w = prep_weights(inputs)
    c = host_consts()
    x = inputs["L0_attn_x"]
    consts = {"idn_bf": c["idn_bf"], "idn_f32": c["idn_f32"]}

    def shard(arr):
        return [np.ascontiguousarray(arr[cid].reshape(16, 128, 2048))
                for cid in range(NCORES)]

    core_ids = list(range(NCORES))

    nc = _get_nc("ffn0", lambda: build_ffn_nc(0))
    base = dict(consts)
    for k in ("l0_wg", "l0_wu", "l0_wd", "l0_sg", "l0_su", "l0_sdown"):
        base[k] = w[k]
    xs = shard(x)
    maps = [dict(base, x_in=xs[i]) for i in range(NCORES)]
    res = run_bass_kernel_spmd(nc, maps, core_ids=core_ids)
    xnT = [r["xnT"] for r in res.results]
    xacc = [r["x_acc"] for r in res.results]

    nc = _get_nc("attn0", lambda: build_attn_nc(0))
    base = dict(consts)
    for k in ("l0_wq", "l0_wk", "l0_wv", "l0_wo", "l0_qscale", "l0_kscale"):
        base[k] = w[k]
    maps = [dict(base, xnT=xnT[i], x_base=xacc[i]) for i in range(NCORES)]
    res = run_bass_kernel_spmd(nc, maps, core_ids=core_ids)
    x1 = [r["x_out"] for r in res.results]

    nc = _get_nc("ffn1", lambda: build_ffn_nc(1))
    base = dict(consts)
    for k in ("l1_w1", "l1_w3", "l1_w2t"):
        base[k] = w[k]
    base["h128"] = c["h128"]
    base["h2x64"] = c["h2x64"]
    maps = [dict(base, x_in=x1[i]) for i in range(NCORES)]
    res = run_bass_kernel_spmd(nc, maps, core_ids=core_ids)
    xnT = [r["xnT"] for r in res.results]
    xacc = [r["x_acc"] for r in res.results]

    nc = _get_nc("attn1", lambda: build_attn_nc(1))
    base = dict(consts)
    for k in ("l1_wq", "l1_wk", "l1_wv", "l1_wv_n", "l1_wo"):
        base[k] = w[k]
    maps = [dict(base, xnT=xnT[i], x_base=xacc[i]) for i in range(NCORES)]
    res = run_bass_kernel_spmd(nc, maps, core_ids=core_ids)

    out = np.empty((NCORES, 2048, 2048), dtype=np.float32)
    for i in range(NCORES):
        out[i] = res.results[i]["x_out"].reshape(2048, 2048)
    return out


# version 08
# speedup vs baseline: 1.5176x; 1.0638x over previous
import numpy as np
import ml_dtypes
from contextlib import ExitStack

import concourse.bass as bass
import concourse.mybir as mybir
from concourse.bass_utils import run_bass_kernel_spmd

F32 = mybir.dt.float32
BF16 = mybir.dt.bfloat16
I8 = mybir.dt.int8
AF = mybir.ActivationFunctionType
ALU = mybir.AluOpType
AX = mybir.AxisListType

NCORES = 8
S_TILES = 16
E_BLKS = 16
O_TILES = 64
CHUNK = 512
N_CHUNKS = 4

RMS_EPS_L0 = 1e-6
RMS_EPS_L1 = float(np.finfo(np.float32).eps)
LN_EPS = 1e-5
FWHT_SCALE = float(1.0 / np.sqrt(8192.0))
SM_SCALE_L1 = float(128.0 ** -0.5)

ENGINES = ("sync", "gpsimd", "scalar", "vector", "tensor")


class Sem:
    def __init__(self, nc, ctx, name):
        self.h = ctx.enter_context(nc.semaphore(name))
        self.n = 0


class Prog:

    def __init__(self, nc, ctx):
        self.nc = nc
        self.ctx = ctx
        self.alloc_ctx = ctx
        self.q = {e: [] for e in ENGINES}
        self._sems = {}

    def barrier(self):
        waits = [(s, None) for s in self._sems.values() if s.n > 0]
        for eng in ENGINES:
            self.emit(eng, lambda e: None, waits=waits)

    def sem(self, name):
        if name not in self._sems:
            self._sems[name] = Sem(self.nc, self.ctx, name)
        return self._sems[name]

    def emit(self, eng, fn, waits=(), incs=()):
        wlist = [(s.h, (s.n if v is None else v)) for (s, v) in waits]
        ilist = []
        for s, k in incs:
            s.n += k
            ilist.append((s.h, k))

        def run(e):
            for h, v in wlist:
                if v > 0:
                    e.wait_ge(h, v)
            inst = fn(e)
            for h, k in ilist:
                assert inst is not None
                inst = inst.then_inc(h, k)
        self.q[eng].append(run)

    def dma(self, eng, out_ap, in_ap, waits=(), incs=()):
        self.emit(eng, lambda e: e.dma_start(out_ap, in_ap), waits, incs)

    def finalize(self):
        with self.nc.Block() as b:
            for name in ENGINES:
                fns = self.q[name]
                if not fns:
                    continue
                getattr(b, name)(lambda e, fns=fns: [f(e) for f in fns])


def _c(a):
    return np.ascontiguousarray(a)


def prep_weights(ins):
    w = {}
    for nm in ("wq", "wk", "wv"):
        a = ins[f"L0_attn_{nm}"]
        w[f"l0_{nm}"] = _c(a.reshape(16, 128, 16, 128).transpose(2, 1, 0, 3))
    w["l0_wo"] = _c(ins["L0_attn_wo"].reshape(16, 128, 2048))
    w["l0_qscale"] = _c(ins["L0_attn_q_scale"].reshape(128, 1))
    w["l0_kscale"] = _c(ins["L0_attn_k_scale"].reshape(128, 1))
    for nm, key in (("wg", "L0_ffn_w_gate"), ("wu", "L0_ffn_w_up")):
        a = ins[key]
        w[f"l0_{nm}"] = _c(a.reshape(16, 128, 64, 128).transpose(2, 1, 0, 3))
    w["l0_wd"] = _c(ins["L0_ffn_w_down"].reshape(64, 128, 2048))
    w["l0_sg"] = _c(ins["L0_ffn_s_gate"].reshape(64, 128).T)
    w["l0_su"] = _c(ins["L0_ffn_s_up"].reshape(64, 128).T)
    w["l0_sdown"] = _c(ins["L0_ffn_s_down"].reshape(1, 2048))
    a = ins["L1_attn_Wq"]
    w["l1_wq"] = _c(a.reshape(16, 128, 16, 128).transpose(0, 3, 2, 1))
    for nm in ("Wk", "Wv"):
        a = ins[f"L1_attn_{nm}"]
        w[f"l1_{nm.lower()}"] = _c(a.reshape(4, 128, 16, 128).transpose(0, 3, 2, 1))
    w["l1_wv_n"] = _c(ins["L1_attn_Wv"].T.reshape(16, 128, 512))
    w["l1_wo"] = _c(ins["L1_attn_Wo"].T.reshape(16, 128, 2048))
    for nm, key in (("w1", "L1_ffn_W1"), ("w3", "L1_ffn_W3")):
        a = ins[key]
        w[f"l1_{nm}"] = _c(a.reshape(64, 128, 16, 128).transpose(0, 3, 2, 1))
    w["l1_w2t"] = _c(ins["L1_ffn_W2"].T.reshape(64, 128, 2048))
    return w


def host_consts():
    c = {}
    c["idn_bf"] = np.eye(128, dtype=ml_dtypes.bfloat16)
    c["idn_f32"] = np.eye(128, dtype=np.float32)

    def had(n):
        h = np.array([[1.0]])
        while h.shape[0] < n:
            h = np.block([[h, h], [h, -h]])
        return h
    c["h128"] = had(128).astype(ml_dtypes.bfloat16)
    h64 = had(64)
    z = np.zeros((64, 64))
    c["h2x64"] = np.block([[h64, z], [z, h64]]).astype(ml_dtypes.bfloat16)
    return c


class Consts:
    def __init__(self, nc, ctx, p, T):
        sb = lambda name, shape, dt: ctx.enter_context(nc.sbuf_tensor(name, shape, dt))
        self.idn_bf = sb("c_idn_bf", [128, 128], BF16)
        self.idn_f32 = sb("c_idn_f32", [128, 128], F32)
        self.bcast1_f32 = sb("c_bcast1", [1, 128], F32)
        self.ones_row_bf = sb("c_ones_row_bf", [1, 128], BF16)
        self.red1_f32 = sb("c_red1_f", [128, 1], F32)
        self.red1_bf = sb("c_red1_b", [128, 1], BF16)
        self.vals = sb("c_vals", [128, 12], F32)
        ld = p.sem("c_ld")
        p.dma("sync", self.idn_bf[:, :], T["idn_bf"].ap()[:, :], incs=[(ld, 16)])
        p.dma("sync", self.idn_f32[:, :], T["idn_f32"].ap()[:, :], incs=[(ld, 16)])
        g = p.sem("c_set")
        for col, v in ((0, -1.0 / 2048.0), (1, 1.0 / 2048.0), (2, LN_EPS),
                       (3, RMS_EPS_L0), (4, RMS_EPS_L1), (5, FWHT_SCALE),
                       (6, 1.0 / 128.0), (7, 1e-24), (8, SM_SCALE_L1)):
            p.emit("vector", lambda e, col=col, v=v: e.memset(self.vals[:, col:col + 1], v),
                   incs=[(g, 1)])
        for t in (self.bcast1_f32, self.ones_row_bf, self.red1_f32, self.red1_bf):
            p.emit("vector", lambda e, t=t: e.memset(t[:, :], 1.0), incs=[(g, 1)])
        self.ld = ld
        self.ready = g

    def v(self, col):
        return self.vals[:, col:col + 1]

    def v1(self, col):
        return self.vals[0:1, col:col + 1]



def emit_ffn(p, C, layer, T, tag):
    nc = p.nc
    ctx = p.alloc_ctx
    sb = lambda name, shape, dt: ctx.enter_context(nc.sbuf_tensor(tag + name, shape, dt))

    x_in = T["x_in"].ap()
    xnT_d = T["xnT"].ap()
    xacc_d = T["x_acc"].ap()
    if layer == 0:
        wg_d, wu_d, wd_d = T["l0_wg"].ap(), T["l0_wu"].ap(), T["l0_wd"].ap()
    else:
        wg_d, wu_d, wd_d = T["l1_w1"].ap(), T["l1_w3"].ap(), T["l1_w2pp"].ap()

    xbuf = sb("xbuf", [128, 2048], F32)
    xsq = sb("xsq", [128, 2048], F32)
    st = sb("stats", [128, 8], F32)
    xn_bf = sb("xn_bf", [128, 2048], BF16)
    xnT_sb = sb("xnT_sb", [128, E_BLKS * CHUNK], BF16)
    hidT = sb("hidT", [128, O_TILES * CHUNK], BF16)
    wgl = [sb(f"wgl{b}", [128, 2048], BF16) for b in range(2)]
    wul = [sb(f"wul{b}", [128, 2048], BF16) for b in range(2)]
    wdp = [sb(f"wdp{b}", [128, 4096], BF16) for b in range(2)]
    silu_s = sb("silu_s", [128, CHUNK], F32)
    up_s = sb("up_s", [128, CHUNK], F32)
    xacc = [sb(f"xacc{i}", [128, 2048], F32) for i in range(4)]
    if layer == 0:
        sg_sb = sb("sg", [128, 64], F32)
        su_sb = sb("su", [128, 64], F32)
        sdb = sb("sdb", [128, 2048], F32)

    xld = p.sem(tag + "xld")
    ald = p.sem(tag + "ald")
    wld = p.sem(tag + "wld")
    ggu = [p.sem(tag + f"ggu{b}") for b in range(2)]
    gwd = [p.sem(tag + f"gwd{b}") for b in range(2)]
    lnv = p.sem(tag + "lnv")
    trp = p.sem(tag + "trp")
    xnv = p.sem(tag + "xnv")
    mmg = p.sem(tag + "mmg")
    ev = p.sem(tag + "ev")
    mmd = p.sem(tag + "mmd")
    dev = p.sem(tag + "dev")
    sto = p.sem(tag + "sto")

    if layer == 0:
        p.dma("sync", sg_sb[:, :], T["l0_sg"].ap()[:, :], incs=[(wld, 16)])
        p.dma("sync", su_sb[:, :], T["l0_su"].ap()[:, :], incs=[(wld, 16)])
        sdrow = sb("sdrow", [1, 2048], F32)
        p.dma("sync", sdrow[:, :], T["l0_sdown"].ap()[:, :], incs=[(wld, 16)])
        with nc.psum_tensor(tag + "pbc", [128, 512], F32) as pbc:
            for i in range(4):
                p.emit("tensor",
                       lambda e, i=i, pbc=pbc: e.matmul(pbc[:, :], C.bcast1_f32[:, :],
                                               sdrow[:, i * 512:(i + 1) * 512],
                                               start=True, stop=True),
                       waits=[(wld, None), (C.ready, None), (ev, None)], incs=[(mmg, 1)])
                p.emit("vector",
                       lambda e, i=i, pbc=pbc: e.tensor_copy(sdb[:, i * 512:(i + 1) * 512], pbc[:, :]),
                       waits=[(mmg, None)], incs=[(ev, 1)])

    gu_ev_done = {}
    gu_mm_done = {}
    dq_mm_done = {}
    for q in range(N_CHUNKS):
        tiles = [q * 4 + i for i in range(4)]
        with nc.psum_tensor(f"{tag}ptr{q}", [128, 512], BF16) as ptr, \
             nc.psum_tensor(f"{tag}pg{q}a", [128, 512], F32) as pg0, \
             nc.psum_tensor(f"{tag}pg{q}b", [128, 512], F32) as pg1, \
             nc.psum_tensor(f"{tag}pu{q}a", [128, 512], F32) as pu0, \
             nc.psum_tensor(f"{tag}pu{q}b", [128, 512], F32) as pu1:
            pg = [pg0, pg1]
            pu = [pu0, pu1]
            for i, t in enumerate(tiles):
                p.dma("sync", xbuf[:, :], x_in[t], waits=[(lnv, None)], incs=[(xld, 16)])
                p.emit("vector", lambda e: e.reduce_sum(st[:, 0:1], xbuf[:, :], axis=AX.X),
                       waits=[(xld, None)], incs=[(lnv, 1)])
                p.emit("scalar", lambda e: e.activation(xsq[:, :], xbuf[:, :], AF.Square),
                       waits=[(xld, None), (lnv, None)], incs=[(lnv, 1)])
                p.emit("vector", lambda e: e.reduce_sum(st[:, 1:2], xsq[:, :], axis=AX.X),
                       waits=[(lnv, None)], incs=[(lnv, 1)])
                p.emit("vector", lambda e: e.tensor_scalar(st[:, 2:3], st[:, 0:1],
                                                           C.v(0), None, ALU.mult),
                       waits=[(lnv, None)], incs=[(lnv, 1)])
                p.emit("vector", lambda e: e.tensor_tensor(st[:, 3:4], st[:, 2:3], st[:, 2:3],
                                                           ALU.mult),
                       waits=[(lnv, None)], incs=[(lnv, 1)])
                p.emit("vector", lambda e: e.scalar_tensor_tensor(st[:, 4:5], st[:, 1:2],
                                                                  C.v(1), st[:, 3:4],
                                                                  ALU.mult, ALU.subtract),
                       waits=[(lnv, None)], incs=[(lnv, 1)])
                p.emit("scalar", lambda e: e.activation(st[:, 5:6], st[:, 4:5], AF.Ln,
                                                        bias=C.v(2)),
                       waits=[(lnv, None)], incs=[(lnv, 1)])
                p.emit("scalar", lambda e: e.activation(st[:, 6:7], st[:, 5:6], AF.Exp,
                                                        scale=-0.5),
                       waits=[(lnv, None)], incs=[(lnv, 1)])
                p.emit("vector", lambda e: e.tensor_scalar(xn_bf[:, :], xbuf[:, :],
                                                           st[:, 2:3], st[:, 6:7],
                                                           ALU.add, ALU.mult),
                       waits=[(lnv, None), (trp, None)], incs=[(lnv, 1)])
                for gblk in range(4):
                    for j in range(4):
                        eb = gblk * 4 + j
                        w = [(lnv, None), (C.ld, None), (xnv, None), (dev, None),
                             (ev, None)] if j == 0 else []
                        p.emit("tensor",
                               lambda e, eb=eb, j=j, ptr=ptr: e.transpose(
                                   ptr[:, j * 128:(j + 1) * 128],
                                   xn_bf[:, eb * 128:(eb + 1) * 128],
                                   C.idn_bf[:, :]),
                               waits=w, incs=([(trp, 1)] if j == 3 else []))
                    dst = xnT_sb[:, :].rearrange("p (eb s) -> p eb s", eb=E_BLKS)
                    p.emit("vector",
                           lambda e, gblk=gblk, i=i, dst=dst, ptr=ptr: e.tensor_copy(
                               dst[:, gblk * 4:gblk * 4 + 4, i * 128:(i + 1) * 128],
                               ptr[:, :].rearrange("p (j s) -> p j s", j=4)),
                           waits=[(trp, None), (sto, None)], incs=[(xnv, 1)])
            for eb in range(E_BLKS):
                p.dma("sync", xnT_d[eb, :, q * CHUNK:(q + 1) * CHUNK],
                      xnT_sb[:, eb * CHUNK:(eb + 1) * CHUNK],
                      waits=[(xnv, None)], incs=[(sto, 16)])

            for ot in range(O_TILES):
                b = ot & 1
                gq = q * O_TILES + ot
                wg_w = [(mmg, gu_mm_done.get(gq - 2, 0))]
                p.dma("gpsimd", wgl[b][:, :], wg_d[ot].rearrange("p c j -> p (c j)"),
                      waits=wg_w, incs=[(ggu[b], 16)])
                p.dma("gpsimd", wul[b][:, :], wu_d[ot].rearrange("p c j -> p (c j)"),
                      incs=[(ggu[b], 16)])

                def mm_gate(e, pg=pg, b=b):
                    for c in range(E_BLKS):
                        inst = e.matmul(pg[b][:, :], wgl[b][:, c * 128:(c + 1) * 128],
                                        xnT_sb[:, c * CHUNK:(c + 1) * CHUNK],
                                        start=(c == 0), stop=(c == E_BLKS - 1))
                    return inst
                g_w = [(ggu[b], None), (xnv, None), (dev, None)]
                if gq >= 2:
                    g_w.append((ev, gu_ev_done[gq - 2]))
                else:
                    g_w.append((ev, None))
                p.emit("tensor", mm_gate, waits=g_w, incs=[(mmg, 1)])

                def mm_up(e, pu=pu, b=b):
                    for c in range(E_BLKS):
                        inst = e.matmul(pu[b][:, :], wul[b][:, c * 128:(c + 1) * 128],
                                        xnT_sb[:, c * CHUNK:(c + 1) * CHUNK],
                                        start=(c == 0), stop=(c == E_BLKS - 1))
                    return inst
                p.emit("tensor", mm_up, incs=[(mmg, 1)])

                if layer == 0:
                    p.emit("scalar",
                           lambda e, ot=ot, pg=pg, b=b: e.activation(
                               silu_s[:, :], pg[b][:, :], AF.Silu,
                               scale=sg_sb[:, ot:ot + 1]),
                           waits=[(mmg, None), (ev, None), (wld, None)], incs=[(ev, 1)])
                    p.emit("vector",
                           lambda e, ot=ot, pu=pu, b=b: e.tensor_scalar(
                               up_s[:, :], pu[b][:, :],
                               su_sb[:, ot:ot + 1], None, ALU.mult),
                           waits=[(mmg, None), (wld, None)], incs=[(ev, 1)])
                    p.emit("vector",
                           lambda e, ot=ot: e.tensor_tensor(
                               hidT[:, ot * CHUNK:(ot + 1) * CHUNK],
                               silu_s[:, :], up_s[:, :], ALU.mult),
                           waits=[(ev, None)], incs=[(ev, 1)])
                else:
                    p.emit("scalar",
                           lambda e, pg=pg, b=b: e.activation(silu_s[:, :], pg[b][:, :],
                                                              AF.Silu),
                           waits=[(mmg, None), (ev, None)], incs=[(ev, 1)])
                    p.emit("vector",
                           lambda e, ot=ot, pu=pu, b=b: e.tensor_tensor(
                               hidT[:, ot * CHUNK:(ot + 1) * CHUNK],
                               silu_s[:, :], pu[b][:, :], ALU.mult),
                           waits=[(mmg, None), (ev, None)], incs=[(ev, 1)])
                gu_ev_done[gq] = ev.n
                gu_mm_done[gq] = mmg.n

        for i, t in enumerate(tiles):
            p.dma("sync", xacc[i][:, :], x_in[t], waits=[(dev, None)], incs=[(ald, 16)])
        with ExitStack() as ps:
            pd = [ps.enter_context(nc.psum_tensor(f"{tag}pd{q}_{i}", [128, 512], F32))
                  for i in range(8)]
            for dh in range(2):
                for qt in range(O_TILES // 4):
                    b = qt & 1
                    gqt = (q * 2 + dh) * (O_TILES // 4) + qt
                    d_w = [(mmd, dq_mm_done.get(gqt - 2, 0))]
                    p.dma("gpsimd", wdp[b][:, :].rearrange("p (t d) -> p t d", t=4),
                          wd_d[4 * qt:4 * qt + 4, :, dh * 1024:(dh + 1) * 1024]
                          .rearrange("t p d -> p t d"),
                          waits=d_w, incs=[(gwd[b], 16)])
                    for j in range(4):
                        tt = 4 * qt + j

                        def mm_down(e, tt=tt, j=j, b=b, pd=pd):
                            for i in range(4):
                                for dc in range(2):
                                    inst = e.matmul(
                                        pd[i * 2 + dc][:, :],
                                        hidT[:, tt * CHUNK + i * 128: tt * CHUNK + (i + 1) * 128],
                                        wdp[b][:, j * 1024 + dc * 512: j * 1024 + (dc + 1) * 512],
                                        start=(tt == 0), stop=(tt == O_TILES - 1))
                            return inst
                        mw = [(gwd[b], None)]
                        if tt == 0:
                            mw += [(ev, None), (dev, None)]
                        p.emit("tensor", mm_down, waits=mw, incs=[(mmd, 1)])
                    dq_mm_done[gqt] = mmd.n
                for i in range(4):
                    for dc in range(2):
                        col = dh * 1024 + dc * 512
                        if layer == 0:
                            p.emit("vector",
                                   lambda e, i=i, dc=dc, col=col, pd=pd: e.tensor_tensor(
                                       up_s[:, :], pd[i * 2 + dc][:, :],
                                       sdb[:, col:col + 512], ALU.mult),
                                   waits=[(mmd, None), (ald, None), (dev, None)],
                                   incs=[(dev, 1)])
                            p.emit("vector",
                                   lambda e, i=i, col=col: e.tensor_tensor(
                                       xacc[i][:, col:col + 512], xacc[i][:, col:col + 512],
                                       up_s[:, :], ALU.add),
                                   waits=[(dev, None)], incs=[(dev, 1)])
                        else:
                            p.emit("vector",
                                   lambda e, i=i, dc=dc, col=col, pd=pd: e.tensor_tensor(
                                       xacc[i][:, col:col + 512], xacc[i][:, col:col + 512],
                                       pd[i * 2 + dc][:, :], ALU.add),
                                   waits=[(mmd, None), (ald, None)], incs=[(dev, 1)])
        for i, t in enumerate(tiles):
            p.dma("sync", xacc_d[t], xacc[i][:, :], waits=[(dev, None)], incs=[(sto, 16)])

    return sto



def emit_w2pp(p, C, T, tag="w2_"):
    nc = p.nc
    ctx = p.alloc_ctx
    sb = lambda name, shape, dt: ctx.enter_context(nc.sbuf_tensor(tag + name, shape, dt))
    w2t = T["l1_w2t"].ap()
    w2a = T["l1_w2a"].ap()
    w2pp_flat = T["l1_w2pp"].ap().rearrange("t p d -> (t p) d")

    h128 = sb("h128", [128, 128], BF16)
    h2x64 = sb("h2x64", [128, 128], BF16)
    win = sb("win", [128, 2048], BF16)
    wout = sb("wout", [128, 2048], BF16)
    ld = p.sem(tag + "ld")
    gld = p.sem(tag + "gld")
    mm = p.sem(tag + "mm")
    evs = p.sem(tag + "ev")
    sto = p.sem(tag + "sto")
    p.dma("sync", h128[:, :], T["h128"].ap()[:, :], incs=[(ld, 16)])
    p.dma("sync", h2x64[:, :], T["h2x64"].ap()[:, :], incs=[(ld, 16)])

    w2a_v = w2a.rearrange("(a b tt) d -> a b tt d", a=64, b=2)
    w2a_g = w2a.rearrange("(g q) d -> g q d", g=64)
    w2pp_v = w2pp_flat.rearrange("(tau gb) d -> tau gb d", gb=128)

    with nc.psum_tensor(tag + "ps", [128, 512], F32) as ps:
        for t in range(64):
            p.dma("gpsimd", win[:, :], w2t[t], waits=[(sto, None)], incs=[(gld, 16)])
            for dchunk in range(4):
                p.emit("tensor",
                       lambda e, dchunk=dchunk: e.matmul(
                           ps[:, :], h128[:, :], win[:, dchunk * 512:(dchunk + 1) * 512],
                           start=True, stop=True),
                       waits=[(ld, None), (gld, None), (evs, None)], incs=[(mm, 1)])
                p.emit("vector",
                       lambda e, dchunk=dchunk: e.tensor_copy(
                           wout[:, dchunk * 512:(dchunk + 1) * 512], ps[:, :]),
                       waits=[(mm, None)], incs=[(evs, 1)])
            p.dma("sync", w2a_v[:, :, t, :].rearrange("a b d -> (a b) d"), wout[:, :],
                  waits=[(evs, None)], incs=[(sto, 16)])
        for g in range(64):
            p.dma("gpsimd", win[:, :], w2a_g[g], waits=[(sto, None)], incs=[(gld, 16)])
            for dchunk in range(4):
                p.emit("tensor",
                       lambda e, dchunk=dchunk: e.matmul(
                           ps[:, :], h2x64[:, :], win[:, dchunk * 512:(dchunk + 1) * 512],
                           start=True, stop=True),
                       waits=[(ld, None), (gld, None), (evs, None)], incs=[(mm, 1)])
                p.emit("vector",
                       lambda e, dchunk=dchunk: e.tensor_scalar(
                           wout[:, dchunk * 512:(dchunk + 1) * 512], ps[:, :],
                           C.v(5), None, ALU.mult),
                       waits=[(mm, None), (C.ready, None)], incs=[(evs, 1)])
            for beta in range(2):
                p.dma("sync", w2pp_v[:, g * 2 + beta, :], wout[beta * 64:(beta + 1) * 64, :],
                      waits=[(evs, None)], incs=[(sto, 16)])
    return sto



def emit_attn(p, C, layer, T, tag, dbg=False):
    nc = p.nc
    ctx = p.alloc_ctx
    sb = lambda name, shape, dt: ctx.enter_context(nc.sbuf_tensor(tag + name, shape, dt))

    xnT_d = T["xnT"].ap()
    xbase_d = T["x_base"].ap()
    xout_d = T["x_out"].ap()
    aoT_d = T["aoT"].ap()
    NH = 16

    xnT_sb = sb("xnT", [128, E_BLKS * 2048], BF16)
    wql = [sb(f"wql{b}", [128, 2048], BF16) for b in range(2)]
    nkv = 2 if layer == 0 else 1
    wkl = [sb(f"wkl{b}", [128, 2048], BF16) for b in range(nkv)]
    wvl = [sb(f"wvl{b}", [128, 2048], BF16) for b in range(nkv)]
    fT = sb("fT", [128, 2048], F32)
    fsq = sb("fsq", [128, 2048], F32)
    rstd2 = sb("rstd2", [128, 2048], F32)
    lnrow = sb("lnrow", [1, 2048], F32)
    qhatT = sb("qhatT", [128, 2048], BF16)
    eT = sb("eT", [128, 16 * 512], BF16)
    cols = sb("cols", [128, 34], F32)
    zrow = sb("zrow", [1, 2048], F32)
    zb = sb("zb", [128, 512], F32)
    ao_p = sb("ao_p", [128, 512], BF16)
    xbh = fsq[:, 0:1024]
    xoh = fsq[:, 1024:2048]
    wop = [wql[b][:, 0:1024] for b in range(2)]
    aop_sb = [wql[b][:, 1024:1536] for b in range(2)]
    if layer == 0:
        e_nat = sb("e_nat", [128, 2048], BF16)
        mrow = sb("mrow", [1, 2048], BF16)
        khatT = [sb("khatT", [128, 2048], BF16)]
        v_nat = [sb("v", [128, 2048], BF16)]
        qs_c = sb("qs_c", [128, 1], F32)
        ks_c = sb("ks_c", [128, 1], F32)
        s2c = sb("s2c", [128, 1], F32)
    else:
        khatT = [sb(f"khatT{g}", [128, 2048], BF16) for g in range(4)]
        vT_bf = [sb(f"vT{g}", [128, 2048], BF16) for g in range(4)]
        v_nat = sb("v", [128, 16 * 512], BF16)
        ut_f = sb("ut_f", [128, 512], F32)
        uv_f = sb("uv_f", [128, 512], F32)
        g2b = sb("g2b", [128, 512], F32)
        g2row = sb("g2row", [1, 2048], F32)

    ld = p.sem(tag + "ld")
    aold = p.sem(tag + "aold")
    xbld = p.sem(tag + "xbld")
    gld = [p.sem(tag + f"gld{b}") for b in range(2)]
    gvn = p.sem(tag + "gvn")
    gkv = p.sem(tag + "gkv")
    glw = [p.sem(tag + f"glw{b}") for b in range(2)]
    aold2 = [p.sem(tag + f"aold{b}") for b in range(2)]
    mm = p.sem(tag + "mm")
    se = p.sem(tag + "se")
    ve = p.sem(tag + "ve")
    sto = p.sem(tag + "sto")

    for eb in range(E_BLKS):
        p.dma("sync", xnT_sb[:, eb * 2048:(eb + 1) * 2048], xnT_d[eb], incs=[(ld, 16)])
    if layer == 0:
        p.dma("sync", qs_c[:, :], T["l0_qscale"].ap()[:, :], incs=[(ld, 16)])
        p.dma("sync", ks_c[:, :], T["l0_kscale"].ap()[:, :], incs=[(ld, 16)])
        p.emit("vector", lambda e: e.tensor_tensor(s2c[:, :], qs_c[:, :], ks_c[:, :],
                                                   ALU.mult),
               waits=[(ld, None)], incs=[(ve, 1)])

    def proj_T(wsb, pks, wsem):
        se_after = {}
        for sc in range(4):
            pq = pks[sc & 1]

            def mmf(e, sc=sc, pq=pq):
                for c in range(E_BLKS):
                    inst = e.matmul(pq[:, :], wsb[:, c * 128:(c + 1) * 128],
                                    xnT_sb[:, c * 2048 + sc * 512: c * 2048 + (sc + 1) * 512],
                                    start=(c == 0), stop=(c == E_BLKS - 1))
                return inst
            w = [(ld, None), (wsem, None)]
            if sc < 2:
                w += [(se, None), (ve, None)]
            else:
                w.append((se, se_after[sc - 2]))
            p.emit("tensor", mmf, waits=w, incs=[(mm, 1)])
            p.emit("scalar",
                   lambda e, sc=sc, pq=pq: e.activation(fT[:, sc * 512:(sc + 1) * 512],
                                                        pq[:, :], AF.Identity),
                   waits=[(mm, None), (ve, None)], incs=[(se, 1)])
            se_after[sc] = se.n

    def rms_rows(prow, pb, eps_col):
        p.emit("vector", lambda e: e.tensor_tensor(fsq[:, :], fT[:, :], fT[:, :], ALU.mult),
               waits=[(se, None), (mm, None)], incs=[(ve, 1)])
        for sc in range(4):
            p.emit("tensor",
                   lambda e, sc=sc: e.matmul(prow[:, :], C.red1_f32[:, :],
                                             fsq[:, sc * 512:(sc + 1) * 512],
                                             start=True, stop=True),
                   waits=[(ve, None), (se, None)], incs=[(mm, 1)])
            p.emit("scalar",
                   lambda e, sc=sc: e.activation(lnrow[0:1, sc * 512:(sc + 1) * 512],
                                                 prow[0:1, :], AF.Ln,
                                                 bias=eps_col, scale=C.v1(6)),
                   waits=[(mm, None)], incs=[(se, 1)])
        for sc in range(4):
            p.emit("tensor",
                   lambda e, sc=sc: e.matmul(pb[:, :], C.bcast1_f32[:, :],
                                             lnrow[0:1, sc * 512:(sc + 1) * 512],
                                             start=True, stop=True),
                   waits=[(se, None), (ve, None)], incs=[(mm, 1)])
            p.emit("scalar",
                   lambda e, sc=sc: e.activation(rstd2[:, sc * 512:(sc + 1) * 512],
                                                 pb[:, :], AF.Exp, scale=-0.5),
                   waits=[(mm, None), (ve, None), (se, None)], incs=[(se, 1)])

    if layer == 1:
        wvn = eT
        for c in range(E_BLKS):
            p.dma("gpsimd", wvn[:, c * 512:(c + 1) * 512], T["l1_wv_n"].ap()[c],
                  incs=[(gvn, 16)])
        with nc.psum_tensor(tag + "pvn", [128, 512], F32) as pv:
            for t in range(S_TILES):
                def mmv(e, t=t, pv=pv):
                    for c in range(E_BLKS):
                        inst = e.matmul(pv[:, :],
                                        xnT_sb[:, c * 2048 + t * 128: c * 2048 + (t + 1) * 128],
                                        wvn[:, c * 512:(c + 1) * 512],
                                        start=(c == 0), stop=(c == E_BLKS - 1))
                    return inst
                p.emit("tensor", mmv, waits=[(ld, None), (gvn, None), (ve, None)],
                       incs=[(mm, 1)])
                p.emit("vector",
                       lambda e, t=t, pv=pv: e.tensor_copy(v_nat[:, t * 512:(t + 1) * 512], pv[:, :]),
                       waits=[(mm, None)], incs=[(ve, 1)])
        for g in range(4):
            with nc.psum_tensor(f"{tag}kk{g}", [128, 512], F32) as pk0, \
                 nc.psum_tensor(f"{tag}kk{g}x", [128, 512], F32) as pk1, \
                 nc.psum_tensor(f"{tag}kr{g}", [1, 512], F32) as prow, \
                 nc.psum_tensor(f"{tag}kb{g}", [128, 512], F32) as pb:
                pk = [pk0, pk1]
                p.dma("gpsimd", wkl[0][:, :], T["l1_wk"].ap()[g].rearrange("p c j -> p (c j)"),
                      waits=[(mm, None)], incs=[(gkv, 16)])
                proj_T(wkl[0], pk, gkv)
                rms_rows(prow, pb, C.v1(4))
                p.emit("vector",
                       lambda e, g=g: e.scalar_tensor_tensor(khatT[g][:, :], fT[:, :],
                                                             C.v(8), rstd2[:, :],
                                                             ALU.mult, ALU.mult),
                       waits=[(se, None)], incs=[(ve, 1)])
                p.dma("gpsimd", wvl[0][:, :], T["l1_wv"].ap()[g].rearrange("p c j -> p (c j)"),
                      waits=[(ve, None)], incs=[(gkv, 16)])
                proj_T(wvl[0], pk, gkv)
                p.emit("vector", lambda e: e.tensor_tensor(fsq[:, :], fT[:, :], fT[:, :],
                                                           ALU.mult),
                       waits=[(se, None)], incs=[(ve, 1)])
                for sc in range(4):
                    p.emit("tensor",
                           lambda e, sc=sc, prow=prow: e.matmul(prow[:, :], C.red1_f32[:, :],
                                                     fsq[:, sc * 512:(sc + 1) * 512],
                                                     start=True, stop=True),
                           waits=[(ve, None), (se, None)], incs=[(mm, 1)])
                    p.emit("vector",
                           lambda e, sc=sc, prow=prow: e.tensor_scalar(
                               lnrow[0:1, sc * 512:(sc + 1) * 512], prow[0:1, :],
                               C.v1(7), None, ALU.max),
                           waits=[(mm, None)], incs=[(ve, 1)])
                    p.emit("scalar",
                           lambda e, sc=sc: e.activation(
                               lnrow[0:1, sc * 512:(sc + 1) * 512],
                               lnrow[0:1, sc * 512:(sc + 1) * 512], AF.Ln),
                           waits=[(ve, None)], incs=[(se, 1)])
                    p.emit("tensor",
                           lambda e, sc=sc, pb=pb: e.matmul(pb[:, :], C.bcast1_f32[:, :],
                                                     lnrow[0:1, sc * 512:(sc + 1) * 512],
                                                     start=True, stop=True),
                           waits=[(se, None)], incs=[(mm, 1)])
                    p.emit("scalar",
                           lambda e, sc=sc, pb=pb: e.activation(rstd2[:, sc * 512:(sc + 1) * 512],
                                                         pb[:, :], AF.Exp, scale=-0.5),
                           waits=[(mm, None), (ve, None)], incs=[(se, 1)])
                    p.emit("vector",
                           lambda e, sc=sc, g=g: e.tensor_tensor(
                               vT_bf[g][:, sc * 512:(sc + 1) * 512],
                               fT[:, sc * 512:(sc + 1) * 512],
                               rstd2[:, sc * 512:(sc + 1) * 512], ALU.mult),
                           waits=[(se, None)], incs=[(ve, 1)])

    head_done = {}
    for h in range(NH):
        hb = h & 1
        with nc.psum_tensor(f"{tag}hk{h}", [128, 512], F32) as pk0, \
             nc.psum_tensor(f"{tag}hk{h}x", [128, 512], F32) as pk1, \
             nc.psum_tensor(f"{tag}hr{h}", [1, 512], F32) as prow, \
             nc.psum_tensor(f"{tag}hb{h}", [128, 512], F32) as pb, \
             nc.psum_tensor(f"{tag}hv{h}", [128, 128], F32) as pv:
            pk = [pk0, pk1]
            if layer == 0:
                hw = [(mm, head_done.get(h - 2, 0))]
                p.dma("gpsimd", wql[hb][:, :], T["l0_wq"].ap()[h].rearrange("p c j -> p (c j)"),
                      waits=hw, incs=[(gld[hb], 16)])
                p.dma("gpsimd", wkl[hb][:, :], T["l0_wk"].ap()[h].rearrange("p c j -> p (c j)"),
                      incs=[(gld[hb], 16)])
                p.dma("gpsimd", wvl[hb][:, :], T["l0_wv"].ap()[h].rearrange("p c j -> p (c j)"),
                      incs=[(gld[hb], 16)])
                proj_T(wkl[hb], pk, gld[hb])
                rms_rows(prow, pb, C.v1(3))
                p.emit("vector",
                       lambda e: e.scalar_tensor_tensor(khatT[0][:, :], fT[:, :],
                                                        s2c[:, :], rstd2[:, :],
                                                        ALU.mult, ALU.mult),
                       waits=[(se, None), (ve, None)], incs=[(ve, 1)])
                for t in range(S_TILES):
                    def mmv(e, t=t, pv=pv, hb=hb):
                        for c in range(E_BLKS):
                            inst = e.matmul(pv[:, :],
                                            xnT_sb[:, c * 2048 + t * 128: c * 2048 + (t + 1) * 128],
                                            wvl[hb][:, c * 128:(c + 1) * 128],
                                            start=(c == 0), stop=(c == E_BLKS - 1))
                        return inst
                    p.emit("tensor", mmv, waits=[(ld, None), (gld[hb], None), (ve, None)],
                           incs=[(mm, 1)])
                    p.emit("vector",
                           lambda e, t=t, pv=pv: e.tensor_copy(v_nat[0][:, t * 128:(t + 1) * 128],
                                                        pv[:, :]),
                           waits=[(mm, None)], incs=[(ve, 1)])
                proj_T(wql[hb], pk, gld[hb])
                rms_rows(prow, pb, C.v1(3))
                p.emit("vector",
                       lambda e: e.tensor_tensor(qhatT[:, :], fT[:, :], rstd2[:, :], ALU.mult),
                       waits=[(se, None)], incs=[(ve, 1)])
                kh = khatT[0]
                head_done[h] = mm.n
            else:
                g = h // 4
                hw = [(mm, head_done.get(h - 2, 0) if h >= 2 else 0)]
                p.dma("gpsimd", wql[hb][:, :], T["l1_wq"].ap()[h].rearrange("p c j -> p (c j)"),
                      waits=hw, incs=[(gld[hb], 16)])
                proj_T(wql[hb], pk, gld[hb])
                rms_rows(prow, pb, C.v1(4))
                p.emit("vector",
                       lambda e: e.tensor_tensor(qhatT[:, :], fT[:, :], rstd2[:, :], ALU.mult),
                       waits=[(se, None)], incs=[(ve, 1)])
                kh = khatT[g]
                head_done[h] = mm.n

        if layer == 0:
            with nc.psum_tensor(f"{tag}sn{h}", [128, 2048], F32) as psn, \
                 nc.psum_tensor(f"{tag}mr{h}", [1, 128], F32) as pmr:
                for t in range(S_TILES):
                    def mmsn(e, t=t, psn=psn):
                        for kc in range(4):
                            inst = e.matmul(psn[:, kc * 512:(kc + 1) * 512],
                                            qhatT[:, t * 128:(t + 1) * 128],
                                            kh[:, kc * 512:(kc + 1) * 512],
                                            start=True, stop=True)
                        return inst
                    p.emit("tensor", mmsn, waits=[(ve, None), (se, None)], incs=[(mm, 1)])
                    p.emit("vector",
                           lambda e, t=t, psn=psn: e.reduce_max(cols[:, t:t + 1], psn[:, :],
                                                       axis=AX.X, negate=True),
                           waits=[(mm, None)], incs=[(ve, 1)])
                    p.emit("scalar",
                           lambda e, t=t, psn=psn: e.activation(e_nat[:, :], psn[:, :], AF.Exp,
                                                       bias=cols[:, t:t + 1],
                                                       accum_out=cols[:, 16 + t:17 + t]),
                           waits=[(ve, None)], incs=[(se, 1)])
                    p.emit("tensor",
                           lambda e, t=t, pmr=pmr: e.transpose(pmr[:, :], cols[:, t:t + 1],
                                                      C.idn_f32[:, :]),
                           waits=[(ve, None), (se, None), (C.ld, None)], incs=[(mm, 1)])
                    p.emit("vector",
                           lambda e, t=t, pmr=pmr: e.tensor_copy(
                               mrow[0:1, t * 128:(t + 1) * 128], pmr[0:1, :]),
                           waits=[(mm, None)], incs=[(ve, 1)])
                p.emit("vector",
                       lambda e: e.reciprocal(cols[:, 16:32], cols[:, 16:32]),
                       waits=[(se, None)], incs=[(ve, 1)])
                for t in range(S_TILES):
                    p.emit("tensor",
                           lambda e, t=t, pmr=pmr: e.transpose(pmr[:, :], cols[:, 16 + t:17 + t],
                                                      C.idn_f32[:, :]),
                           waits=[(ve, None), (C.ld, None)], incs=[(mm, 1)])
                    p.emit("vector",
                           lambda e, t=t, pmr=pmr: e.tensor_copy(
                               zrow[0:1, t * 128:(t + 1) * 128], pmr[0:1, :]),
                           waits=[(mm, None)], incs=[(ve, 1)])
        with nc.psum_tensor(f"{tag}sT{h}", [128, 512], F32) as pT, \
             nc.psum_tensor(f"{tag}av{h}", [128, 512], F32) as pav, \
             nc.psum_tensor(f"{tag}zz{h}", [1, 512], F32) as pz, \
             nc.psum_tensor(f"{tag}dt{h}", [1, 512], F32) as pdot, \
             nc.psum_tensor(f"{tag}zb{h}", [128, 512], F32) as pzb:
            for qc in range(4):
                for kt in range(S_TILES):
                    def mmsT(e, kt=kt, qc=qc, pT=pT, kh=kh):
                        if layer == 0:
                            e.matmul(pT[:, :], kh[:, kt * 128:(kt + 1) * 128],
                                     qhatT[:, qc * 512:(qc + 1) * 512],
                                     start=True, stop=False)
                            return e.matmul(pT[:, :], C.ones_row_bf[:, :],
                                            mrow[0:1, qc * 512:(qc + 1) * 512],
                                            start=False, stop=True)
                        return e.matmul(pT[:, :], kh[:, kt * 128:(kt + 1) * 128],
                                        qhatT[:, qc * 512:(qc + 1) * 512],
                                        start=True, stop=True)
                    p.emit("tensor", mmsT,
                           waits=[(se, None)] + ([(ve, None)] if kt == 0 else []),
                           incs=[(mm, 1)])
                    p.emit("scalar",
                           lambda e, kt=kt, pT=pT: e.activation(eT[:, kt * 512:(kt + 1) * 512],
                                                         pT[:, :], AF.Exp),
                           waits=[(mm, None)], incs=[(se, 1)])

                def mmav(e, qc=qc, h=h, pav=pav):
                    for kt in range(S_TILES):
                        lhs = (v_nat[0][:, kt * 128:(kt + 1) * 128] if layer == 0 else
                               v_nat[:, kt * 512 + (h // 4) * 128:
                                     kt * 512 + (h // 4) * 128 + 128])
                        inst = e.matmul(pav[:, :], lhs, eT[:, kt * 512:(kt + 1) * 512],
                                        start=(kt == 0), stop=(kt == S_TILES - 1))
                    return inst
                p.emit("tensor", mmav, waits=[(se, None), (ve, None)], incs=[(mm, 1)])

                if layer == 0:
                    p.emit("tensor",
                           lambda e, qc=qc, pzb=pzb: e.matmul(pzb[:, :], C.bcast1_f32[:, :],
                                                     zrow[0:1, qc * 512:(qc + 1) * 512],
                                                     start=True, stop=True),
                           waits=[(ve, None)], incs=[(mm, 1)])
                    p.emit("vector",
                           lambda e, pzb=pzb: e.tensor_copy(zb[:, :], pzb[:, :]),
                           waits=[(mm, None)], incs=[(ve, 1)])
                    p.emit("vector",
                           lambda e, pav=pav: e.tensor_tensor(ao_p[:, :], pav[:, :], zb[:, :],
                                                     ALU.mult),
                           waits=[(sto, None), (ve, None)], incs=[(ve, 1)])
                else:
                    g = h // 4

                    def mmz(e, pz=pz):
                        for kt in range(S_TILES):
                            inst = e.matmul(pz[:, :], C.red1_bf[:, :],
                                            eT[:, kt * 512:(kt + 1) * 512],
                                            start=(kt == 0), stop=(kt == S_TILES - 1))
                        return inst
                    p.emit("tensor", mmz, waits=[(ve, None)], incs=[(mm, 1)])
                    p.emit("vector",
                           lambda e, g=g, qc=qc, pav=pav: e.tensor_tensor(
                               uv_f[:, :], pav[:, :],
                               vT_bf[g][:, qc * 512:(qc + 1) * 512], ALU.mult),
                           waits=[(mm, None)], incs=[(ve, 1)])
                    p.emit("scalar",
                           lambda e, pav=pav: e.activation(ut_f[:, :], pav[:, :], AF.Identity),
                           waits=[(mm, None), (ve, None)], incs=[(se, 1)])
                    p.emit("tensor",
                           lambda e, pdot=pdot: e.matmul(pdot[:, :], C.red1_f32[:, :], uv_f[:, :],
                                              start=True, stop=True),
                           waits=[(ve, None)], incs=[(mm, 1)])
                    p.emit("vector",
                           lambda e, qc=qc, pz=pz: e.reciprocal(
                               zrow[0:1, qc * 512:(qc + 1) * 512], pz[0:1, :]),
                           waits=[(mm, None)], incs=[(ve, 1)])
                    p.emit("vector",
                           lambda e, qc=qc, pdot=pdot: e.tensor_tensor(
                               g2row[0:1, qc * 512:(qc + 1) * 512], pdot[0:1, :],
                               zrow[0:1, qc * 512:(qc + 1) * 512], ALU.mult),
                           waits=[(mm, None), (ve, None)], incs=[(ve, 1)])
                    p.emit("tensor",
                           lambda e, qc=qc, pzb=pzb: e.matmul(pzb[:, :], C.bcast1_f32[:, :],
                                                     zrow[0:1, qc * 512:(qc + 1) * 512],
                                                     start=True, stop=True),
                           waits=[(ve, None)], incs=[(mm, 1)])
                    p.emit("vector", lambda e, pzb=pzb: e.tensor_copy(zb[:, :], pzb[:, :]),
                           waits=[(mm, None)], incs=[(ve, 1)])
                    p.emit("tensor",
                           lambda e, qc=qc, pzb=pzb: e.matmul(pzb[:, :], C.bcast1_f32[:, :],
                                                     g2row[0:1, qc * 512:(qc + 1) * 512],
                                                     start=True, stop=True),
                           waits=[(ve, None)], incs=[(mm, 1)])
                    p.emit("vector", lambda e, pzb=pzb: e.tensor_copy(g2b[:, :], pzb[:, :]),
                           waits=[(mm, None)], incs=[(ve, 1)])
                    p.emit("vector",
                           lambda e: e.tensor_tensor(ut_f[:, :], ut_f[:, :], zb[:, :],
                                                     ALU.mult),
                           waits=[(se, None), (ve, None)], incs=[(ve, 1)])
                    p.emit("vector",
                           lambda e, g=g, qc=qc: e.tensor_tensor(
                               uv_f[:, :], g2b[:, :],
                               vT_bf[g][:, qc * 512:(qc + 1) * 512], ALU.mult),
                           waits=[(ve, None)], incs=[(ve, 1)])
                    p.emit("vector",
                           lambda e: e.tensor_tensor(ao_p[:, :], ut_f[:, :], uv_f[:, :],
                                                     ALU.subtract),
                           waits=[(sto, None), (ve, None)], incs=[(ve, 1)])
                p.dma("sync", aoT_d[h, :, qc * 512:(qc + 1) * 512], ao_p[:, :],
                      waits=[(ve, None)], incs=[(sto, 16)])

    wo_d = T["l0_wo"].ap() if layer == 0 else T["l1_wo"].ap()
    wo_done = {}
    for tg in range(4):
        for eh in range(2):
            with ExitStack() as ps:
                pw = [ps.enter_context(nc.psum_tensor(f"{tag}pw{tg}{eh}_{i}", [128, 512], F32))
                      for i in range(8)]
                for c in range(16):
                    gc = (tg * 2 + eh) * 16 + c
                    wb = gc & 1
                    ww = [(mm, wo_done[gc - 2])] if gc >= 2 else [(mm, None)]
                    p.dma("gpsimd", wop[wb], wo_d[c, :, eh * 1024:(eh + 1) * 1024],
                          waits=ww, incs=[(glw[wb], 16)])
                    p.dma("sync", aop_sb[wb], aoT_d[c, :, tg * 512:(tg + 1) * 512],
                          waits=list(ww), incs=[(aold2[wb], 16)])

                    def mmw(e, c=c, pw=pw, wb=wb):
                        for i in range(4):
                            for ec in range(2):
                                inst = e.matmul(pw[i * 2 + ec][:, :],
                                                aop_sb[wb][:, i * 128:(i + 1) * 128],
                                                wop[wb][:, ec * 512:(ec + 1) * 512],
                                                start=(c == 0), stop=(c == 15))
                        return inst
                    p.emit("tensor", mmw,
                           waits=[(aold2[wb], None), (glw[wb], None)] + ([(ve, None)] if c == 0 else []),
                           incs=[(mm, 1)])
                    wo_done[gc] = mm.n
                for i in range(4):
                    t = tg * 4 + i
                    p.dma("sync", xbh, xbase_d[t, :, eh * 1024:(eh + 1) * 1024],
                          waits=[(ve, None)], incs=[(xbld, 16)])
                    for ec in range(2):
                        p.emit("vector",
                               lambda e, i=i, ec=ec, pw=pw: e.tensor_tensor(
                                   fsq[:, 1024 + ec * 512:1024 + (ec + 1) * 512],
                                   fsq[:, ec * 512:(ec + 1) * 512],
                                   pw[i * 2 + ec][:, :], ALU.add),
                               waits=[(mm, None), (xbld, None), (sto, None)], incs=[(ve, 1)])
                    p.dma("sync", xout_d[t, :, eh * 1024:(eh + 1) * 1024], xoh,
                          waits=[(ve, None)], incs=[(sto, 16)])
    return sto


def _mk_dram(nc, specs):
    T = {}
    for name, shape, dt, kind in specs:
        T[name] = nc.dram_tensor(name, list(shape), dt, kind=kind)
    return T


_FFN_L0_W = [
    ("l0_wg", (64, 128, 16, 128), I8, "ExternalInput"),
    ("l0_wu", (64, 128, 16, 128), I8, "ExternalInput"),
    ("l0_wd", (64, 128, 2048), I8, "ExternalInput"),
    ("l0_sg", (128, 64), F32, "ExternalInput"),
    ("l0_su", (128, 64), F32, "ExternalInput"),
    ("l0_sdown", (1, 2048), F32, "ExternalInput"),
]
_FFN_L1_W = [
    ("l1_w1", (64, 128, 16, 128), F32, "ExternalInput"),
    ("l1_w3", (64, 128, 16, 128), F32, "ExternalInput"),
    ("l1_w2t", (64, 128, 2048), F32, "ExternalInput"),
    ("h128", (128, 128), BF16, "ExternalInput"),
    ("h2x64", (128, 128), BF16, "ExternalInput"),
    ("l1_w2a", (8192, 2048), BF16, "Internal"),
    ("l1_w2pp", (64, 128, 2048), BF16, "Internal"),
]


_ATTN_L0_W = [
    ("l0_wq", (16, 128, 16, 128), F32, "ExternalInput"),
    ("l0_wk", (16, 128, 16, 128), F32, "ExternalInput"),
    ("l0_wv", (16, 128, 16, 128), F32, "ExternalInput"),
    ("l0_wo", (16, 128, 2048), F32, "ExternalInput"),
    ("l0_qscale", (128, 1), F32, "ExternalInput"),
    ("l0_kscale", (128, 1), F32, "ExternalInput"),
]
_ATTN_L1_W = [
    ("l1_wq", (16, 128, 16, 128), F32, "ExternalInput"),
    ("l1_wk", (4, 128, 16, 128), F32, "ExternalInput"),
    ("l1_wv", (4, 128, 16, 128), F32, "ExternalInput"),
    ("l1_wv_n", (16, 128, 512), F32, "ExternalInput"),
    ("l1_wo", (16, 128, 2048), F32, "ExternalInput"),
]


def build_attn_nc(layer, dbg=False):
    nc = bass.Bass("TRN2", target_bir_lowering=False)
    specs = [
        ("xnT", (16, 128, 2048), BF16, "ExternalInput"),
        ("x_base", (16, 128, 2048), F32, "ExternalInput"),
        ("idn_bf", (128, 128), BF16, "ExternalInput"),
        ("idn_f32", (128, 128), F32, "ExternalInput"),
        ("x_out", (16, 128, 2048), F32, "ExternalOutput"),
        ("aoT", (16, 128, 2048), BF16, "ExternalOutput" if dbg else "Internal"),
    ]
    if dbg:
        specs += [
            ("d_qhatT", (128, 2048), BF16, "ExternalOutput"),
            ("d_fT", (128, 2048), F32, "ExternalOutput"),
            ("d_wql", (128, 2048), BF16, "ExternalOutput"),
            ("d_xnT0", (128, 2048), BF16, "ExternalOutput"),
            ("d_rstd2", (128, 2048), F32, "ExternalOutput"),
            ("d_khatT", (128, 2048), BF16, "ExternalOutput"),
            ("d_v", (128, 2048), BF16, "ExternalOutput"),
            ("d_cols", (128, 34), F32, "ExternalOutput"),
            ("d_mrow", (1, 2048), BF16, "ExternalOutput"),
            ("d_zrow", (1, 2048), F32, "ExternalOutput"),
            ("d_eT", (128, 8192), BF16, "ExternalOutput"),
        ]
    specs += _ATTN_L0_W if layer == 0 else _ATTN_L1_W
    with ExitStack() as ctx:
        T = _mk_dram(nc, specs)
        p = Prog(nc, ctx)
        C = Consts(nc, ctx, p, T)
        sto = emit_attn(p, C, layer, T, tag=f"a{layer}_", dbg=dbg)
        p.emit("sync", lambda e: None, waits=[(sto, None)])
        p.finalize()
    return nc


def build_ffn_nc(layer):
    nc = bass.Bass("TRN2", target_bir_lowering=False)
    specs = [
        ("x_in", (16, 128, 2048), F32, "ExternalInput"),
        ("idn_bf", (128, 128), BF16, "ExternalInput"),
        ("idn_f32", (128, 128), F32, "ExternalInput"),
        ("xnT", (16, 128, 2048), BF16, "ExternalOutput"),
        ("x_acc", (16, 128, 2048), F32, "ExternalOutput"),
    ]
    specs += _FFN_L0_W if layer == 0 else _FFN_L1_W
    with ExitStack() as ctx:
        T = _mk_dram(nc, specs)
        p = Prog(nc, ctx)
        C = Consts(nc, ctx, p, T)
        if layer == 1:
            w2done = emit_w2pp(p, C, T)
            for eng in ENGINES:
                p.emit(eng, lambda e: None, waits=[(w2done, None)])
        sto = emit_ffn(p, C, layer, T, tag=f"f{layer}_")
        p.emit("sync", lambda e: None, waits=[(sto, None)])
        p.finalize()
    return nc


def build_fused_nc():
    nc = bass.Bass("TRN2", target_bir_lowering=False)
    specs = [
        ("x_in", (16, 128, 2048), F32, "ExternalInput"),
        ("idn_bf", (128, 128), BF16, "ExternalInput"),
        ("idn_f32", (128, 128), F32, "ExternalInput"),
        ("h128", (128, 128), BF16, "ExternalInput"),
        ("h2x64", (128, 128), BF16, "ExternalInput"),
        ("out", (16, 128, 2048), F32, "ExternalOutput"),
        ("xnT_s", (16, 128, 2048), BF16, "Internal"),
        ("xacc_s", (16, 128, 2048), F32, "Internal"),
        ("x1_s", (16, 128, 2048), F32, "Internal"),
        ("aoT_s", (16, 128, 2048), BF16, "Internal"),
        ("l1_w2a", (8192, 2048), BF16, "Internal"),
        ("l1_w2pp", (64, 128, 2048), BF16, "Internal"),
    ]
    specs += _FFN_L0_W + _ATTN_L0_W
    specs += [t for t in _FFN_L1_W if t[0] in ("l1_w1", "l1_w3", "l1_w2t")]
    specs += _ATTN_L1_W
    with ExitStack() as ctx:
        T = _mk_dram(nc, specs)
        p = Prog(nc, ctx)
        C = Consts(nc, ctx, p, T)

        def phase(emitter):
            with ExitStack() as pc:
                p.alloc_ctx = pc
                emitter()
            p.alloc_ctx = ctx
            p.barrier()

        Tf0 = dict(T, x_in=T["x_in"], xnT=T["xnT_s"], x_acc=T["xacc_s"])
        Ta0 = dict(T, xnT=T["xnT_s"], x_base=T["xacc_s"], x_out=T["x1_s"], aoT=T["aoT_s"])
        Tf1 = dict(T, x_in=T["x1_s"], xnT=T["xnT_s"], x_acc=T["xacc_s"])
        Ta1 = dict(T, xnT=T["xnT_s"], x_base=T["xacc_s"], x_out=T["out"], aoT=T["aoT_s"])

        phase(lambda: emit_w2pp(p, C, T))
        phase(lambda: emit_ffn(p, C, 0, Tf0, tag="f0_"))
        phase(lambda: emit_attn(p, C, 0, Ta0, tag="a0_"))
        phase(lambda: emit_ffn(p, C, 1, Tf1, tag="f1_"))
        phase(lambda: emit_attn(p, C, 1, Ta1, tag="a1_"))
        p.finalize()
    return nc


_BUILD_CACHE = {}


def _get_nc(key, builder):
    if key not in _BUILD_CACHE:
        _BUILD_CACHE[key] = builder()
    return _BUILD_CACHE[key]


def kernel_fused(**inputs):
    w = prep_weights(inputs)
    c = host_consts()
    x = inputs["L0_attn_x"]
    nc = _get_nc("fused", build_fused_nc)
    base = {"idn_bf": c["idn_bf"], "idn_f32": c["idn_f32"],
            "h128": c["h128"], "h2x64": c["h2x64"]}
    for k in ("l0_wg", "l0_wu", "l0_wd", "l0_sg", "l0_su", "l0_sdown",
              "l0_wq", "l0_wk", "l0_wv", "l0_wo", "l0_qscale", "l0_kscale",
              "l1_w1", "l1_w3", "l1_w2t",
              "l1_wq", "l1_wk", "l1_wv", "l1_wv_n", "l1_wo"):
        base[k] = w[k]
    maps = [dict(base, x_in=np.ascontiguousarray(x[i].reshape(16, 128, 2048)))
            for i in range(NCORES)]
    res = run_bass_kernel_spmd(nc, maps, core_ids=list(range(NCORES)))
    out = np.empty((NCORES, 2048, 2048), dtype=np.float32)
    for i in range(NCORES):
        out[i] = res.results[i]["out"].reshape(2048, 2048)
    return out


def kernel(**inputs):
    return kernel_fused(**inputs)


def kernel_unfused(**inputs):
    w = prep_weights(inputs)
    c = host_consts()
    x = inputs["L0_attn_x"]
    consts = {"idn_bf": c["idn_bf"], "idn_f32": c["idn_f32"]}

    def shard(arr):
        return [np.ascontiguousarray(arr[cid].reshape(16, 128, 2048))
                for cid in range(NCORES)]

    core_ids = list(range(NCORES))

    nc = _get_nc("ffn0", lambda: build_ffn_nc(0))
    base = dict(consts)
    for k in ("l0_wg", "l0_wu", "l0_wd", "l0_sg", "l0_su", "l0_sdown"):
        base[k] = w[k]
    xs = shard(x)
    maps = [dict(base, x_in=xs[i]) for i in range(NCORES)]
    res = run_bass_kernel_spmd(nc, maps, core_ids=core_ids)
    xnT = [r["xnT"] for r in res.results]
    xacc = [r["x_acc"] for r in res.results]

    nc = _get_nc("attn0", lambda: build_attn_nc(0))
    base = dict(consts)
    for k in ("l0_wq", "l0_wk", "l0_wv", "l0_wo", "l0_qscale", "l0_kscale"):
        base[k] = w[k]
    maps = [dict(base, xnT=xnT[i], x_base=xacc[i]) for i in range(NCORES)]
    res = run_bass_kernel_spmd(nc, maps, core_ids=core_ids)
    x1 = [r["x_out"] for r in res.results]

    nc = _get_nc("ffn1", lambda: build_ffn_nc(1))
    base = dict(consts)
    for k in ("l1_w1", "l1_w3", "l1_w2t"):
        base[k] = w[k]
    base["h128"] = c["h128"]
    base["h2x64"] = c["h2x64"]
    maps = [dict(base, x_in=x1[i]) for i in range(NCORES)]
    res = run_bass_kernel_spmd(nc, maps, core_ids=core_ids)
    xnT = [r["xnT"] for r in res.results]
    xacc = [r["x_acc"] for r in res.results]

    nc = _get_nc("attn1", lambda: build_attn_nc(1))
    base = dict(consts)
    for k in ("l1_wq", "l1_wk", "l1_wv", "l1_wv_n", "l1_wo"):
        base[k] = w[k]
    maps = [dict(base, xnT=xnT[i], x_base=xacc[i]) for i in range(NCORES)]
    res = run_bass_kernel_spmd(nc, maps, core_ids=core_ids)

    out = np.empty((NCORES, 2048, 2048), dtype=np.float32)
    for i in range(NCORES):
        out[i] = res.results[i]["x_out"].reshape(2048, 2048)
    return out


# version 09
# speedup vs baseline: 1.8028x; 1.1879x over previous
import numpy as np
import ml_dtypes
from contextlib import ExitStack

import concourse.bass as bass
import concourse.mybir as mybir
from concourse.bass_utils import run_bass_kernel_spmd

F32 = mybir.dt.float32
BF16 = mybir.dt.bfloat16
I8 = mybir.dt.int8
AF = mybir.ActivationFunctionType
ALU = mybir.AluOpType
AX = mybir.AxisListType

NCORES = 8
S_TILES = 16
E_BLKS = 16
O_TILES = 64
CHUNK = 512
N_CHUNKS = 4

RMS_EPS_L0 = 1e-6
RMS_EPS_L1 = float(np.finfo(np.float32).eps)
LN_EPS = 1e-5
FWHT_SCALE = float(1.0 / np.sqrt(8192.0))
SM_SCALE_L1 = float(128.0 ** -0.5)

ENGINES = ("sync", "gpsimd", "scalar", "vector", "tensor")


class Sem:
    def __init__(self, nc, ctx, name):
        self.h = ctx.enter_context(nc.semaphore(name))
        self.n = 0


class Prog:

    def __init__(self, nc, ctx):
        self.nc = nc
        self.ctx = ctx
        self.alloc_ctx = ctx
        self.q = {e: [] for e in ENGINES}
        self._sems = {}

    def barrier(self):
        waits = [(s, None) for s in self._sems.values() if s.n > 0]
        for eng in ENGINES:
            self.emit(eng, lambda e: None, waits=waits)

    def sem(self, name):
        if name not in self._sems:
            self._sems[name] = Sem(self.nc, self.ctx, name)
        return self._sems[name]

    def emit(self, eng, fn, waits=(), incs=()):
        wlist = [(s.h, (s.n if v is None else v)) for (s, v) in waits]
        ilist = []
        for s, k in incs:
            s.n += k
            ilist.append((s.h, k))

        def run(e):
            for h, v in wlist:
                if v > 0:
                    e.wait_ge(h, v)
            inst = fn(e)
            for h, k in ilist:
                assert inst is not None
                inst = inst.then_inc(h, k)
        self.q[eng].append(run)

    def dma(self, eng, out_ap, in_ap, waits=(), incs=()):
        self.emit(eng, lambda e: e.dma_start(out_ap, in_ap), waits, incs)

    def finalize(self):
        with self.nc.Block() as b:
            for name in ENGINES:
                fns = self.q[name]
                if not fns:
                    continue
                getattr(b, name)(lambda e, fns=fns: [f(e) for f in fns])


def _c(a):
    return np.ascontiguousarray(a)


def prep_weights(ins):
    w = {}
    for nm in ("wq", "wk", "wv"):
        a = ins[f"L0_attn_{nm}"]
        w[f"l0_{nm}"] = _c(a.reshape(16, 128, 16, 128).transpose(2, 1, 0, 3))
    w["l0_wo"] = _c(ins["L0_attn_wo"].reshape(16, 128, 2048))
    w["l0_qscale"] = _c(ins["L0_attn_q_scale"].reshape(128, 1))
    w["l0_kscale"] = _c(ins["L0_attn_k_scale"].reshape(128, 1))
    for nm, key in (("wg", "L0_ffn_w_gate"), ("wu", "L0_ffn_w_up")):
        a = ins[key]
        w[f"l0_{nm}"] = _c(a.reshape(16, 128, 64, 128).transpose(2, 1, 0, 3))
    w["l0_wd"] = _c(ins["L0_ffn_w_down"].reshape(64, 128, 2048))
    w["l0_sg"] = _c(ins["L0_ffn_s_gate"].reshape(64, 128).T)
    w["l0_su"] = _c(ins["L0_ffn_s_up"].reshape(64, 128).T)
    w["l0_sdown"] = _c(ins["L0_ffn_s_down"].reshape(1, 2048))
    a = ins["L1_attn_Wq"]
    w["l1_wq"] = _c(a.reshape(16, 128, 16, 128).transpose(0, 3, 2, 1))
    for nm in ("Wk", "Wv"):
        a = ins[f"L1_attn_{nm}"]
        w[f"l1_{nm.lower()}"] = _c(a.reshape(4, 128, 16, 128).transpose(0, 3, 2, 1))
    w["l1_wv_n"] = _c(ins["L1_attn_Wv"].T.reshape(16, 128, 512))
    w["l1_wo"] = _c(ins["L1_attn_Wo"].T.reshape(16, 128, 2048))
    for nm, key in (("w1", "L1_ffn_W1"), ("w3", "L1_ffn_W3")):
        a = ins[key]
        w[f"l1_{nm}"] = _c(a.reshape(64, 128, 16, 128).transpose(0, 3, 2, 1))
    w["l1_w2t"] = _c(ins["L1_ffn_W2"].T.reshape(64, 128, 2048))
    return w


def host_consts():
    c = {}
    c["idn_bf"] = np.eye(128, dtype=ml_dtypes.bfloat16)
    c["idn_f32"] = np.eye(128, dtype=np.float32)

    def had(n):
        h = np.array([[1.0]])
        while h.shape[0] < n:
            h = np.block([[h, h], [h, -h]])
        return h
    c["h128"] = had(128).astype(ml_dtypes.bfloat16)
    h64 = had(64)
    z = np.zeros((64, 64))
    c["h2x64"] = np.block([[h64, z], [z, h64]]).astype(ml_dtypes.bfloat16)
    return c


class Consts:
    def __init__(self, nc, ctx, p, T):
        sb = lambda name, shape, dt: ctx.enter_context(nc.sbuf_tensor(name, shape, dt))
        self.idn_bf = sb("c_idn_bf", [128, 128], BF16)
        self.idn_f32 = sb("c_idn_f32", [128, 128], F32)
        self.bcast1_f32 = sb("c_bcast1", [1, 128], F32)
        self.ones_row_bf = sb("c_ones_row_bf", [1, 128], BF16)
        self.red1_f32 = sb("c_red1_f", [128, 1], F32)
        self.red1_bf = sb("c_red1_b", [128, 1], BF16)
        self.vals = sb("c_vals", [128, 12], F32)
        ld = p.sem("c_ld")
        p.dma("sync", self.idn_bf[:, :], T["idn_bf"].ap()[:, :], incs=[(ld, 16)])
        p.dma("sync", self.idn_f32[:, :], T["idn_f32"].ap()[:, :], incs=[(ld, 16)])
        g = p.sem("c_set")
        for col, v in ((0, -1.0 / 2048.0), (1, 1.0 / 2048.0), (2, LN_EPS),
                       (3, RMS_EPS_L0), (4, RMS_EPS_L1), (5, FWHT_SCALE),
                       (6, 1.0 / 128.0), (7, 1e-24), (8, SM_SCALE_L1)):
            p.emit("vector", lambda e, col=col, v=v: e.memset(self.vals[:, col:col + 1], v),
                   incs=[(g, 1)])
        for t in (self.bcast1_f32, self.ones_row_bf, self.red1_f32, self.red1_bf):
            p.emit("vector", lambda e, t=t: e.memset(t[:, :], 1.0), incs=[(g, 1)])
        self.ld = ld
        self.ready = g

    def v(self, col):
        return self.vals[:, col:col + 1]

    def v1(self, col):
        return self.vals[0:1, col:col + 1]



def emit_ffn(p, C, layer, T, tag):
    nc = p.nc
    ctx = p.alloc_ctx
    sb = lambda name, shape, dt: ctx.enter_context(nc.sbuf_tensor(tag + name, shape, dt))

    x_in = T["x_in"].ap()
    xnT_d = T["xnT"].ap()
    xacc_d = T["x_acc"].ap()
    if layer == 0:
        wg_d, wu_d, wd_d = T["l0_wg"].ap(), T["l0_wu"].ap(), T["l0_wd"].ap()
    else:
        wg_d, wu_d, wd_d = T["l1_w1"].ap(), T["l1_w3"].ap(), T["l1_w2pp"].ap()

    xbuf = sb("xbuf", [128, 2048], F32)
    xsq = sb("xsq", [128, 2048], F32)
    st = sb("stats", [128, 8], F32)
    xn_bf = sb("xn_bf", [128, 2048], BF16)
    xnT_sb = sb("xnT_sb", [128, E_BLKS * CHUNK], BF16)
    hidT = sb("hidT", [128, O_TILES * CHUNK], BF16)
    wgl = [sb(f"wgl{b}", [128, 2048], BF16) for b in range(2)]
    wul = [sb(f"wul{b}", [128, 2048], BF16) for b in range(2)]
    wdp = [sb(f"wdp{b}", [128, 4096], BF16) for b in range(2)]
    silu_s = sb("silu_s", [128, CHUNK], F32)
    up_s = sb("up_s", [128, CHUNK], F32)
    xacc = [sb(f"xacc{i}", [128, 2048], F32) for i in range(4)]
    if layer == 0:
        sg_sb = sb("sg", [128, 64], F32)
        su_sb = sb("su", [128, 64], F32)
        sdb = sb("sdb", [128, 2048], F32)

    xld = p.sem(tag + "xld")
    ald = p.sem(tag + "ald")
    wld = p.sem(tag + "wld")
    ggu = [p.sem(tag + f"ggu{b}") for b in range(2)]
    gwd = [p.sem(tag + f"gwd{b}") for b in range(2)]
    lnv = p.sem(tag + "lnv")
    trp = p.sem(tag + "trp")
    xnv = p.sem(tag + "xnv")
    mmg = p.sem(tag + "mmg")
    ev = p.sem(tag + "ev")
    mmd = p.sem(tag + "mmd")
    dev = p.sem(tag + "dev")
    sto = p.sem(tag + "sto")

    if layer == 0:
        p.dma("sync", sg_sb[:, :], T["l0_sg"].ap()[:, :], incs=[(wld, 16)])
        p.dma("sync", su_sb[:, :], T["l0_su"].ap()[:, :], incs=[(wld, 16)])
        sdrow = sb("sdrow", [1, 2048], F32)
        p.dma("sync", sdrow[:, :], T["l0_sdown"].ap()[:, :], incs=[(wld, 16)])
        with nc.psum_tensor(tag + "pbc", [128, 512], F32) as pbc:
            for i in range(4):
                p.emit("tensor",
                       lambda e, i=i, pbc=pbc: e.matmul(pbc[:, :], C.bcast1_f32[:, :],
                                               sdrow[:, i * 512:(i + 1) * 512],
                                               start=True, stop=True),
                       waits=[(wld, None), (C.ready, None), (ev, None)], incs=[(mmg, 1)])
                p.emit("vector",
                       lambda e, i=i, pbc=pbc: e.tensor_copy(sdb[:, i * 512:(i + 1) * 512], pbc[:, :]),
                       waits=[(mmg, None)], incs=[(ev, 1)])

    gu_ev_done = {}
    gu_mm_done = {}
    dq_mm_done = {}
    for q in range(N_CHUNKS):
        tiles = [q * 4 + i for i in range(4)]
        with nc.psum_tensor(f"{tag}ptr{q}", [128, 512], BF16) as ptr, \
             nc.psum_tensor(f"{tag}pg{q}a", [128, 512], F32) as pg0, \
             nc.psum_tensor(f"{tag}pg{q}b", [128, 512], F32) as pg1, \
             nc.psum_tensor(f"{tag}pu{q}a", [128, 512], F32) as pu0, \
             nc.psum_tensor(f"{tag}pu{q}b", [128, 512], F32) as pu1:
            pg = [pg0, pg1]
            pu = [pu0, pu1]
            for i, t in enumerate(tiles):
                p.dma("sync", xbuf[:, :], x_in[t], waits=[(lnv, None)], incs=[(xld, 16)])
                p.emit("vector", lambda e: e.reduce_sum(st[:, 0:1], xbuf[:, :], axis=AX.X),
                       waits=[(xld, None)], incs=[(lnv, 1)])
                p.emit("scalar", lambda e: e.activation(xsq[:, :], xbuf[:, :], AF.Square),
                       waits=[(xld, None), (lnv, None)], incs=[(lnv, 1)])
                p.emit("vector", lambda e: e.reduce_sum(st[:, 1:2], xsq[:, :], axis=AX.X),
                       waits=[(lnv, None)], incs=[(lnv, 1)])
                p.emit("vector", lambda e: e.tensor_scalar(st[:, 2:3], st[:, 0:1],
                                                           C.v(0), None, ALU.mult),
                       waits=[(lnv, None)], incs=[(lnv, 1)])
                p.emit("vector", lambda e: e.tensor_tensor(st[:, 3:4], st[:, 2:3], st[:, 2:3],
                                                           ALU.mult),
                       waits=[(lnv, None)], incs=[(lnv, 1)])
                p.emit("vector", lambda e: e.scalar_tensor_tensor(st[:, 4:5], st[:, 1:2],
                                                                  C.v(1), st[:, 3:4],
                                                                  ALU.mult, ALU.subtract),
                       waits=[(lnv, None)], incs=[(lnv, 1)])
                p.emit("scalar", lambda e: e.activation(st[:, 5:6], st[:, 4:5], AF.Ln,
                                                        bias=C.v(2)),
                       waits=[(lnv, None)], incs=[(lnv, 1)])
                p.emit("scalar", lambda e: e.activation(st[:, 6:7], st[:, 5:6], AF.Exp,
                                                        scale=-0.5),
                       waits=[(lnv, None)], incs=[(lnv, 1)])
                p.emit("vector", lambda e: e.tensor_scalar(xn_bf[:, :], xbuf[:, :],
                                                           st[:, 2:3], st[:, 6:7],
                                                           ALU.add, ALU.mult),
                       waits=[(lnv, None), (trp, None)], incs=[(lnv, 1)])
                for gblk in range(4):
                    for j in range(4):
                        eb = gblk * 4 + j
                        w = [(lnv, None), (C.ld, None), (xnv, None), (dev, None),
                             (ev, None)] if j == 0 else []
                        p.emit("tensor",
                               lambda e, eb=eb, j=j, ptr=ptr: e.transpose(
                                   ptr[:, j * 128:(j + 1) * 128],
                                   xn_bf[:, eb * 128:(eb + 1) * 128],
                                   C.idn_bf[:, :]),
                               waits=w, incs=([(trp, 1)] if j == 3 else []))
                    dst = xnT_sb[:, :].rearrange("p (eb s) -> p eb s", eb=E_BLKS)
                    p.emit("vector",
                           lambda e, gblk=gblk, i=i, dst=dst, ptr=ptr: e.tensor_copy(
                               dst[:, gblk * 4:gblk * 4 + 4, i * 128:(i + 1) * 128],
                               ptr[:, :].rearrange("p (j s) -> p j s", j=4)),
                           waits=[(trp, None), (sto, None)], incs=[(xnv, 1)])
            for eb in range(E_BLKS):
                p.dma("sync", xnT_d[eb, :, q * CHUNK:(q + 1) * CHUNK],
                      xnT_sb[:, eb * CHUNK:(eb + 1) * CHUNK],
                      waits=[(xnv, None)], incs=[(sto, 16)])

            for ot in range(O_TILES):
                b = ot & 1
                gq = q * O_TILES + ot
                wg_w = [(mmg, gu_mm_done.get(gq - 2, 0))]
                p.dma("gpsimd", wgl[b][:, :], wg_d[ot].rearrange("p c j -> p (c j)"),
                      waits=wg_w, incs=[(ggu[b], 16)])
                p.dma("gpsimd", wul[b][:, :], wu_d[ot].rearrange("p c j -> p (c j)"),
                      incs=[(ggu[b], 16)])

                def mm_gate(e, pg=pg, b=b):
                    for c in range(E_BLKS):
                        inst = e.matmul(pg[b][:, :], wgl[b][:, c * 128:(c + 1) * 128],
                                        xnT_sb[:, c * CHUNK:(c + 1) * CHUNK],
                                        start=(c == 0), stop=(c == E_BLKS - 1))
                    return inst
                g_w = [(ggu[b], None), (xnv, None), (dev, None)]
                if gq >= 2:
                    g_w.append((ev, gu_ev_done[gq - 2]))
                else:
                    g_w.append((ev, None))
                p.emit("tensor", mm_gate, waits=g_w, incs=[(mmg, 1)])

                def mm_up(e, pu=pu, b=b):
                    for c in range(E_BLKS):
                        inst = e.matmul(pu[b][:, :], wul[b][:, c * 128:(c + 1) * 128],
                                        xnT_sb[:, c * CHUNK:(c + 1) * CHUNK],
                                        start=(c == 0), stop=(c == E_BLKS - 1))
                    return inst
                p.emit("tensor", mm_up, incs=[(mmg, 1)])

                if layer == 0:
                    p.emit("scalar",
                           lambda e, ot=ot, pg=pg, b=b: e.activation(
                               silu_s[:, :], pg[b][:, :], AF.Silu,
                               scale=sg_sb[:, ot:ot + 1]),
                           waits=[(mmg, None), (ev, None), (wld, None)], incs=[(ev, 1)])
                    p.emit("vector",
                           lambda e, ot=ot, pu=pu, b=b: e.tensor_scalar(
                               up_s[:, :], pu[b][:, :],
                               su_sb[:, ot:ot + 1], None, ALU.mult),
                           waits=[(mmg, None), (wld, None)], incs=[(ev, 1)])
                    p.emit("vector",
                           lambda e, ot=ot: e.tensor_tensor(
                               hidT[:, ot * CHUNK:(ot + 1) * CHUNK],
                               silu_s[:, :], up_s[:, :], ALU.mult),
                           waits=[(ev, None)], incs=[(ev, 1)])
                else:
                    p.emit("scalar",
                           lambda e, pg=pg, b=b: e.activation(silu_s[:, :], pg[b][:, :],
                                                              AF.Silu),
                           waits=[(mmg, None), (ev, None)], incs=[(ev, 1)])
                    p.emit("vector",
                           lambda e, ot=ot, pu=pu, b=b: e.tensor_tensor(
                               hidT[:, ot * CHUNK:(ot + 1) * CHUNK],
                               silu_s[:, :], pu[b][:, :], ALU.mult),
                           waits=[(mmg, None), (ev, None)], incs=[(ev, 1)])
                gu_ev_done[gq] = ev.n
                gu_mm_done[gq] = mmg.n

        for i, t in enumerate(tiles):
            p.dma("sync", xacc[i][:, :], x_in[t], waits=[(dev, None)], incs=[(ald, 16)])
        with ExitStack() as ps:
            pd = [ps.enter_context(nc.psum_tensor(f"{tag}pd{q}_{i}", [128, 512], F32))
                  for i in range(8)]
            for dh in range(2):
                for qt in range(O_TILES // 4):
                    b = qt & 1
                    gqt = (q * 2 + dh) * (O_TILES // 4) + qt
                    d_w = [(mmd, dq_mm_done.get(gqt - 2, 0))]
                    p.dma("gpsimd", wdp[b][:, :].rearrange("p (t d) -> p t d", t=4),
                          wd_d[4 * qt:4 * qt + 4, :, dh * 1024:(dh + 1) * 1024]
                          .rearrange("t p d -> p t d"),
                          waits=d_w, incs=[(gwd[b], 16)])
                    for j in range(4):
                        tt = 4 * qt + j

                        def mm_down(e, tt=tt, j=j, b=b, pd=pd):
                            for i in range(4):
                                for dc in range(2):
                                    inst = e.matmul(
                                        pd[i * 2 + dc][:, :],
                                        hidT[:, tt * CHUNK + i * 128: tt * CHUNK + (i + 1) * 128],
                                        wdp[b][:, j * 1024 + dc * 512: j * 1024 + (dc + 1) * 512],
                                        start=(tt == 0), stop=(tt == O_TILES - 1))
                            return inst
                        mw = [(gwd[b], None)]
                        if tt == 0:
                            mw += [(ev, None), (dev, None)]
                        p.emit("tensor", mm_down, waits=mw, incs=[(mmd, 1)])
                    dq_mm_done[gqt] = mmd.n
                for i in range(4):
                    for dc in range(2):
                        col = dh * 1024 + dc * 512
                        if layer == 0:
                            p.emit("vector",
                                   lambda e, i=i, dc=dc, col=col, pd=pd: e.tensor_tensor(
                                       up_s[:, :], pd[i * 2 + dc][:, :],
                                       sdb[:, col:col + 512], ALU.mult),
                                   waits=[(mmd, None), (ald, None), (dev, None)],
                                   incs=[(dev, 1)])
                            p.emit("vector",
                                   lambda e, i=i, col=col: e.tensor_tensor(
                                       xacc[i][:, col:col + 512], xacc[i][:, col:col + 512],
                                       up_s[:, :], ALU.add),
                                   waits=[(dev, None)], incs=[(dev, 1)])
                        else:
                            p.emit("vector",
                                   lambda e, i=i, dc=dc, col=col, pd=pd: e.tensor_tensor(
                                       xacc[i][:, col:col + 512], xacc[i][:, col:col + 512],
                                       pd[i * 2 + dc][:, :], ALU.add),
                                   waits=[(mmd, None), (ald, None)], incs=[(dev, 1)])
        for i, t in enumerate(tiles):
            p.dma("sync", xacc_d[t], xacc[i][:, :], waits=[(dev, None)], incs=[(sto, 16)])

    return sto



def emit_w2pp(p, C, T, tag="w2_"):
    nc = p.nc
    ctx = p.alloc_ctx
    sb = lambda name, shape, dt: ctx.enter_context(nc.sbuf_tensor(tag + name, shape, dt))
    w2t = T["l1_w2t"].ap()
    w2a = T["l1_w2a"].ap()
    w2pp_flat = T["l1_w2pp"].ap().rearrange("t p d -> (t p) d")

    h128 = sb("h128", [128, 128], BF16)
    h2x64 = sb("h2x64", [128, 128], BF16)
    win = [sb(f"win{b}", [128, 2048], BF16) for b in range(2)]
    wout = [sb(f"wout{b}", [128, 2048], BF16) for b in range(2)]
    ld = p.sem(tag + "ld")
    gld = [p.sem(tag + f"gld{b}") for b in range(2)]
    mm = p.sem(tag + "mm")
    evs = p.sem(tag + "ev")
    sto = p.sem(tag + "sto")
    p.dma("sync", h128[:, :], T["h128"].ap()[:, :], incs=[(ld, 16)])
    p.dma("sync", h2x64[:, :], T["h2x64"].ap()[:, :], incs=[(ld, 16)])

    w2a_v = w2a.rearrange("(a b tt) d -> a b tt d", a=64, b=2)
    w2a_g = w2a.rearrange("(g q) d -> g q d", g=64)
    w2pp_v = w2pp_flat.rearrange("(tau gb) d -> tau gb d", gb=128)

    mm_done = {}
    sto_done = {}
    with nc.psum_tensor(tag + "psA", [128, 512], F32) as ps0, \
         nc.psum_tensor(tag + "psB", [128, 512], F32) as ps1:
        pss = [ps0, ps1]
        for t in range(128):
            b = t & 1
            is2 = t >= 64
            src = w2a_g[t - 64] if is2 else w2t[t]
            dw = [(mm, mm_done.get(t - 2, 0))]
            if is2:
                dw.append((sto, sto_done[63]))
            p.dma("gpsimd", win[b][:, :], src, waits=dw, incs=[(gld[b], 16)])
            hmat = h2x64 if is2 else h128
            for dchunk in range(4):
                ps = pss[dchunk & 1]

                def mmw2(e, dchunk=dchunk, b=b, hmat=hmat, ps=ps):
                    return e.matmul(ps[:, :], hmat[:, :],
                                    win[b][:, dchunk * 512:(dchunk + 1) * 512],
                                    start=True, stop=True)
                p.emit("tensor", mmw2,
                       waits=[(ld, None), (gld[b], None), (evs, None)], incs=[(mm, 1)])
                if is2:
                    p.emit("vector",
                           lambda e, dchunk=dchunk, b=b, ps=ps: e.tensor_scalar(
                               wout[b][:, dchunk * 512:(dchunk + 1) * 512], ps[:, :],
                               C.v(5), None, ALU.mult),
                           waits=[(mm, None), (C.ready, None), (sto, sto_done.get(t - 2, 0))],
                           incs=[(evs, 1)])
                else:
                    p.emit("vector",
                           lambda e, dchunk=dchunk, b=b, ps=ps: e.tensor_copy(
                               wout[b][:, dchunk * 512:(dchunk + 1) * 512], ps[:, :]),
                           waits=[(mm, None), (sto, sto_done.get(t - 2, 0))],
                           incs=[(evs, 1)])
            mm_done[t] = mm.n
            if is2:
                g = t - 64
                for beta in range(2):
                    p.dma("sync", w2pp_v[:, g * 2 + beta, :],
                          wout[b][beta * 64:(beta + 1) * 64, :],
                          waits=[(evs, None)], incs=[(sto, 16)])
            else:
                p.dma("sync", w2a_v[:, :, t, :].rearrange("a b d -> (a b) d"), wout[b][:, :],
                      waits=[(evs, None)], incs=[(sto, 16)])
            sto_done[t] = sto.n
    return sto



def emit_attn(p, C, layer, T, tag, dbg=False):
    nc = p.nc
    ctx = p.alloc_ctx
    sb = lambda name, shape, dt: ctx.enter_context(nc.sbuf_tensor(tag + name, shape, dt))

    xnT_d = T["xnT"].ap()
    xbase_d = T["x_base"].ap()
    xout_d = T["x_out"].ap()
    aoT_d = T["aoT"].ap()
    NH = 16

    xnT_sb = sb("xnT", [128, E_BLKS * 2048], BF16)
    wql = [sb(f"wql{b}", [128, 2048], BF16) for b in range(2)]
    nkv = 2 if layer == 0 else 1
    wkl = [sb(f"wkl{b}", [128, 2048], BF16) for b in range(nkv)]
    wvl = [sb(f"wvl{b}", [128, 2048], BF16) for b in range(nkv)]
    fT = sb("fT", [128, 2048], F32)
    fsq = sb("fsq", [128, 2048], F32)
    rstd2 = sb("rstd2", [128, 2048], F32)
    lnrow = sb("lnrow", [1, 2048], F32)
    qhatT = sb("qhatT", [128, 2048], BF16)
    eT = sb("eT", [128, 16 * 512], BF16)
    cols = sb("cols", [128, 34], F32)
    zrow = sb("zrow", [1, 2048], F32)
    zb = sb("zb", [128, 512], F32)
    ao_p = sb("ao_p", [128, 512], BF16)
    xbh = fsq[:, 0:1024]
    xoh = fsq[:, 1024:2048]
    wop = [wql[b][:, 0:1024] for b in range(2)]
    aop_sb = [wql[b][:, 1024:1536] for b in range(2)]
    if layer == 0:
        e_nat = sb("e_nat", [128, 2048], BF16)
        mrow = sb("mrow", [1, 2048], BF16)
        khatT = [sb("khatT", [128, 2048], BF16)]
        v_nat = [sb("v", [128, 2048], BF16)]
        qs_c = sb("qs_c", [128, 1], F32)
        ks_c = sb("ks_c", [128, 1], F32)
        s2c = sb("s2c", [128, 1], F32)
    else:
        khatT = [sb(f"khatT{g}", [128, 2048], BF16) for g in range(4)]
        vT_bf = [sb(f"vT{g}", [128, 2048], BF16) for g in range(4)]
        v_nat = sb("v", [128, 16 * 512], BF16)
        ut_f = sb("ut_f", [128, 512], F32)
        uv_f = sb("uv_f", [128, 512], F32)
        g2b = sb("g2b", [128, 512], F32)
        g2row = sb("g2row", [1, 2048], F32)

    ld = p.sem(tag + "ld")
    aold = p.sem(tag + "aold")
    xbld = p.sem(tag + "xbld")
    gld = [p.sem(tag + f"gld{b}") for b in range(2)]
    gvn = p.sem(tag + "gvn")
    gkv = p.sem(tag + "gkv")
    glw = [p.sem(tag + f"glw{b}") for b in range(2)]
    aold2 = [p.sem(tag + f"aold{b}") for b in range(2)]
    mm = p.sem(tag + "mm")
    se = p.sem(tag + "se")
    ve = p.sem(tag + "ve")
    sto = p.sem(tag + "sto")

    for eb in range(E_BLKS):
        p.dma("sync", xnT_sb[:, eb * 2048:(eb + 1) * 2048], xnT_d[eb], incs=[(ld, 16)])
    if layer == 0:
        p.dma("sync", qs_c[:, :], T["l0_qscale"].ap()[:, :], incs=[(ld, 16)])
        p.dma("sync", ks_c[:, :], T["l0_kscale"].ap()[:, :], incs=[(ld, 16)])
        p.emit("vector", lambda e: e.tensor_tensor(s2c[:, :], qs_c[:, :], ks_c[:, :],
                                                   ALU.mult),
               waits=[(ld, None)], incs=[(ve, 1)])

    def proj_T(wsb, pks, wsem):
        se_after = {}
        for sc in range(4):
            pq = pks[sc & 1]

            def mmf(e, sc=sc, pq=pq):
                for c in range(E_BLKS):
                    inst = e.matmul(pq[:, :], wsb[:, c * 128:(c + 1) * 128],
                                    xnT_sb[:, c * 2048 + sc * 512: c * 2048 + (sc + 1) * 512],
                                    start=(c == 0), stop=(c == E_BLKS - 1))
                return inst
            w = [(ld, None), (wsem, None)]
            if sc < 2:
                w += [(se, None), (ve, None)]
            else:
                w.append((se, se_after[sc - 2]))
            p.emit("tensor", mmf, waits=w, incs=[(mm, 1)])
            p.emit("scalar",
                   lambda e, sc=sc, pq=pq: e.activation(fT[:, sc * 512:(sc + 1) * 512],
                                                        pq[:, :], AF.Identity),
                   waits=[(mm, None), (ve, None)], incs=[(se, 1)])
            se_after[sc] = se.n

    def rms_rows(prow, pb, eps_col):
        p.emit("vector", lambda e: e.tensor_tensor(fsq[:, :], fT[:, :], fT[:, :], ALU.mult),
               waits=[(se, None), (mm, None)], incs=[(ve, 1)])
        for sc in range(4):
            p.emit("tensor",
                   lambda e, sc=sc: e.matmul(prow[:, :], C.red1_f32[:, :],
                                             fsq[:, sc * 512:(sc + 1) * 512],
                                             start=True, stop=True),
                   waits=[(ve, None), (se, None)], incs=[(mm, 1)])
            p.emit("scalar",
                   lambda e, sc=sc: e.activation(lnrow[0:1, sc * 512:(sc + 1) * 512],
                                                 prow[0:1, :], AF.Ln,
                                                 bias=eps_col, scale=C.v1(6)),
                   waits=[(mm, None)], incs=[(se, 1)])
        for sc in range(4):
            p.emit("tensor",
                   lambda e, sc=sc: e.matmul(pb[:, :], C.bcast1_f32[:, :],
                                             lnrow[0:1, sc * 512:(sc + 1) * 512],
                                             start=True, stop=True),
                   waits=[(se, None), (ve, None)], incs=[(mm, 1)])
            p.emit("scalar",
                   lambda e, sc=sc: e.activation(rstd2[:, sc * 512:(sc + 1) * 512],
                                                 pb[:, :], AF.Exp, scale=-0.5),
                   waits=[(mm, None), (ve, None), (se, None)], incs=[(se, 1)])

    if layer == 1:
        wvn = eT
        for c in range(E_BLKS):
            p.dma("gpsimd", wvn[:, c * 512:(c + 1) * 512], T["l1_wv_n"].ap()[c],
                  incs=[(gvn, 16)])
        with nc.psum_tensor(tag + "pvn", [128, 512], F32) as pv:
            for t in range(S_TILES):
                def mmv(e, t=t, pv=pv):
                    for c in range(E_BLKS):
                        inst = e.matmul(pv[:, :],
                                        xnT_sb[:, c * 2048 + t * 128: c * 2048 + (t + 1) * 128],
                                        wvn[:, c * 512:(c + 1) * 512],
                                        start=(c == 0), stop=(c == E_BLKS - 1))
                    return inst
                p.emit("tensor", mmv, waits=[(ld, None), (gvn, None), (ve, None)],
                       incs=[(mm, 1)])
                p.emit("vector",
                       lambda e, t=t, pv=pv: e.tensor_copy(v_nat[:, t * 512:(t + 1) * 512], pv[:, :]),
                       waits=[(mm, None)], incs=[(ve, 1)])
        for g in range(4):
            with nc.psum_tensor(f"{tag}kk{g}", [128, 512], F32) as pk0, \
                 nc.psum_tensor(f"{tag}kk{g}x", [128, 512], F32) as pk1, \
                 nc.psum_tensor(f"{tag}kr{g}", [1, 512], F32) as prow, \
                 nc.psum_tensor(f"{tag}kb{g}", [128, 512], F32) as pb:
                pk = [pk0, pk1]
                p.dma("gpsimd", wkl[0][:, :], T["l1_wk"].ap()[g].rearrange("p c j -> p (c j)"),
                      waits=[(mm, None)], incs=[(gkv, 16)])
                proj_T(wkl[0], pk, gkv)
                rms_rows(prow, pb, C.v1(4))
                p.emit("vector",
                       lambda e, g=g: e.scalar_tensor_tensor(khatT[g][:, :], fT[:, :],
                                                             C.v(8), rstd2[:, :],
                                                             ALU.mult, ALU.mult),
                       waits=[(se, None)], incs=[(ve, 1)])
                p.dma("gpsimd", wvl[0][:, :], T["l1_wv"].ap()[g].rearrange("p c j -> p (c j)"),
                      waits=[(ve, None)], incs=[(gkv, 16)])
                proj_T(wvl[0], pk, gkv)
                p.emit("vector", lambda e: e.tensor_tensor(fsq[:, :], fT[:, :], fT[:, :],
                                                           ALU.mult),
                       waits=[(se, None)], incs=[(ve, 1)])
                for sc in range(4):
                    p.emit("tensor",
                           lambda e, sc=sc, prow=prow: e.matmul(prow[:, :], C.red1_f32[:, :],
                                                     fsq[:, sc * 512:(sc + 1) * 512],
                                                     start=True, stop=True),
                           waits=[(ve, None), (se, None)], incs=[(mm, 1)])
                    p.emit("vector",
                           lambda e, sc=sc, prow=prow: e.tensor_scalar(
                               lnrow[0:1, sc * 512:(sc + 1) * 512], prow[0:1, :],
                               C.v1(7), None, ALU.max),
                           waits=[(mm, None)], incs=[(ve, 1)])
                    p.emit("scalar",
                           lambda e, sc=sc: e.activation(
                               lnrow[0:1, sc * 512:(sc + 1) * 512],
                               lnrow[0:1, sc * 512:(sc + 1) * 512], AF.Ln),
                           waits=[(ve, None)], incs=[(se, 1)])
                    p.emit("tensor",
                           lambda e, sc=sc, pb=pb: e.matmul(pb[:, :], C.bcast1_f32[:, :],
                                                     lnrow[0:1, sc * 512:(sc + 1) * 512],
                                                     start=True, stop=True),
                           waits=[(se, None)], incs=[(mm, 1)])
                    p.emit("scalar",
                           lambda e, sc=sc, pb=pb: e.activation(rstd2[:, sc * 512:(sc + 1) * 512],
                                                         pb[:, :], AF.Exp, scale=-0.5),
                           waits=[(mm, None), (ve, None)], incs=[(se, 1)])
                    p.emit("vector",
                           lambda e, sc=sc, g=g: e.tensor_tensor(
                               vT_bf[g][:, sc * 512:(sc + 1) * 512],
                               fT[:, sc * 512:(sc + 1) * 512],
                               rstd2[:, sc * 512:(sc + 1) * 512], ALU.mult),
                           waits=[(se, None)], incs=[(ve, 1)])

    head_done = {}
    for h in range(NH):
        hb = h & 1
        with nc.psum_tensor(f"{tag}hk{h}", [128, 512], F32) as pk0, \
             nc.psum_tensor(f"{tag}hk{h}x", [128, 512], F32) as pk1, \
             nc.psum_tensor(f"{tag}hr{h}", [1, 512], F32) as prow, \
             nc.psum_tensor(f"{tag}hb{h}", [128, 512], F32) as pb, \
             nc.psum_tensor(f"{tag}hv{h}", [128, 128], F32) as pv:
            pk = [pk0, pk1]
            if layer == 0:
                hw = [(mm, head_done.get(h - 2, 0))]
                p.dma("gpsimd", wql[hb][:, :], T["l0_wq"].ap()[h].rearrange("p c j -> p (c j)"),
                      waits=hw, incs=[(gld[hb], 16)])
                p.dma("gpsimd", wkl[hb][:, :], T["l0_wk"].ap()[h].rearrange("p c j -> p (c j)"),
                      incs=[(gld[hb], 16)])
                p.dma("gpsimd", wvl[hb][:, :], T["l0_wv"].ap()[h].rearrange("p c j -> p (c j)"),
                      incs=[(gld[hb], 16)])
                proj_T(wkl[hb], pk, gld[hb])
                rms_rows(prow, pb, C.v1(3))
                p.emit("vector",
                       lambda e: e.scalar_tensor_tensor(khatT[0][:, :], fT[:, :],
                                                        s2c[:, :], rstd2[:, :],
                                                        ALU.mult, ALU.mult),
                       waits=[(se, None), (ve, None)], incs=[(ve, 1)])
                for t in range(S_TILES):
                    def mmv(e, t=t, pv=pv, hb=hb):
                        for c in range(E_BLKS):
                            inst = e.matmul(pv[:, :],
                                            xnT_sb[:, c * 2048 + t * 128: c * 2048 + (t + 1) * 128],
                                            wvl[hb][:, c * 128:(c + 1) * 128],
                                            start=(c == 0), stop=(c == E_BLKS - 1))
                        return inst
                    p.emit("tensor", mmv, waits=[(ld, None), (gld[hb], None), (ve, None)],
                           incs=[(mm, 1)])
                    p.emit("vector",
                           lambda e, t=t, pv=pv: e.tensor_copy(v_nat[0][:, t * 128:(t + 1) * 128],
                                                        pv[:, :]),
                           waits=[(mm, None)], incs=[(ve, 1)])
                proj_T(wql[hb], pk, gld[hb])
                rms_rows(prow, pb, C.v1(3))
                p.emit("vector",
                       lambda e: e.tensor_tensor(qhatT[:, :], fT[:, :], rstd2[:, :], ALU.mult),
                       waits=[(se, None)], incs=[(ve, 1)])
                kh = khatT[0]
                head_done[h] = mm.n
            else:
                g = h // 4
                hw = [(mm, head_done.get(h - 2, 0) if h >= 2 else 0)]
                p.dma("gpsimd", wql[hb][:, :], T["l1_wq"].ap()[h].rearrange("p c j -> p (c j)"),
                      waits=hw, incs=[(gld[hb], 16)])
                proj_T(wql[hb], pk, gld[hb])
                rms_rows(prow, pb, C.v1(4))
                p.emit("vector",
                       lambda e: e.tensor_tensor(qhatT[:, :], fT[:, :], rstd2[:, :], ALU.mult),
                       waits=[(se, None)], incs=[(ve, 1)])
                kh = khatT[g]
                head_done[h] = mm.n

        if layer == 0:
            with nc.psum_tensor(f"{tag}sn{h}a", [128, 2048], F32) as psn0, \
                 nc.psum_tensor(f"{tag}sn{h}b", [128, 2048], F32) as psn1:
                psns = [psn0, psn1]
                amax_done = {}
                aexp_done = {}
                for t in range(S_TILES):
                    psn = psns[t & 1]

                    def mmsn(e, t=t, psn=psn):
                        for kc in range(4):
                            inst = e.matmul(psn[:, kc * 512:(kc + 1) * 512],
                                            qhatT[:, t * 128:(t + 1) * 128],
                                            kh[:, kc * 512:(kc + 1) * 512],
                                            start=True, stop=True)
                        return inst
                    if t < 2:
                        aw = [(ve, None), (se, None)]
                    else:
                        aw = [(ve, amax_done[t - 2]), (se, aexp_done[t - 2])]
                    p.emit("tensor", mmsn, waits=aw, incs=[(mm, 1)])
                    p.emit("vector",
                           lambda e, t=t, psn=psn: e.reduce_max(cols[:, t:t + 1], psn[:, :],
                                                       axis=AX.X, negate=True),
                           waits=[(mm, None)], incs=[(ve, 1)])
                    amax_done[t] = ve.n
                    p.emit("scalar",
                           lambda e, t=t, psn=psn: e.activation(e_nat[:, :], psn[:, :], AF.Exp,
                                                       bias=cols[:, t:t + 1],
                                                       accum_out=cols[:, 16 + t:17 + t]),
                           waits=[(ve, amax_done[t])], incs=[(se, 1)])
                    aexp_done[t] = se.n
            with nc.psum_tensor(f"{tag}mr{h}", [1, 128], F32) as pmr:
                for t in range(S_TILES):
                    p.emit("tensor",
                           lambda e, t=t, pmr=pmr: e.transpose(pmr[:, :], cols[:, t:t + 1],
                                                      C.idn_f32[:, :]),
                           waits=[(ve, None), (se, None), (C.ld, None)], incs=[(mm, 1)])
                    p.emit("vector",
                           lambda e, t=t, pmr=pmr: e.tensor_copy(
                               mrow[0:1, t * 128:(t + 1) * 128], pmr[0:1, :]),
                           waits=[(mm, None)], incs=[(ve, 1)])
                p.emit("vector",
                       lambda e: e.reciprocal(cols[:, 16:32], cols[:, 16:32]),
                       waits=[(se, None)], incs=[(ve, 1)])
                for t in range(S_TILES):
                    p.emit("tensor",
                           lambda e, t=t, pmr=pmr: e.transpose(pmr[:, :], cols[:, 16 + t:17 + t],
                                                      C.idn_f32[:, :]),
                           waits=[(ve, None), (C.ld, None)], incs=[(mm, 1)])
                    p.emit("vector",
                           lambda e, t=t, pmr=pmr: e.tensor_copy(
                               zrow[0:1, t * 128:(t + 1) * 128], pmr[0:1, :]),
                           waits=[(mm, None)], incs=[(ve, 1)])
        with nc.psum_tensor(f"{tag}sT{h}a", [128, 512], F32) as pT0, \
             nc.psum_tensor(f"{tag}sT{h}b", [128, 512], F32) as pT1, \
             nc.psum_tensor(f"{tag}av{h}", [128, 512], F32) as pav, \
             nc.psum_tensor(f"{tag}zz{h}", [1, 512], F32) as pz, \
             nc.psum_tensor(f"{tag}dt{h}", [1, 512], F32) as pdot, \
             nc.psum_tensor(f"{tag}zb{h}", [128, 512], F32) as pzb:
            pTs = [pT0, pT1]
            cexp_done = {}
            for qc in range(4):
                for kt in range(S_TILES):
                    ck = qc * S_TILES + kt
                    pT = pTs[ck & 1]

                    def mmsT(e, kt=kt, qc=qc, pT=pT, kh=kh):
                        if layer == 0:
                            e.matmul(pT[:, :], kh[:, kt * 128:(kt + 1) * 128],
                                     qhatT[:, qc * 512:(qc + 1) * 512],
                                     start=True, stop=False)
                            return e.matmul(pT[:, :], C.ones_row_bf[:, :],
                                            mrow[0:1, qc * 512:(qc + 1) * 512],
                                            start=False, stop=True)
                        return e.matmul(pT[:, :], kh[:, kt * 128:(kt + 1) * 128],
                                        qhatT[:, qc * 512:(qc + 1) * 512],
                                        start=True, stop=True)
                    cw = [(se, cexp_done[ck - 2])] if ck >= 2 else [(se, None)]
                    if ck <= 1:
                        cw.append((ve, None))
                    p.emit("tensor", mmsT, waits=cw, incs=[(mm, 1)])
                    p.emit("scalar",
                           lambda e, kt=kt, pT=pT: e.activation(eT[:, kt * 512:(kt + 1) * 512],
                                                         pT[:, :], AF.Exp),
                           waits=[(mm, None)], incs=[(se, 1)])
                    cexp_done[ck] = se.n

                def mmav(e, qc=qc, h=h, pav=pav):
                    for kt in range(S_TILES):
                        lhs = (v_nat[0][:, kt * 128:(kt + 1) * 128] if layer == 0 else
                               v_nat[:, kt * 512 + (h // 4) * 128:
                                     kt * 512 + (h // 4) * 128 + 128])
                        inst = e.matmul(pav[:, :], lhs, eT[:, kt * 512:(kt + 1) * 512],
                                        start=(kt == 0), stop=(kt == S_TILES - 1))
                    return inst
                p.emit("tensor", mmav, waits=[(se, None), (ve, None)], incs=[(mm, 1)])

                if layer == 0:
                    p.emit("tensor",
                           lambda e, qc=qc, pzb=pzb: e.matmul(pzb[:, :], C.bcast1_f32[:, :],
                                                     zrow[0:1, qc * 512:(qc + 1) * 512],
                                                     start=True, stop=True),
                           waits=[(ve, None)], incs=[(mm, 1)])
                    p.emit("vector",
                           lambda e, pzb=pzb: e.tensor_copy(zb[:, :], pzb[:, :]),
                           waits=[(mm, None)], incs=[(ve, 1)])
                    p.emit("vector",
                           lambda e, pav=pav: e.tensor_tensor(ao_p[:, :], pav[:, :], zb[:, :],
                                                     ALU.mult),
                           waits=[(sto, None), (ve, None)], incs=[(ve, 1)])
                else:
                    g = h // 4

                    def mmz(e, pz=pz):
                        for kt in range(S_TILES):
                            inst = e.matmul(pz[:, :], C.red1_bf[:, :],
                                            eT[:, kt * 512:(kt + 1) * 512],
                                            start=(kt == 0), stop=(kt == S_TILES - 1))
                        return inst
                    p.emit("tensor", mmz, waits=[(ve, None)], incs=[(mm, 1)])
                    p.emit("vector",
                           lambda e, g=g, qc=qc, pav=pav: e.tensor_tensor(
                               uv_f[:, :], pav[:, :],
                               vT_bf[g][:, qc * 512:(qc + 1) * 512], ALU.mult),
                           waits=[(mm, None)], incs=[(ve, 1)])
                    p.emit("scalar",
                           lambda e, pav=pav: e.activation(ut_f[:, :], pav[:, :], AF.Identity),
                           waits=[(mm, None), (ve, None)], incs=[(se, 1)])
                    p.emit("tensor",
                           lambda e, pdot=pdot: e.matmul(pdot[:, :], C.red1_f32[:, :], uv_f[:, :],
                                              start=True, stop=True),
                           waits=[(ve, None)], incs=[(mm, 1)])
                    p.emit("vector",
                           lambda e, qc=qc, pz=pz: e.reciprocal(
                               zrow[0:1, qc * 512:(qc + 1) * 512], pz[0:1, :]),
                           waits=[(mm, None)], incs=[(ve, 1)])
                    p.emit("vector",
                           lambda e, qc=qc, pdot=pdot: e.tensor_tensor(
                               g2row[0:1, qc * 512:(qc + 1) * 512], pdot[0:1, :],
                               zrow[0:1, qc * 512:(qc + 1) * 512], ALU.mult),
                           waits=[(mm, None), (ve, None)], incs=[(ve, 1)])
                    p.emit("tensor",
                           lambda e, qc=qc, pzb=pzb: e.matmul(pzb[:, :], C.bcast1_f32[:, :],
                                                     zrow[0:1, qc * 512:(qc + 1) * 512],
                                                     start=True, stop=True),
                           waits=[(ve, None)], incs=[(mm, 1)])
                    p.emit("vector", lambda e, pzb=pzb: e.tensor_copy(zb[:, :], pzb[:, :]),
                           waits=[(mm, None)], incs=[(ve, 1)])
                    p.emit("tensor",
                           lambda e, qc=qc, pzb=pzb: e.matmul(pzb[:, :], C.bcast1_f32[:, :],
                                                     g2row[0:1, qc * 512:(qc + 1) * 512],
                                                     start=True, stop=True),
                           waits=[(ve, None)], incs=[(mm, 1)])
                    p.emit("vector", lambda e, pzb=pzb: e.tensor_copy(g2b[:, :], pzb[:, :]),
                           waits=[(mm, None)], incs=[(ve, 1)])
                    p.emit("vector",
                           lambda e: e.tensor_tensor(ut_f[:, :], ut_f[:, :], zb[:, :],
                                                     ALU.mult),
                           waits=[(se, None), (ve, None)], incs=[(ve, 1)])
                    p.emit("vector",
                           lambda e, g=g, qc=qc: e.tensor_tensor(
                               uv_f[:, :], g2b[:, :],
                               vT_bf[g][:, qc * 512:(qc + 1) * 512], ALU.mult),
                           waits=[(ve, None)], incs=[(ve, 1)])
                    p.emit("vector",
                           lambda e: e.tensor_tensor(ao_p[:, :], ut_f[:, :], uv_f[:, :],
                                                     ALU.subtract),
                           waits=[(sto, None), (ve, None)], incs=[(ve, 1)])
                p.dma("sync", aoT_d[h, :, qc * 512:(qc + 1) * 512], ao_p[:, :],
                      waits=[(ve, None)], incs=[(sto, 16)])

    wo_d = T["l0_wo"].ap() if layer == 0 else T["l1_wo"].ap()
    wo_done = {}
    for tg in range(4):
        for eh in range(2):
            with ExitStack() as ps:
                pw = [ps.enter_context(nc.psum_tensor(f"{tag}pw{tg}{eh}_{i}", [128, 512], F32))
                      for i in range(8)]
                for c in range(16):
                    gc = (tg * 2 + eh) * 16 + c
                    wb = gc & 1
                    ww = [(mm, wo_done[gc - 2])] if gc >= 2 else [(mm, None)]
                    p.dma("gpsimd", wop[wb], wo_d[c, :, eh * 1024:(eh + 1) * 1024],
                          waits=ww, incs=[(glw[wb], 16)])
                    p.dma("sync", aop_sb[wb], aoT_d[c, :, tg * 512:(tg + 1) * 512],
                          waits=list(ww), incs=[(aold2[wb], 16)])

                    def mmw(e, c=c, pw=pw, wb=wb):
                        for i in range(4):
                            for ec in range(2):
                                inst = e.matmul(pw[i * 2 + ec][:, :],
                                                aop_sb[wb][:, i * 128:(i + 1) * 128],
                                                wop[wb][:, ec * 512:(ec + 1) * 512],
                                                start=(c == 0), stop=(c == 15))
                        return inst
                    p.emit("tensor", mmw,
                           waits=[(aold2[wb], None), (glw[wb], None)] + ([(ve, None)] if c == 0 else []),
                           incs=[(mm, 1)])
                    wo_done[gc] = mm.n
                for i in range(4):
                    t = tg * 4 + i
                    p.dma("sync", xbh, xbase_d[t, :, eh * 1024:(eh + 1) * 1024],
                          waits=[(ve, None)], incs=[(xbld, 16)])
                    for ec in range(2):
                        p.emit("vector",
                               lambda e, i=i, ec=ec, pw=pw: e.tensor_tensor(
                                   fsq[:, 1024 + ec * 512:1024 + (ec + 1) * 512],
                                   fsq[:, ec * 512:(ec + 1) * 512],
                                   pw[i * 2 + ec][:, :], ALU.add),
                               waits=[(mm, None), (xbld, None), (sto, None)], incs=[(ve, 1)])
                    p.dma("sync", xout_d[t, :, eh * 1024:(eh + 1) * 1024], xoh,
                          waits=[(ve, None)], incs=[(sto, 16)])
    return sto


def _mk_dram(nc, specs):
    T = {}
    for name, shape, dt, kind in specs:
        T[name] = nc.dram_tensor(name, list(shape), dt, kind=kind)
    return T


_FFN_L0_W = [
    ("l0_wg", (64, 128, 16, 128), I8, "ExternalInput"),
    ("l0_wu", (64, 128, 16, 128), I8, "ExternalInput"),
    ("l0_wd", (64, 128, 2048), I8, "ExternalInput"),
    ("l0_sg", (128, 64), F32, "ExternalInput"),
    ("l0_su", (128, 64), F32, "ExternalInput"),
    ("l0_sdown", (1, 2048), F32, "ExternalInput"),
]
_FFN_L1_W = [
    ("l1_w1", (64, 128, 16, 128), F32, "ExternalInput"),
    ("l1_w3", (64, 128, 16, 128), F32, "ExternalInput"),
    ("l1_w2t", (64, 128, 2048), F32, "ExternalInput"),
    ("h128", (128, 128), BF16, "ExternalInput"),
    ("h2x64", (128, 128), BF16, "ExternalInput"),
    ("l1_w2a", (8192, 2048), BF16, "Internal"),
    ("l1_w2pp", (64, 128, 2048), BF16, "Internal"),
]


_ATTN_L0_W = [
    ("l0_wq", (16, 128, 16, 128), F32, "ExternalInput"),
    ("l0_wk", (16, 128, 16, 128), F32, "ExternalInput"),
    ("l0_wv", (16, 128, 16, 128), F32, "ExternalInput"),
    ("l0_wo", (16, 128, 2048), F32, "ExternalInput"),
    ("l0_qscale", (128, 1), F32, "ExternalInput"),
    ("l0_kscale", (128, 1), F32, "ExternalInput"),
]
_ATTN_L1_W = [
    ("l1_wq", (16, 128, 16, 128), F32, "ExternalInput"),
    ("l1_wk", (4, 128, 16, 128), F32, "ExternalInput"),
    ("l1_wv", (4, 128, 16, 128), F32, "ExternalInput"),
    ("l1_wv_n", (16, 128, 512), F32, "ExternalInput"),
    ("l1_wo", (16, 128, 2048), F32, "ExternalInput"),
]


def build_attn_nc(layer, dbg=False):
    nc = bass.Bass("TRN2", target_bir_lowering=False)
    specs = [
        ("xnT", (16, 128, 2048), BF16, "ExternalInput"),
        ("x_base", (16, 128, 2048), F32, "ExternalInput"),
        ("idn_bf", (128, 128), BF16, "ExternalInput"),
        ("idn_f32", (128, 128), F32, "ExternalInput"),
        ("x_out", (16, 128, 2048), F32, "ExternalOutput"),
        ("aoT", (16, 128, 2048), BF16, "ExternalOutput" if dbg else "Internal"),
    ]
    if dbg:
        specs += [
            ("d_qhatT", (128, 2048), BF16, "ExternalOutput"),
            ("d_fT", (128, 2048), F32, "ExternalOutput"),
            ("d_wql", (128, 2048), BF16, "ExternalOutput"),
            ("d_xnT0", (128, 2048), BF16, "ExternalOutput"),
            ("d_rstd2", (128, 2048), F32, "ExternalOutput"),
            ("d_khatT", (128, 2048), BF16, "ExternalOutput"),
            ("d_v", (128, 2048), BF16, "ExternalOutput"),
            ("d_cols", (128, 34), F32, "ExternalOutput"),
            ("d_mrow", (1, 2048), BF16, "ExternalOutput"),
            ("d_zrow", (1, 2048), F32, "ExternalOutput"),
            ("d_eT", (128, 8192), BF16, "ExternalOutput"),
        ]
    specs += _ATTN_L0_W if layer == 0 else _ATTN_L1_W
    with ExitStack() as ctx:
        T = _mk_dram(nc, specs)
        p = Prog(nc, ctx)
        C = Consts(nc, ctx, p, T)
        sto = emit_attn(p, C, layer, T, tag=f"a{layer}_", dbg=dbg)
        p.emit("sync", lambda e: None, waits=[(sto, None)])
        p.finalize()
    return nc


def build_ffn_nc(layer):
    nc = bass.Bass("TRN2", target_bir_lowering=False)
    specs = [
        ("x_in", (16, 128, 2048), F32, "ExternalInput"),
        ("idn_bf", (128, 128), BF16, "ExternalInput"),
        ("idn_f32", (128, 128), F32, "ExternalInput"),
        ("xnT", (16, 128, 2048), BF16, "ExternalOutput"),
        ("x_acc", (16, 128, 2048), F32, "ExternalOutput"),
    ]
    specs += _FFN_L0_W if layer == 0 else _FFN_L1_W
    with ExitStack() as ctx:
        T = _mk_dram(nc, specs)
        p = Prog(nc, ctx)
        C = Consts(nc, ctx, p, T)
        if layer == 1:
            w2done = emit_w2pp(p, C, T)
            for eng in ENGINES:
                p.emit(eng, lambda e: None, waits=[(w2done, None)])
        sto = emit_ffn(p, C, layer, T, tag=f"f{layer}_")
        p.emit("sync", lambda e: None, waits=[(sto, None)])
        p.finalize()
    return nc


def build_fused_nc():
    nc = bass.Bass("TRN2", target_bir_lowering=False)
    specs = [
        ("x_in", (16, 128, 2048), F32, "ExternalInput"),
        ("idn_bf", (128, 128), BF16, "ExternalInput"),
        ("idn_f32", (128, 128), F32, "ExternalInput"),
        ("h128", (128, 128), BF16, "ExternalInput"),
        ("h2x64", (128, 128), BF16, "ExternalInput"),
        ("out", (16, 128, 2048), F32, "ExternalOutput"),
        ("xnT_s", (16, 128, 2048), BF16, "Internal"),
        ("xacc_s", (16, 128, 2048), F32, "Internal"),
        ("x1_s", (16, 128, 2048), F32, "Internal"),
        ("aoT_s", (16, 128, 2048), BF16, "Internal"),
        ("l1_w2a", (8192, 2048), BF16, "Internal"),
        ("l1_w2pp", (64, 128, 2048), BF16, "Internal"),
    ]
    specs += _FFN_L0_W + _ATTN_L0_W
    specs += [t for t in _FFN_L1_W if t[0] in ("l1_w1", "l1_w3", "l1_w2t")]
    specs += _ATTN_L1_W
    with ExitStack() as ctx:
        T = _mk_dram(nc, specs)
        p = Prog(nc, ctx)
        C = Consts(nc, ctx, p, T)

        def phase(emitter):
            with ExitStack() as pc:
                p.alloc_ctx = pc
                emitter()
            p.alloc_ctx = ctx
            p.barrier()

        Tf0 = dict(T, x_in=T["x_in"], xnT=T["xnT_s"], x_acc=T["xacc_s"])
        Ta0 = dict(T, xnT=T["xnT_s"], x_base=T["xacc_s"], x_out=T["x1_s"], aoT=T["aoT_s"])
        Tf1 = dict(T, x_in=T["x1_s"], xnT=T["xnT_s"], x_acc=T["xacc_s"])
        Ta1 = dict(T, xnT=T["xnT_s"], x_base=T["xacc_s"], x_out=T["out"], aoT=T["aoT_s"])

        phase(lambda: emit_w2pp(p, C, T))
        phase(lambda: emit_ffn(p, C, 0, Tf0, tag="f0_"))
        phase(lambda: emit_attn(p, C, 0, Ta0, tag="a0_"))
        phase(lambda: emit_ffn(p, C, 1, Tf1, tag="f1_"))
        phase(lambda: emit_attn(p, C, 1, Ta1, tag="a1_"))
        p.finalize()
    return nc


_BUILD_CACHE = {}


def _get_nc(key, builder):
    if key not in _BUILD_CACHE:
        _BUILD_CACHE[key] = builder()
    return _BUILD_CACHE[key]


def kernel_fused(**inputs):
    w = prep_weights(inputs)
    c = host_consts()
    x = inputs["L0_attn_x"]
    nc = _get_nc("fused", build_fused_nc)
    base = {"idn_bf": c["idn_bf"], "idn_f32": c["idn_f32"],
            "h128": c["h128"], "h2x64": c["h2x64"]}
    for k in ("l0_wg", "l0_wu", "l0_wd", "l0_sg", "l0_su", "l0_sdown",
              "l0_wq", "l0_wk", "l0_wv", "l0_wo", "l0_qscale", "l0_kscale",
              "l1_w1", "l1_w3", "l1_w2t",
              "l1_wq", "l1_wk", "l1_wv", "l1_wv_n", "l1_wo"):
        base[k] = w[k]
    maps = [dict(base, x_in=np.ascontiguousarray(x[i].reshape(16, 128, 2048)))
            for i in range(NCORES)]
    res = run_bass_kernel_spmd(nc, maps, core_ids=list(range(NCORES)))
    out = np.empty((NCORES, 2048, 2048), dtype=np.float32)
    for i in range(NCORES):
        out[i] = res.results[i]["out"].reshape(2048, 2048)
    return out


def kernel(**inputs):
    return kernel_fused(**inputs)


def kernel_unfused(**inputs):
    w = prep_weights(inputs)
    c = host_consts()
    x = inputs["L0_attn_x"]
    consts = {"idn_bf": c["idn_bf"], "idn_f32": c["idn_f32"]}

    def shard(arr):
        return [np.ascontiguousarray(arr[cid].reshape(16, 128, 2048))
                for cid in range(NCORES)]

    core_ids = list(range(NCORES))

    nc = _get_nc("ffn0", lambda: build_ffn_nc(0))
    base = dict(consts)
    for k in ("l0_wg", "l0_wu", "l0_wd", "l0_sg", "l0_su", "l0_sdown"):
        base[k] = w[k]
    xs = shard(x)
    maps = [dict(base, x_in=xs[i]) for i in range(NCORES)]
    res = run_bass_kernel_spmd(nc, maps, core_ids=core_ids)
    xnT = [r["xnT"] for r in res.results]
    xacc = [r["x_acc"] for r in res.results]

    nc = _get_nc("attn0", lambda: build_attn_nc(0))
    base = dict(consts)
    for k in ("l0_wq", "l0_wk", "l0_wv", "l0_wo", "l0_qscale", "l0_kscale"):
        base[k] = w[k]
    maps = [dict(base, xnT=xnT[i], x_base=xacc[i]) for i in range(NCORES)]
    res = run_bass_kernel_spmd(nc, maps, core_ids=core_ids)
    x1 = [r["x_out"] for r in res.results]

    nc = _get_nc("ffn1", lambda: build_ffn_nc(1))
    base = dict(consts)
    for k in ("l1_w1", "l1_w3", "l1_w2t"):
        base[k] = w[k]
    base["h128"] = c["h128"]
    base["h2x64"] = c["h2x64"]
    maps = [dict(base, x_in=x1[i]) for i in range(NCORES)]
    res = run_bass_kernel_spmd(nc, maps, core_ids=core_ids)
    xnT = [r["xnT"] for r in res.results]
    xacc = [r["x_acc"] for r in res.results]

    nc = _get_nc("attn1", lambda: build_attn_nc(1))
    base = dict(consts)
    for k in ("l1_wq", "l1_wk", "l1_wv", "l1_wv_n", "l1_wo"):
        base[k] = w[k]
    maps = [dict(base, xnT=xnT[i], x_base=xacc[i]) for i in range(NCORES)]
    res = run_bass_kernel_spmd(nc, maps, core_ids=core_ids)

    out = np.empty((NCORES, 2048, 2048), dtype=np.float32)
    for i in range(NCORES):
        out[i] = res.results[i]["x_out"].reshape(2048, 2048)
    return out
